# version 01
# baseline (speedup 1.0000x reference)
import numpy as np
import ml_dtypes
import concourse.bass as bass
import concourse.mybir as mybir
from concourse.bass_utils import run_bass_kernel_spmd

F32 = mybir.dt.float32
BF16 = mybir.dt.bfloat16
I32 = mybir.dt.int32
AF = mybir.ActivationFunctionType
ALU = mybir.AluOpType
BF = ml_dtypes.bfloat16

NCORES = 8
D = 2048
DSSM = 4096
NH = 64
ECH_IN = 67
KD = 16
KF = 32
KH = 44
T = 512
NTILE = 8
NCH = 4
EPS = 1e-5

E_ORDER = [66] + list(range(32, 66)) + list(range(32))


def _wslices():
    sl = []
    for j in E_ORDER:
        sl.append(("w_in", j, 0, 8)); sl.append(("w_in", j, 8, 16))
    for j in range(16):
        for q in range(4):
            sl.append(("w_out", j, q * 8, q * 8 + 8))
    for j in range(44):
        sl.append(("wqg", j, 0, 8)); sl.append(("wqg", j, 8, 16))
        sl.append(("wqu", j, 0, 8)); sl.append(("wqu", j, 8, 16))
    for j in range(16):
        for q in range(6):
            sl.append(("wqd", j, q * 8, min(q * 8 + 8, 44)))
    return sl


WSLICES = _wslices()
DEBUG_TILE0 = False
NSL = len(WSLICES)
NSLOT = 8


DMAPOOL = 24


class Prog:

    def __init__(self):
        self.ops = []
        self.last_w = {}
        self.readers = {}
        self.dmacnt = {"sync": 0, "gq": 0}

    def op(self, sem_key, fn, reads=(), writes=(), dma=False, tiny=False, waits=()):
        idx = len(self.ops)
        raw = set()
        for r in reads:
            w = self.last_w.get(r)
            if w is not None:
                raw.add(w)
        for r in waits:
            w = self.last_w.get(r)
            if w is not None:
                raw.add(w)
        deps = set(raw)
        for r in writes:
            w = self.last_w.get(r)
            if w is not None:
                deps.add(w)
            for rd in self.readers.get(r, ()):
                deps.add(rd)
        if dma:
            queue = "sync" if sem_key.startswith("sync") else "gq"
            qi = self.dmacnt[queue]
            self.dmacnt[queue] = qi + 1
            stream = queue
            sem = (queue, qi % DMAPOOL)
        else:
            stream = sem_key
            sem = sem_key
        import os
        if os.environ.get("KDRAIN_ALL") and not dma:
            tiny = True
        self.ops.append({"sem": sem, "stream": stream, "fn": fn, "deps": deps,
                         "raw": raw, "dma": dma, "tiny": tiny})
        for r in reads:
            self.readers.setdefault(r, []).append(idx)
        for r in writes:
            self.last_w[r] = idx
            self.readers[r] = []
        return idx

    def finalize(self):
        n = len(self.ops)
        self.need_inc = [o["dma"] for o in self.ops]
        for o in self.ops:
            for d in o["deps"]:
                if self.ops[d]["stream"] != o["stream"] or self.ops[d]["dma"]:
                    self.need_inc[d] = True
        cnt = {}
        self.val = [0] * n
        for i, o in enumerate(self.ops):
            if self.need_inc[i]:
                cnt[o["sem"]] = cnt.get(o["sem"], 0) + (16 if o["dma"] else 1)
            self.val[i] = cnt.get(o["sem"], 0)
        self.totals = cnt

    def emit_stream(self, eng, sems, streams, spacer=None):
        watermark = {}
        drained_at = -1
        prev = None
        for i, o in enumerate(self.ops):
            if o["stream"] not in streams:
                continue
            want = {}
            need_drain = False
            for d in o["deps"]:
                dop = self.ops[d]
                if dop["stream"] == o["stream"] and not dop["dma"]:
                    if dop.get("tiny") and d > drained_at and not o["dma"]:
                        need_drain = True
                    continue
                if dop["stream"] == o["stream"] and dop["dma"] and not o["dma"]:
                    pass
                v = self.val[d]
                ds = dop["sem"]
                if v > want.get(ds, 0):
                    want[ds] = v
            if o["dma"]:
                pv = self.val[i] - 16
                if pv > 0 and pv > want.get(o["sem"], 0):
                    want[o["sem"]] = pv
            for ds, v in sorted(want.items(), key=lambda kv: str(kv[0])):
                if v > watermark.get(ds, 0):
                    eng.wait_ge(sems[ds], v)
                    watermark[ds] = v
            if need_drain:
                eng.drain()
                drained_at = i
            if (spacer is not None and prev is not None and prev in o["raw"]
                    and self.ops[prev]["tiny"] and not self.need_inc[prev]):
                spacer(eng)
                self.n_spacers = getattr(self, "n_spacers", 0) + 1
            ins = o["fn"](eng)
            if self.need_inc[i]:
                if o.get("tiny") and not o["dma"]:
                    eng.drain().then_inc(sems[o["sem"]], 1)
                    drained_at = i
                else:
                    ins.then_inc(sems[o["sem"]], 16 if o["dma"] else 1)
            prev = i


def host_prep(inputs):
    d = {}
    u = np.asarray(inputs["L0_attn_u"], np.float32)
    w_in = np.asarray(inputs["L0_attn_W_in"], np.float32)
    w_pad = np.zeros((ECH_IN * 128, D), np.float32)
    w_pad[:8512] = w_in
    d["w_in"] = np.ascontiguousarray(
        w_pad.reshape(ECH_IN, 128, KD, 128).transpose(0, 3, 2, 1)).reshape(ECH_IN, 128, KD * 128)
    w_out = np.asarray(inputs["L0_attn_W_out"], np.float32)
    d["w_out"] = np.ascontiguousarray(
        w_out.reshape(16, 128, KF, 128).transpose(0, 3, 2, 1)).reshape(16, 128, KF * 128)
    for nm, key, ne, kk in (("wqg", "L0_ffn_wq_gate", 44, KD),
                            ("wqu", "L0_ffn_wq_up", 44, KD),
                            ("wqd", "L0_ffn_wq_down", 16, KH)):
        wq = np.asarray(inputs[key], np.int32)
        d[nm] = np.ascontiguousarray(
            wq.reshape(ne, 128, kk, 128).transpose(0, 3, 2, 1)).reshape(ne, 128, kk * 128)
    cw = np.asarray(inputs["L0_attn_conv_w"], np.float32).reshape(4352, 4)
    d["cw"] = np.ascontiguousarray(cw.reshape(34, 128, 4).transpose(1, 0, 2).reshape(128, 136))
    d["cb"] = np.ascontiguousarray(np.asarray(inputs["L0_attn_conv_b"], np.float32).reshape(34, 128).T)
    d["dtb"] = np.ascontiguousarray(np.asarray(inputs["L0_attn_dt_bias"], np.float32).reshape(NH, 1))
    d["alog"] = np.ascontiguousarray(np.asarray(inputs["L0_attn_A_log"], np.float32).reshape(NH, 1))
    d["db"] = np.ascontiguousarray(
        np.broadcast_to(np.asarray(inputs["L0_attn_D"], np.float32)[None, :], (128, NH)))
    d["nw"] = np.ascontiguousarray(np.asarray(inputs["L0_attn_norm_w"], np.float32).reshape(32, 128).T)
    d["sg"] = np.ascontiguousarray(np.asarray(inputs["L0_ffn_s_gate"], np.float32).reshape(44, 128).T)
    d["su"] = np.ascontiguousarray(np.asarray(inputs["L0_ffn_s_up"], np.float32).reshape(44, 128).T)
    d["sd"] = np.ascontiguousarray(np.asarray(inputs["L0_ffn_s_down"], np.float32).reshape(16, 128).T)
    d["idf"] = np.eye(128, dtype=np.float32)
    d["idb"] = np.eye(128, dtype=np.float32).astype(BF)
    d["id64"] = np.eye(64, dtype=np.float32).astype(BF)
    d["ones128"] = np.ones((128, 1), np.float32).astype(BF)
    d["ones1"] = np.ones((1, 128), np.float32).astype(BF)
    mask = np.zeros((128, 128), np.float32)
    mv = np.float32(ml_dtypes.bfloat16(np.exp(5.0)))
    for s in range(128):
        mask[s, s:] = mv
    d["mask"] = mask.astype(BF)
    d["zeros"] = np.zeros((128, DSSM), np.float32)
    us = [np.ascontiguousarray(u[2 * c:2 * c + 2].transpose(0, 2, 1)) for c in range(NCORES)]
    return d, us


def build_program():
    nc = bass.Bass("TRN2", target_bir_lowering=False)
    io = {}

    def din(name, shape, dt):
        io[name] = nc.dram_tensor(name, shape, dt, kind="ExternalInput").ap()
        return io[name]

    u_d = din("u", [2, D, 2048], F32)
    wdram = {
        "w_in": din("w_in", [ECH_IN, 128, KD * 128], F32),
        "w_out": din("w_out", [16, 128, KF * 128], F32),
        "wqg": din("wqg", [44, 128, KD * 128], I32),
        "wqu": din("wqu", [44, 128, KD * 128], I32),
        "wqd": din("wqd", [16, 128, KH * 128], I32),
    }
    cw_d = din("cw", [128, 136], F32)
    cb_d = din("cb", [128, 34], F32)
    dtb_d = din("dtb", [NH, 1], F32)
    alog_d = din("alog", [NH, 1], F32)
    db_d = din("db", [128, NH], F32)
    nw_d = din("nw", [128, 32], F32)
    sg_d = din("sg", [128, 44], F32)
    su_d = din("su", [128, 44], F32)
    sd_d = din("sd", [128, 16], F32)
    idf_d = din("idf", [128, 128], F32)
    idb_d = din("idb", [128, 128], BF16)
    id64_d = din("id64", [64, 64], BF16)
    ones128_d = din("ones128", [128, 1], BF16)
    ones1_d = din("ones1", [1, 128], BF16)
    mask_d = din("mask", [128, 128], BF16)
    zeros_d = din("zeros", [128, DSSM], F32)

    o_d = nc.dram_tensor("o", [2, 2048, D], F32, kind="ExternalOutput").ap()
    scr = {
        "w_in": nc.dram_tensor("scr_in", [ECH_IN, 128, KD * 128], BF16, kind="Internal").ap(),
        "w_out": nc.dram_tensor("scr_out", [16, 128, KF * 128], BF16, kind="Internal").ap(),
        "wqg": nc.dram_tensor("scr_g", [44, 128, KD * 128], BF16, kind="Internal").ap(),
        "wqu": nc.dram_tensor("scr_u", [44, 128, KD * 128], BF16, kind="Internal").ap(),
        "wqd": nc.dram_tensor("scr_d", [16, 128, KH * 128], BF16, kind="Internal").ap(),
    }
    scr_x1n = nc.dram_tensor("scr_x1n", [KD, 128, T], F32, kind="Internal").ap()
    shield = nc.dram_tensor("shield", [NTILE, NCH, 128, 64], F32, kind="Internal").ap()
    import os as _os7
    TAP7 = bool(_os7.environ.get("K_TAP7"))
    d7 = {}
    if TAP7:
        d7["d7_stats"] = nc.dram_tensor("d7_stats", [128, 8], F32, kind="ExternalOutput").ap()
        d7["d7_lncol"] = nc.dram_tensor("d7_lncol", [128, 8], F32, kind="ExternalOutput").ap()
        d7["d7_x2"] = nc.dram_tensor("d7_x2", [128, 2048], F32, kind="ExternalOutput").ap()
    dbg = {}
    if DEBUG_TILE0:
        for nm, shape, dt in (
            ("d_ubf", [128, KD * T], BF16), ("d_zbf", [128, 32 * T], BF16),
            ("d_xtm", [128, 32 * T], BF16), ("d_bcbf", [128, 2 * T], BF16),
            ("d_dtsoft", [64, T], F32), ("d_cum", [64, T], F32),
            ("d_biasH", [64, T], BF16), ("d_biasL", [64, T], BF16),
            ("d_outscT", [128, 256], F32), ("d_expcum", [128, 256], F32),
            ("d_expTotB", [128, 256], F32), ("d_cumth", [128, 256], F32),
            ("d_Y0", [128, 4096], F32), ("d_ynfm", [128, 32 * T], BF16),
            ("d_x1", [128, KD * T], F32), ("d_x1nbf", [128, KD * T], BF16),
            ("d_h", [128, 44 * T], BF16), ("d_x2tm", [128, 2048], F32),
            ("d_S1", [128, 4096], F32),
            ("d_Ydsk", [128, 4096], F32), ("d_yg", [128, 4096], F32),
            ("d_Y1", [128, 4096], F32), ("d_M41", [128, 512], BF16), ("d_D41", [128, 512], BF16),
            ("d_ygc1", [128, 4096], F32), ("d_rstd1", [128, 53], F32),
            ("d_yn1", [128, 32 * T], BF16), ("d_ynfm2", [128, 32 * T], BF16),
            ("d_ln2", [128, 32], F32), ("d_x2post", [128, 2048], F32),
            ("d_rstd", [128, 53], F32), ("d_yn", [128, 32 * T], BF16),
        ):
            dbg[nm] = nc.dram_tensor(nm, shape, dt, kind="ExternalOutput").ap()

    p = Prog()

    import contextlib
    with contextlib.ExitStack() as _stk:
        c_idf = _stk.enter_context(nc.sbuf_tensor("c_idf", [128, 128], F32))
        c_idb = _stk.enter_context(nc.sbuf_tensor("c_idb", [128, 128], BF16))
        c_id64 = _stk.enter_context(nc.sbuf_tensor("c_id64", [64, 64], BF16))
        c_ones128 = _stk.enter_context(nc.sbuf_tensor("c_ones128", [128, 1], BF16))
        c_ones1 = _stk.enter_context(nc.sbuf_tensor("c_ones1", [1, 128], BF16))
        c_mask = _stk.enter_context(nc.sbuf_tensor("c_mask", [128, 128], BF16))
        c_cw = _stk.enter_context(nc.sbuf_tensor("c_cw", [128, 136], F32))
        c_cb = _stk.enter_context(nc.sbuf_tensor("c_cb", [128, 34], F32))
        c_dtb = _stk.enter_context(nc.sbuf_tensor("c_dtb", [NH, 1], F32))
        c_alog = _stk.enter_context(nc.sbuf_tensor("c_alog", [NH, 1], F32))
        c_negea = _stk.enter_context(nc.sbuf_tensor("c_negea", [NH, 2], F32))
        c_db = _stk.enter_context(nc.sbuf_tensor("c_db", [128, NH], F32))
        c_nw = _stk.enter_context(nc.sbuf_tensor("c_nw", [128, 32], F32))
        c_sg = _stk.enter_context(nc.sbuf_tensor("c_sg", [128, 44], F32))
        c_su = _stk.enter_context(nc.sbuf_tensor("c_su", [128, 44], F32))
        c_sd = _stk.enter_context(nc.sbuf_tensor("c_sd", [128, 16], F32))
        u_bf = _stk.enter_context(nc.sbuf_tensor("u_bf", [128, KD * T], BF16))
        zbf = _stk.enter_context(nc.sbuf_tensor("zbf", [128, 32 * T], BF16))
        xtm = _stk.enter_context(nc.sbuf_tensor("xtm", [128, 32 * T], BF16))
        bcbf = _stk.enter_context(nc.sbuf_tensor("bcbf", [128, 2 * T], BF16))
        S_sb = _stk.enter_context(nc.sbuf_tensor("S", [128, DSSM], F32))
        S_bf0 = _stk.enter_context(nc.sbuf_tensor("S_bf0", [128, 1024], BF16))
        S_bf1 = _stk.enter_context(nc.sbuf_tensor("S_bf1", [128, 1024], BF16))
        arena = _stk.enter_context(nc.sbuf_tensor("arena", [128, 22528], BF16))
        dtraw_t = _stk.enter_context(nc.sbuf_tensor("dtraw_t", [64, T], F32))
        tmpA_t = _stk.enter_context(nc.sbuf_tensor("tmpA_t", [64, T], F32))
        tmpB_t = _stk.enter_context(nc.sbuf_tensor("tmpB_t", [64, T], F32))
        dtsoft_t = _stk.enter_context(nc.sbuf_tensor("dtsoft_t", [64, T], F32))
        lndt_t = _stk.enter_context(nc.sbuf_tensor("lndt_t", [64, T], F32))
        cum_t = _stk.enter_context(nc.sbuf_tensor("cum_t", [64, T], F32))
        totB_t = _stk.enter_context(nc.sbuf_tensor("totB_t", [64, T], F32))
        outsc_t = _stk.enter_context(nc.sbuf_tensor("outsc_t", [64, T], F32))
        biasH = _stk.enter_context(nc.sbuf_tensor("biasH", [64, T], BF16))
        biasL = _stk.enter_context(nc.sbuf_tensor("biasL", [64, T], BF16))
        zt64 = _stk.enter_context(nc.sbuf_tensor("zt64", [64, 128], F32))
        cum_th = _stk.enter_context(nc.sbuf_tensor("cum_th", [128, 256], F32))
        expcum = _stk.enter_context(nc.sbuf_tensor("expcum", [128, 256], F32))
        outscT = _stk.enter_context(nc.sbuf_tensor("outscT", [128, 256], F32))
        expTotB = _stk.enter_context(nc.sbuf_tensor("expTotB", [128, 256], F32))
        Gtm0 = _stk.enter_context(nc.sbuf_tensor("Gtm0", [128, 128], BF16))
        Gtm1 = _stk.enter_context(nc.sbuf_tensor("Gtm1", [128, 128], BF16))
        BT0 = _stk.enter_context(nc.sbuf_tensor("BT0", [128, 128], BF16))
        BT1 = _stk.enter_context(nc.sbuf_tensor("BT1", [128, 128], BF16))
        M4_0 = _stk.enter_context(nc.sbuf_tensor("M4_0", [128, 512], BF16))
        M4_1 = _stk.enter_context(nc.sbuf_tensor("M4_1", [128, 512], BF16))
        D4_0 = _stk.enter_context(nc.sbuf_tensor("D4_0", [128, 512], BF16))
        D4_1 = _stk.enter_context(nc.sbuf_tensor("D4_1", [128, 512], BF16))
        xo0 = _stk.enter_context(nc.sbuf_tensor("xo0", [128, 256], BF16))
        xo1 = _stk.enter_context(nc.sbuf_tensor("xo1", [128, 256], BF16))
        t4_0 = _stk.enter_context(nc.sbuf_tensor("t4_0", [128, 256], BF16))
        t4_1 = _stk.enter_context(nc.sbuf_tensor("t4_1", [128, 256], BF16))
        stats = _stk.enter_context(nc.sbuf_tensor("stats", [128, 216], F32))
        cin0 = _stk.enter_context(nc.sbuf_tensor("cin0", [128, 515], F32))
        cin1 = _stk.enter_context(nc.sbuf_tensor("cin1", [128, 515], F32))
        cout0 = _stk.enter_context(nc.sbuf_tensor("cout0", [128, 512], F32))
        cout1 = _stk.enter_context(nc.sbuf_tensor("cout1", [128, 512], F32))
        sx0 = _stk.enter_context(nc.sbuf_tensor("sx0", [128, 512], BF16))
        sx1 = _stk.enter_context(nc.sbuf_tensor("sx1", [128, 512], BF16))
        tails = _stk.enter_context(nc.sbuf_tensor("tails", [128, 102], F32))
        wring = _stk.enter_context(nc.sbuf_tensor("wring", [128, NSLOT * 1024], BF16))
        ustg0 = _stk.enter_context(nc.sbuf_tensor("ustg0", [128, T], F32))
        lncol = _stk.enter_context(nc.sbuf_tensor("lncol", [128, 8], F32))
        psum = _stk.enter_context(nc.psum_tensor("psum", [128, 4096], F32))
        s_pe = _stk.enter_context(nc.semaphore("s_pe"))
        s_act = _stk.enter_context(nc.semaphore("s_act"))
        s_dve = _stk.enter_context(nc.semaphore("s_dve"))
        dma_sems = {}
        for qn in ("sync", "gq"):
            for qi in range(DMAPOOL):
                dma_sems[(qn, qi)] = _stk.enter_context(nc.semaphore(f"s_{qn}{qi}"))
        block = _stk.enter_context(nc.Block())
        sems = {"pe": s_pe, "act": s_act, "dve": s_dve}
        sems.update(dma_sems)
        Ys = [arena[:, 0:8192].bitcast(F32), arena[:, 8192:16384].bitcast(F32)]
        silu_sl = [arena[:, 16384:17408].bitcast(F32), arena[:, 17408:18432].bitcast(F32)]
        stage_f = [arena[:, 18432:20480].bitcast(F32), arena[:, 20480:22528].bitcast(F32)]
        stage_i = [arena[:, 18432:20480].bitcast(I32), arena[:, 20480:22528].bitcast(I32)]
        stage5_f = [xtm[:, 0:2048].bitcast(F32), xtm[:, 2048:4096].bitcast(F32)]
        stage5_i = [xtm[:, 0:2048].bitcast(I32), xtm[:, 2048:4096].bitcast(I32)]
        x1_fm = arena[:, 0:16384].bitcast(F32)
        h_bf = arena
        x2tm = [zbf[:, c * 4096:(c + 1) * 4096].bitcast(F32) for c in range(NCH)]
        Gtms = [Gtm0, Gtm1]; BTs = [BT0, BT1]; M4s = [M4_0, M4_1]; D4s = [D4_0, D4_1]
        xos = [xo0, xo1]; t4s = [t4_0, t4_1]; S_bfs = [S_bf0, S_bf1]
        cins = [cin0, cin1]; couts = [cout0, cout1]; sxs = [sx0, sx1]
        ustgs = [ustg0, ustg0]; xas = couts; xbs = sxs
        x1ss = [cin0[:, 0:512], cin1[:, 0:512]]

        def pb(b, lo=0, hi=512):
            return psum[:, 512 * b + lo: 512 * b + hi]

        def pb_bf(b, lo=0, hi=1024):
            return psum[:, 512 * b: 512 * (b + 1)].bitcast(BF16)[:, lo:hi]

        def xtm3(c):
            return xtm[:].rearrange("p (f r) -> p f r", r=512)[:, :, c * 128:(c + 1) * 128]

        def xtm_head(c, h):
            off = (h // 2) * 512 + c * 128 + (h % 2) * 64
            return xtm[:, off: off + 64]

        for dst, src, wr in (
            (c_idf[:], idf_d, []), (c_idb[:], idb_d, []), (c_id64[:], id64_d, []),
            (c_ones128[:], ones128_d, []), (c_ones1[:], ones1_d, []), (c_mask[:], mask_d, []),
            (c_cw[:], cw_d, []), (c_cb[:], cb_d, []), (c_dtb[:], dtb_d, ["dtb"]),
            (c_alog[:], alog_d, ["alog"]), (c_db[:], db_d, []), (c_nw[:], nw_d, []),
            (c_sg[:], sg_d, []), (c_su[:], su_d, []), (c_sd[:], sd_d, []),
            (zt64[:], zeros_d[0:64, 0:128], []),
        ):
            p.op("gq", lambda e, d=dst, s=src: e.dma_start(out=d, in_=s),
                 reads=[], writes=wr, dma=True)
        p.op("act", lambda e: e.activation(out=c_negea[:, 0:1], in_=c_alog[:], func=AF.Exp),
             reads=["alog"], writes=["negea"])
        for _tw in range(2):
            p.op("dve", lambda e: e.tensor_scalar_mul(c_negea[:, 1:2], c_negea[:, 0:1], -1.0),
                 reads=["negea"] if _tw == 0 else [], writes=["negea"], tiny=True)
        p.op("gq", lambda e: e.dma_start(out=tails[:], in_=zeros_d[:, 0:102]),
             reads=[], writes=["tails", "constbar"], dma=True)
        for ek in ("pe", "act", "dve"):
            p.op(ek, lambda e: e.nop(), reads=["constbar"], writes=[])

        wslot_of = {}
        wcursor = {"tile": -1, "i": 0}
        WLOOK = 6

        def emit_slice(tile, si):
            skey = "sync0" if tile < 4 else "sync1"
            mat, j, k0, k1 = WSLICES[si]
            gi = tile * NSL + si
            slot = gi % NSLOT
            wslot_of[gi] = slot
            ncol = (k1 - k0) * 128
            dstb = wring[:, slot * 1024: slot * 1024 + ncol]
            if tile == 0:
                st = si % 2
                is_int = mat in ("wqg", "wqu", "wqd")
                src = wdram[mat][j][:, k0 * 128: k1 * 128]
                if is_int:
                    extra = [("ynfm", cc, fc) for cc in range(NCH)
                             for fc in (range(0, 4) if st == 0 else range(4, 8))]
                    stg = stage5_i[st][:, 0:ncol]
                    p.op("sync0", lambda e, d=stg, s=src: e.dma_start(out=d, in_=s),
                         reads=[], writes=[("stg5", st)] + extra, dma=True)
                    p.op("dve", lambda e, d=dstb, s=stg: e.tensor_scalar(
                            out=d, in0=s, scalar1=128.0, scalar2=None, op0=ALU.subtract),
                         reads=[("stg5", st)], writes=[("wr", slot)])
                else:
                    stg = stage_f[st][:, 0:ncol]
                    p.op("sync0", lambda e, d=stg, s=src: e.dma_start(out=d, in_=s),
                         reads=[], writes=[("stg", st)], dma=True)
                    p.op("dve", lambda e, d=dstb, s=stg: e.tensor_copy(out=d, in_=s),
                         reads=[("stg", st)], writes=[("wr", slot)])
                sdst = scr[mat][j][:, k0 * 128: k1 * 128]
                p.op("gq", lambda e, d=sdst, s=dstb: e.dma_start(out=d, in_=s),
                     reads=[("wr", slot)], writes=[("scr", mat, j, k0)], dma=True)
            else:
                src = scr[mat][j][:, k0 * 128: k1 * 128]
                p.op(skey, lambda e, d=dstb, s=src: e.dma_start(out=d, in_=s),
                     reads=[("scr", mat, j, k0)], writes=[("wr", slot)], dma=True)

        def ensure_slices(tile, si_first, si_last, cap=None):
            if wcursor["tile"] != tile:
                wcursor["tile"] = tile
                wcursor["i"] = 0
            target = min(si_first + NSLOT, si_last + 1 + WLOOK, NSL)
            if cap is not None:
                target = min(target, cap)
            target = max(target, si_last + 1)
            while wcursor["i"] < target:
                emit_slice(tile, wcursor["i"])
                wcursor["i"] += 1

        def wres(tile, si):
            return ("wr", wslot_of[tile * NSL + si])

        def wap(tile, si, k):
            slot = wslot_of[tile * NSL + si]
            return wring[:, slot * 1024 + k * 128: slot * 1024 + (k + 1) * 128]

        import os as _os
        _tapsel = _os.environ.get("K_TAPS")
        _tapset = set(_tapsel.split(",")) if _tapsel else None

        def dtap(name, ap, reads):
            if _tapset is not None and name not in _tapset:
                return
            if DEBUG_TILE0 and name in dbg:
                p.op("gq", lambda e, d=dbg[name], s=ap: e.dma_start(out=d, in_=s),
                     reads=list(reads), writes=[], dma=True)

        def emit_uload(tile):
            s, b = divmod(tile, 4)
            for k in range(KD):
                st = k % 2
                src = u_d[s][k * 128:(k + 1) * 128, b * T:(b + 1) * T]
                p.op("gq", lambda e, st=st, src=src: e.dma_start(out=ustgs[st][:], in_=src),
                     reads=[], writes=[("ustg", 0)], dma=True)
                p.op("dve", lambda e, st=st, k=k: e.tensor_copy(
                        out=u_bf[:, k * T:(k + 1) * T], in_=ustgs[st][:]),
                     reads=[("ustg", 0)], writes=[("ubf", k)])

        for tile in range(NTILE):
            s, b = divmod(tile, 4)
            if tile == 0:
                emit_uload(0)
            if tile % 4 == 0:
                p.op("gq", lambda e: e.dma_start(out=S_sb[:], in_=zeros_d),
                     reads=[], writes=[("S", g) for g in range(16)], dma=True)
                p.op("gq", lambda e: e.dma_start(out=tails[:], in_=zeros_d[:, 0:102]),
                     reads=[], writes=["tails"], dma=True)

            for pos, j in enumerate(E_ORDER):
                bank = pos % 2
                si0 = 2 * pos
                ensure_slices(tile, si0, si0 + 1, cap=198)
                for k in range(KD):
                    si = si0 + (k // 8)
                    p.op("pe",
                         lambda e, bank=bank, tile=tile, si=si, k=k: e.matmul(
                             pb(bank), wap(tile, si, k % 8),
                             u_bf[:, k * T:(k + 1) * T],
                             start=(k == 0), stop=(k == KD - 1)),
                         waits=[wres(tile, si), ("ubf", k)],
                         reads=([wres(tile, si)] if k % 8 == 7 else []) +
                               ([("ubf", k)] if pos == len(E_ORDER) - 1 else []),
                         writes=[("ps", bank)] if (k == KD - 1 or k == 0) else [])
                if j == 66:
                    p.op("dve", lambda e, bank=bank: e.tensor_scalar(
                            out=dtraw_t[:], in0=pb(bank)[0:64, :],
                            scalar1=c_dtb[:], scalar2=None, op0=ALU.add),
                         reads=[("ps", bank), "dtb"], writes=["dtraw"], tiny=True)
                    dtraw = dtraw_t[:]; tmpA = tmpA_t[:]
                    tmpB = tmpB_t[:]; dtsoft = dtsoft_t[:]
                    lndt = lndt_t[:]; cum = cum_t[:]
                    totB = totB_t[:]; outsc = outsc_t[:]
                    p.op("act", lambda e: e.activation(out=tmpA, in_=dtraw, func=AF.Relu),
                         reads=["dtraw"], writes=["tmpA"], tiny=True)
                    p.op("act", lambda e: e.activation(out=tmpB, in_=dtraw, func=AF.Abs),
                         reads=["dtraw"], writes=["tmpB"], tiny=True)
                    p.op("act", lambda e: e.activation(out=tmpB, in_=tmpB, func=AF.Exp, scale=-1.0),
                         reads=["tmpB"], writes=["tmpB"], tiny=True)
                    p.op("act", lambda e: e.activation(out=tmpB, in_=tmpB, func=AF.Ln, bias=1.0),
                         reads=["tmpB"], writes=["tmpB"], tiny=True)
                    p.op("dve", lambda e: e.tensor_add(dtsoft, tmpA, tmpB),
                         reads=["tmpA", "tmpB"], writes=["dtsoft"], tiny=True)
                    p.op("act", lambda e: e.activation(out=lndt, in_=dtsoft, func=AF.Ln),
                         reads=["dtsoft"], writes=["lndt"], tiny=True)
                    p.op("dve", lambda e: e.tensor_scalar(out=tmpA, in0=dtsoft, scalar1=c_negea[:, 1:2],
                                                          scalar2=None, op0=ALU.mult),
                         reads=["dtsoft", "negea"], writes=["tmpA"], tiny=True)
                    for c in range(NCH):
                        p.op("dve", lambda e, c=c: e.tensor_tensor_scan(
                                out=cum[:, c * 128:(c + 1) * 128],
                                data0=tmpA[:, c * 128:(c + 1) * 128],
                                data1=zt64[:], initial=0.0, op0=ALU.add, op1=ALU.add),
                             reads=["tmpA"], writes=["cum"], tiny=True)
                    p.op("dve", lambda e: e.tensor_sub(dtraw, lndt, cum),
                         reads=["lndt", "cum"], writes=["dtraw"], tiny=True)
                    _C5 = float(np.log(np.float64(np.float32(ml_dtypes.bfloat16(np.exp(5.0))))))
                    p.op("dve", lambda e: e.tensor_scalar(out=dtraw, in0=dtraw, scalar1=-_C5,
                                                          scalar2=None, op0=ALU.add),
                         reads=["dtraw"], writes=["dtraw"], tiny=True)
                    p.op("dve", lambda e: e.tensor_copy(out=biasH[:], in_=dtraw),
                         reads=["dtraw"], writes=["biasHL"], tiny=True)
                    p.op("dve", lambda e: e.tensor_sub(tmpB, dtraw, biasH[:]),
                         reads=["dtraw", "biasHL"], writes=["tmpB"], tiny=True)
                    p.op("dve", lambda e: e.tensor_copy(out=biasL[:], in_=tmpB),
                         reads=["tmpB"], writes=["biasHL"], tiny=True)
                    for c in range(NCH):
                        p.op("dve", lambda e, c=c: e.tensor_copy(
                                out=totB[:, c * 128:(c + 1) * 128],
                                in_=cum[:, c * 128 + 127: c * 128 + 128].broadcast_to([64, 128])),
                             reads=["cum"], writes=["totB"], tiny=True)
                    p.op("dve", lambda e: e.tensor_sub(outsc, totB, cum),
                         reads=["totB", "cum"], writes=["outsc"], tiny=True)
                    p.op("dve", lambda e: e.tensor_add(outsc, outsc, lndt),
                         reads=["outsc", "lndt"], writes=["outsc"], tiny=True)
                    p.op("act", lambda e: e.activation(out=outsc, in_=outsc, func=AF.Exp),
                         reads=["outsc"], writes=["outsc"], tiny=True)
                    for c in range(NCH):
                        p.op("pe", lambda e, c=c: e.matmul(
                                pb(3, 0, 64), cum[:, c * 128:(c + 1) * 128], c_idf[0:64, 0:64],
                                is_transpose=True, start=True, stop=True),
                             reads=["cum"], writes=[("ps", 3)])
                        p.op("dve", lambda e, c=c: e.tensor_copy(
                                out=cum_th[:, c * 64:(c + 1) * 64], in_=pb(3, 0, 64)),
                             reads=[("ps", 3)], writes=["cum_th"], tiny=True)
                        p.op("pe", lambda e, c=c: e.matmul(
                                pb(3, 0, 64), outsc[:, c * 128:(c + 1) * 128], c_idf[0:64, 0:64],
                                is_transpose=True, start=True, stop=True),
                             reads=["outsc"], writes=[("ps", 3)])
                        p.op("dve", lambda e, c=c: e.tensor_copy(
                                out=outscT[:, c * 64:(c + 1) * 64], in_=pb(3, 0, 64)),
                             reads=[("ps", 3)], writes=["outscT"], tiny=True)
                        p.op("pe", lambda e, c=c: e.matmul(
                                pb(3, 0, 64), totB[:, c * 128:(c + 1) * 128], c_idf[0:64, 0:64],
                                is_transpose=True, start=True, stop=True),
                             reads=["totB"], writes=[("ps", 3)])
                        p.op("act", lambda e, c=c: e.activation(
                                out=expTotB[:, c * 64:(c + 1) * 64], in_=pb(3, 0, 64), func=AF.Exp),
                             reads=[("ps", 3)], writes=["expTotB"], tiny=True)
                        p.op("act", lambda e, c=c: e.activation(
                                out=expcum[:, c * 64:(c + 1) * 64],
                                in_=cum_th[:, c * 64:(c + 1) * 64], func=AF.Exp),
                             reads=["cum_th"], writes=["expcum"], tiny=True)
                elif 32 <= j < 66:
                    fc = j - 32
                    ci = fc % 2
                    p.op("dve", lambda e, fc=fc, ci=ci: e.tensor_copy(
                            out=cins[ci][:, 0:3], in_=tails[:, fc * 3:(fc + 1) * 3]),
                         reads=["tails"], writes=[("cin", ci)], tiny=True)
                    p.op("act", lambda e, bank=bank, ci=ci: e.activation(
                            out=cins[ci][:, 3:515], in_=pb(bank), func=AF.Copy),
                         reads=[("ps", bank)], writes=[("cin", ci)])
                    p.op("dve", lambda e, fc=fc, ci=ci: e.tensor_copy(
                            out=tails[:, fc * 3:(fc + 1) * 3], in_=cins[ci][:, 512:515]),
                         reads=[("cin", ci)], writes=["tails"], tiny=True)
                    p.op("dve", lambda e, ci=ci, fc=fc: e.tensor_scalar(
                            out=couts[ci][:], in0=cins[ci][:, 0:512],
                            scalar1=c_cw[:, fc * 4: fc * 4 + 1], scalar2=None, op0=ALU.mult),
                         reads=[("cin", ci)], writes=[("cout", ci)])
                    for tap in range(1, 4):
                        p.op("dve", lambda e, ci=ci, fc=fc, tap=tap: e.scalar_tensor_tensor(
                                out=couts[ci][:], in0=cins[ci][:, tap:512 + tap],
                                scalar=c_cw[:, fc * 4 + tap: fc * 4 + tap + 1],
                                in1=couts[ci][:], op0=ALU.mult, op1=ALU.add),
                             reads=[("cin", ci), ("cout", ci)], writes=[("cout", ci)])
                    if fc < 32:
                        p.op("act", lambda e, ci=ci, fc=fc: e.activation(
                                out=sxs[ci][:], in_=couts[ci][:], func=AF.Silu,
                                bias=c_cb[:, fc: fc + 1]),
                             reads=[("cout", ci)], writes=[("sx", ci)])
                        tb = 2 + fc % 2
                        extra = [("stg5", 0)] if fc == 0 else ([("stg5", 1)] if fc == 4 else [])
                        for c in range(NCH):
                            p.op("pe", lambda e, ci=ci, c=c, tb=tb: e.matmul(
                                    pb_bf(tb, c * 128, (c + 1) * 128),
                                    sxs[ci][:, c * 128:(c + 1) * 128],
                                    c_idb[:], is_transpose=True, start=(c == 0), stop=(c == 3)),
                                 reads=[("sx", ci)], writes=[("ps", tb)])
                        for c in range(NCH):
                            p.op("dve", lambda e, c=c, fc=fc, tb=tb: e.tensor_copy(
                                    out=xtm[:, fc * 512 + c * 128: fc * 512 + (c + 1) * 128],
                                    in_=pb_bf(tb, c * 128, (c + 1) * 128)),
                                 reads=[("ps", tb)],
                                 writes=[("xtm", c, fc), ("ynfm", c, fc)] + extra, tiny=True)
                    else:
                        bc = fc - 32
                        p.op("act", lambda e, ci=ci, fc=fc, bc=bc: e.activation(
                                out=bcbf[:, bc * 512:(bc + 1) * 512], in_=couts[ci][:],
                                func=AF.Silu, bias=c_cb[:, fc: fc + 1]),
                             reads=[("cout", ci)], writes=[("bcbf", bc)])
                else:
                    zc = j
                    p.op("act", lambda e, bank=bank, zc=zc: e.activation(
                            out=zbf[:, zc * 512:(zc + 1) * 512], in_=pb(bank), func=AF.Copy),
                         reads=[("ps", bank)],
                         writes=[("zbf", zc)] + ([("x2tm", zc // 8, jj) for jj in range(16)]
                                                 if zc % 8 == 0 else []))

            if tile == 0:
                dtap("d_ubf", u_bf[:], [("ubf", k) for k in range(KD)])
                dtap("d_zbf", zbf[:], [("zbf", zc) for zc in range(32)])
                dtap("d_xtm", xtm[:], [("xtm", c, fc) for c in range(NCH) for fc in range(32)])
                dtap("d_bcbf", bcbf[:], [("bcbf", 0), ("bcbf", 1)])
                dtap("d_dtsoft", dtsoft_t[:], ["dtsoft"])
                dtap("d_cum", cum_t[:], ["cum"])
                dtap("d_biasH", biasH[:], ["biasHL"])
                dtap("d_biasL", biasL[:], ["biasHL"])
                dtap("d_outscT", outscT[:], ["outscT"])
                dtap("d_expcum", expcum[:], ["expcum"])
                dtap("d_expTotB", expTotB[:], ["expTotB"])
                dtap("d_cumth", cum_th[:], ["cum_th"])

            for c in range(NCH):
                ci = c % 2
                B_c = bcbf[:, c * 128:(c + 1) * 128]
                C_c = bcbf[:, 512 + c * 128: 512 + (c + 1) * 128]
                p.op("pe", lambda e, B_c=B_c, C_c=C_c: e.matmul(
                        pb(6, 0, 128), B_c, C_c, start=True, stop=True),
                     reads=[("bcbf", 0), ("bcbf", 1)], writes=[("ps", 6)])
                p.op("dve", lambda e, ci=ci: e.tensor_mul(Gtms[ci][:], pb(6, 0, 128), c_mask[:]),
                     reads=[("ps", 6)], writes=[("Gtm", ci)], tiny=True)
                p.op("pe", lambda e, B_c=B_c: e.matmul(
                        pb_bf(7, 0, 128), B_c, c_idb[:], is_transpose=True, start=True, stop=True),
                     reads=[("bcbf", 0)], writes=[("ps", 7)])
                p.op("dve", lambda e, ci=ci: e.tensor_copy(out=BTs[ci][:], in_=pb_bf(7, 0, 128)),
                     reads=[("ps", 7)], writes=[("BT", ci)])

                for g in range(16):
                    gi = g % 2
                    q = g // 4
                    if g % 4 == 0:
                        p.op("act", lambda e, q=q: e.activation(
                                out=S_bfs[q % 2][:], in_=S_sb[:, q * 1024:(q + 1) * 1024],
                                func=AF.Copy),
                             reads=[("S", 4 * q), ("S", 4 * q + 1), ("S", 4 * q + 2),
                                    ("S", 4 * q + 3)],
                             writes=[("Sbf", q % 2)])
                    bR = 2 + gi
                    for i in range(4):
                        p.op("pe", lambda e, i=i, g=g, c=c, bR=bR: e.matmul(
                                pb(bR, i * 128, (i + 1) * 128),
                                cum_th[:, c * 64 + 4 * g + i: c * 64 + 4 * g + i + 1]
                                    .broadcast_to([128, 128]),
                                c_idf[:], is_transpose=True, start=(i == 0), stop=False),
                             reads=["cum_th"], writes=[("ps", bR)])
                    selb = c_id64[:, 4 * g: 4 * g + 4].unsqueeze(2).broadcast_to([64, 4, 128])
                    p.op("pe", lambda e, bR=bR, sb=selb, c=c: e.matmul(
                            pb(bR).rearrange("p (h t) -> p h t", h=4),
                            biasH[:, c * 128:(c + 1) * 128], sb, start=False, stop=False),
                         reads=["biasHL"], writes=[("ps", bR)])
                    p.op("pe", lambda e, bR=bR, sb=selb, c=c: e.matmul(
                            pb(bR).rearrange("p (h t) -> p h t", h=4),
                            biasL[:, c * 128:(c + 1) * 128], sb, start=False, stop=True),
                         reads=["biasHL"], writes=[("ps", bR)])
                    bAB = 4 + gi
                    p.op("act", lambda e, gi=gi, bR=bR, bAB=bAB: e.activation(
                            out=pb(bAB), in_=pb(bR), func=AF.Relu, scale=-1.0),
                         reads=[("ps", bR)], writes=[("ps", bAB)])
                    p.op("act", lambda e, gi=gi, bAB=bAB: e.activation(
                            out=D4s[gi][:], in_=pb(bAB), func=AF.Exp, scale=-1.0),
                         reads=[("ps", bAB)], writes=[("D4", gi)])
                    p.op("dve", lambda e, gi=gi, ci=ci: e.tensor_mul(
                            M4s[gi][:].rearrange("p (h t) -> p h t", h=4),
                            D4s[gi][:].rearrange("p (h t) -> p h t", h=4),
                            Gtms[ci][:].unsqueeze(1).broadcast_to([128, 4, 128])),
                         reads=[("D4", gi), ("Gtm", ci)], writes=[("M4", gi)])
                    xpair = xtm[:].rearrange("p (f r) -> p f r", r=512)[
                        :, 2 * g:2 * g + 2, c * 128:(c + 1) * 128].rearrange(
                        "p h (q w) -> p h q w", q=2)
                    oscb = outscT[:, c * 64 + 4 * g: c * 64 + 4 * g + 4].rearrange(
                        "p (h q) -> p h q", h=2).unsqueeze(3).broadcast_to([128, 2, 2, 64])
                    p.op("dve", lambda e, gi=gi, xp=xpair, ob=oscb: e.tensor_mul(
                            xos[gi][:].rearrange("p (h q w) -> p h q w", h=2, q=2),
                            xp, ob),
                         reads=[("xtm", c, 2 * g), ("xtm", c, 2 * g + 1), "outscT"],
                         writes=[("xo", gi)])
                    for i in range(4):
                        h = 4 * g + i
                        p.op("pe", lambda e, i=i, h=h, gi=gi, bAB=bAB, c=c: e.matmul(
                                pb(bAB, i * 64, (i + 1) * 64),
                                M4s[gi][:, i * 128:(i + 1) * 128],
                                xtm_head(c, h), start=True, stop=True),
                             reads=[("M4", gi), ("xtm", c, h // 2)], writes=[("ps", bAB)])
                    p.op("pe", lambda e, gi=gi, g=g, q=q, bAB=bAB, C_c=C_c: e.matmul(
                            pb(bAB, 256, 512), C_c,
                            S_bfs[q % 2][:, (g % 4) * 256:(g % 4 + 1) * 256],
                            start=True, stop=True),
                         reads=[("bcbf", 1), ("Sbf", q % 2)], writes=[("ps", bAB)])
                    bC = 6 + gi
                    p.op("pe", lambda e, gi=gi, bC=bC, ci=ci: e.matmul(
                            pb(bC, 0, 256), BTs[ci][:], xos[gi][:], start=True, stop=True),
                         reads=[("BT", ci), ("xo", gi)], writes=[("ps", bC)])
                    extra_y = []
                    if c == 0 and g == 0 and ci == 0:
                        extra_y = [("h", jj) for jj in range(44)] + \
                                  [("x1", j2) for j2 in range(KD)] + \
                                  [("stg", 0), ("stg", 1)]
                    p.op("dve", lambda e, gi=gi, g=g, c=c, bAB=bAB: e.tensor_mul(
                            t4s[gi][:].rearrange("p (h w) -> p h w", h=4),
                            pb(bAB, 256, 512).rearrange("p (h w) -> p h w", h=4),
                            expcum[:, c * 64 + 4 * g: c * 64 + 4 * g + 4].unsqueeze(2)
                                .broadcast_to([128, 4, 64])),
                         reads=[("ps", bAB), "expcum"], writes=[("t4", gi)], tiny=True)
                    p.op("dve", lambda e, gi=gi, g=g, ci=ci, bAB=bAB: e.tensor_add(
                            Ys[ci][:, g * 256:(g + 1) * 256], t4s[gi][:], pb(bAB, 0, 256)),
                         reads=[("t4", gi), ("ps", bAB)],
                         writes=[("Y", ci, g)] + extra_y, tiny=True)
                    p.op("dve", lambda e, g=g, c=c: e.tensor_mul(
                            S_sb[:, g * 256:(g + 1) * 256].rearrange("p (h w) -> p h w", h=4),
                            S_sb[:, g * 256:(g + 1) * 256].rearrange("p (h w) -> p h w", h=4),
                            expTotB[:, c * 64 + 4 * g: c * 64 + 4 * g + 4].unsqueeze(2)
                                .broadcast_to([128, 4, 64])),
                         reads=[("S", g), "expTotB"], writes=[("S", g)], tiny=True)
                    p.op("dve", lambda e, g=g, gi=gi, bC=bC: e.tensor_add(
                            S_sb[:, g * 256:(g + 1) * 256], S_sb[:, g * 256:(g + 1) * 256],
                            pb(bC, 0, 256)),
                         reads=[("S", g), ("ps", bC)], writes=[("S", g)], tiny=True)

                if tile == 0 and c == 0:
                    dtap("d_Y0", Ys[0][:], [("Y", 0, g) for g in range(16)])
                    dtap("d_S1", S_sb[:], [("S", g) for g in range(16)])
                if tile == 0 and c == 1:
                    dtap("d_Y1", Ys[1][:], [("Y", 1, g) for g in range(16)])
                    dtap("d_M41", M4s[1][:], [("M4", 1)])
                    dtap("d_D41", D4s[1][:], [("D4", 1)])

                dskb = c_db[:].rearrange("p (f q) -> p f q", q=2).unsqueeze(3) \
                    .broadcast_to([128, 32, 2, 64])
                xall4 = xtm[:].rearrange("p (f r) -> p f r", r=512)[
                    :, :, c * 128:(c + 1) * 128].rearrange("p f (q w) -> p f q w", q=2)
                p.op("dve", lambda e, xa4=xall4, db4=dskb: e.tensor_mul(xa4, xa4, db4),
                     reads=[("xtm", c, fc) for fc in range(32)],
                     writes=[("xtm", c, fc) for fc in range(32)], tiny=True)
                xall3 = xtm[:].rearrange("p (f r) -> p f r", r=512)[:, :, c * 128:(c + 1) * 128]
                p.op("dve", lambda e, ci=ci, xa3=xall3: e.tensor_add(
                        Ys[ci][:].rearrange("p (f r) -> p f r", r=128),
                        Ys[ci][:].rearrange("p (f r) -> p f r", r=128), xa3),
                     reads=[("Y", ci, g) for g in range(16)] +
                           [("xtm", c, fc) for fc in range(32)],
                     writes=[("Y", ci, g) for g in range(16)], tiny=True)
                if tile == 0 and c == 0:
                    dtap("d_Ydsk", Ys[0][:], [("Y", 0, g) for g in range(16)])
                for ss in range(8):
                    tb = 2 + ss % 2
                    for i in range(4):
                        fc = ss * 4 + i
                        p.op("pe", lambda e, fc=fc, i=i, tb=tb, c=c: e.matmul(
                                pb_bf(tb, i * 128, (i + 1) * 128),
                                zbf[:, fc * 512 + c * 128: fc * 512 + (c + 1) * 128],
                                c_idb[:], is_transpose=True, start=(i == 0), stop=(i == 3)),
                             reads=[("zbf", fc)], writes=[("ps", tb)])
                    extra_s = [("h", 32), ("h", 33), ("h", 34), ("h", 35)] \
                        if (ss < 2 and c == 0) else []
                    p.op("act", lambda e, ss=ss, tb=tb: e.activation(
                            out=silu_sl[ss % 2][:], in_=pb_bf(tb, 0, 512), func=AF.Silu),
                         reads=[("ps", tb)], writes=[("silu", ss % 2)] + extra_s)
                    p.op("dve", lambda e, ss=ss, ci=ci: e.tensor_mul(
                            Ys[ci][:, ss * 512:(ss + 1) * 512],
                            Ys[ci][:, ss * 512:(ss + 1) * 512], silu_sl[ss % 2][:]),
                         reads=[("silu", ss % 2), ("Y", ci, 2 * ss), ("Y", ci, 2 * ss + 1)],
                         writes=[("Y", ci, 2 * ss), ("Y", ci, 2 * ss + 1)], tiny=True)
                for ss in range(8):
                    p.op("act", lambda e, ss=ss, ci=ci, c=c: e.activation(
                            out=pb(4), in_=Ys[ci][:, ss * 512:(ss + 1) * 512], func=AF.Square,
                            accum_out=stats[:, c * 54 + ss: c * 54 + ss + 1]),
                         reads=[("Y", ci, 2 * ss), ("Y", ci, 2 * ss + 1)],
                         writes=[("bn", c, ss), ("ps", 4)], tiny=True)
                p.op("dve", lambda e, c=c: e.tensor_reduce(
                        out=stats[:, c * 54 + 50: c * 54 + 51], in_=stats[:, c * 54: c * 54 + 8],
                        axis=mybir.AxisListType.X, op=ALU.add),
                     reads=[("bn", c, ss) for ss in range(8)], writes=[("msq", c)], tiny=True)
                p.op("dve", lambda e, c=c: e.tensor_scalar(out=stats[:, c * 54 + 51: c * 54 + 52], in0=stats[:, c * 54 + 50: c * 54 + 51],
                                                      scalar1=1.0 / DSSM, scalar2=EPS, op0=ALU.mult, op1=ALU.add),
                     reads=[("msq", c)], writes=[("lnm", c)], tiny=True)
                p.op("dve", lambda e, c=c: e.tensor_scalar(
                        out=stats[:, c * 54 + 52: c * 54 + 53].bitcast(I32),
                        in0=stats[:, c * 54 + 51: c * 54 + 52].bitcast(I32),
                        scalar1=1, scalar2=None, op0=ALU.arith_shift_right),
                     reads=[("lnm", c)], writes=[("rstd_rms", c)], tiny=True)
                p.op("dve", lambda e, c=c: e.tensor_scalar(
                        out=stats[:, c * 54 + 52: c * 54 + 53].bitcast(I32),
                        in0=stats[:, c * 54 + 52: c * 54 + 53].bitcast(I32),
                        scalar1=-1, scalar2=0x5F3759DF, op0=ALU.mult, op1=ALU.add),
                     reads=[("rstd_rms", c)], writes=[("rstd_rms", c)], tiny=True)
                for _nit in range(2):
                    p.op("dve", lambda e, c=c: e.tensor_mul(
                            stats[:, c * 54 + 53: c * 54 + 54],
                            stats[:, c * 54 + 52: c * 54 + 53], stats[:, c * 54 + 52: c * 54 + 53]),
                         reads=[("rstd_rms", c)], writes=[("rsT", c)], tiny=True)
                    p.op("dve", lambda e, c=c: e.tensor_mul(
                            stats[:, c * 54 + 53: c * 54 + 54],
                            stats[:, c * 54 + 53: c * 54 + 54], stats[:, c * 54 + 51: c * 54 + 52]),
                         reads=[("rsT", c), ("lnm", c)], writes=[("rsT", c)], tiny=True)
                    p.op("dve", lambda e, c=c: e.tensor_scalar(
                            out=stats[:, c * 54 + 53: c * 54 + 54],
                            in0=stats[:, c * 54 + 53: c * 54 + 54], scalar1=-0.5, scalar2=1.5,
                            op0=ALU.mult, op1=ALU.add),
                         reads=[("rsT", c)], writes=[("rsT", c)], tiny=True)
                    p.op("dve", lambda e, c=c: e.tensor_mul(
                            stats[:, c * 54 + 52: c * 54 + 53],
                            stats[:, c * 54 + 52: c * 54 + 53], stats[:, c * 54 + 53: c * 54 + 54]),
                         reads=[("rsT", c)], writes=[("rstd_rms", c)], tiny=True)
                import os as _os2
                if _os2.environ.get("K_SHIELD"):
                    p.op("gq", lambda e, tile=tile, c=c: e.dma_start(
                            out=shield[tile][c][:, 0:53], in_=stats[:, c * 54: c * 54 + 53]),
                         reads=[("bn", c, ss) for ss in range(8)] +
                               [("bnagg", c), ("msq", c), ("lnm", c), ("rstd_rms", c)],
                         writes=[], dma=True)
                if tile == 0 and c == 0:
                    dtap("d_yg", Ys[0][:], [("Y", 0, g) for g in range(16)])
                    dtap("d_rstd", stats[:, 0:53],
                         [("bn", c, ss) for ss in range(8)] + [("bnagg", c), ("msq", c), ("lnm", c), ("rstd_rms", c)])
                if tile == 0 and c == 1:
                    dtap("d_ygc1", Ys[1][:], [("Y", 1, g) for g in range(16)])
                    dtap("d_rstd1", stats[:, 54:107],
                         [("bn", c, ss) for ss in range(8)] + [("bnagg", c), ("msq", c), ("lnm", c), ("rstd_rms", c)])
                p.op("dve", lambda e, ci=ci, xa3=xall3, c=c: e.tensor_scalar(
                        out=xa3, in0=Ys[ci][:].rearrange("p (f r) -> p f r", r=128),
                        scalar1=stats[:, c * 54 + 52: c * 54 + 53], scalar2=None, op0=ALU.mult),
                     reads=[("Y", ci, g) for g in range(16)] + [("rstd_rms", c)],
                     writes=[("xtm", c, fc) for fc in range(32)], tiny=True)
                if tile == 0 and c == 0:
                    dtap("d_yn", xtm[:], [("xtm", 0, fc) for fc in range(32)])
                if tile == 0 and c == 1:
                    dtap("d_yn1", xtm[:], [("xtm", 1, fc) for fc in range(32)])
                for ss in range(8):
                    tb = 2 + ss % 2
                    for i in range(4):
                        fc = ss * 4 + i
                        p.op("pe", lambda e, fc=fc, i=i, tb=tb, c=c: e.matmul(
                                pb_bf(tb, i * 128, (i + 1) * 128),
                                xtm[:, fc * 512 + c * 128: fc * 512 + (c + 1) * 128],
                                c_idb[:], is_transpose=True, start=(i == 0), stop=(i == 3)),
                             reads=[("xtm", c, fc)], writes=[("ps", tb)])
                    for i in range(4):
                        fc = ss * 4 + i
                        p.op("dve", lambda e, fc=fc, i=i, tb=tb, c=c: e.tensor_scalar(
                                out=xtm[:, fc * 512 + c * 128: fc * 512 + (c + 1) * 128],
                                in0=pb_bf(tb, i * 128, (i + 1) * 128),
                                scalar1=c_nw[:, fc: fc + 1], scalar2=None, op0=ALU.mult),
                             reads=[("ps", tb)], writes=[("ynfm", c, fc), ("xtm", c, fc)], tiny=True)

            if tile == 0:
                dtap("d_ynfm", xtm[:], [("ynfm", c, fc) for c in range(NCH) for fc in range(32)])

            si_op = 2 * len(E_ORDER)
            for j in range(16):
                bank = j % 2
                ensure_slices(tile, si_op + j * 4, si_op + j * 4 + 3, cap=198)
                for k in range(KF):
                    si = si_op + j * 4 + k // 8
                    p.op("pe", lambda e, bank=bank, tile=tile, si=si, k=k: e.matmul(
                            pb(bank), wap(tile, si, k % 8),
                            xtm[:, k * 512:(k + 1) * 512], start=(k == 0), stop=(k == KF - 1)),
                         waits=[wres(tile, si)] + [("ynfm", cc, k) for cc in range(NCH)],
                         reads=([wres(tile, si)] if k % 8 == 7 else []) +
                               ([("ynfm", cc, kk) for cc in range(NCH) for kk in range(KF)]
                                if (j == 15 and k == KF - 1) else []),
                         writes=([("ps", bank), ("p4done",)] if k == KF - 1 else
                                 ([("ps", bank)] if k == 0 else [])))
                st = j % 2
                src = u_d[s][j * 128:(j + 1) * 128, b * T:(b + 1) * T]
                p.op("gq", lambda e, st=st, src=src: e.dma_start(out=ustgs[st][:], in_=src),
                     reads=[], writes=[("ustg", 0)], dma=True)
                extra_x1 = [("Y", 0, 2 * j), ("Y", 0, 2 * j + 1)] if j < 8 else \
                           [("Y", 1, 2 * (j - 8)), ("Y", 1, 2 * (j - 8) + 1)]
                if j < 2:
                    extra_x1 += [("silu", j)]
                p.op("dve", lambda e, j=j, bank=bank, st=st: e.tensor_add(
                        x1_fm[:, j * T:(j + 1) * T], pb(bank), ustgs[st][:]),
                     reads=[("ps", bank), ("ustg", 0)], writes=[("x1", j)] + extra_x1)
                p.op("dve", lambda e, j=j, st=st: e.tensor_copy(
                        out=xbs[st][:], in_=x1_fm[:, j * T:(j + 1) * T]),
                     reads=[("x1", j)], writes=[("sx", st)])
                p.op("pe", lambda e, j=j, st=st: e.matmul(
                        pb(6, 0, 512)[0:1, :], c_ones128[:], xbs[st][:],
                        start=(j == 0), stop=(j == 15)),
                     reads=[("sx", st)], writes=[("ps", 6)])
                p.op("dve", lambda e, st=st: e.tensor_mul(xbs[st][:], xbs[st][:], xbs[st][:]),
                     reads=[("sx", st)], writes=[("sx", st)])
                p.op("pe", lambda e, j=j, st=st: e.matmul(
                        pb(7, 0, 512)[0:1, :], c_ones128[:], xbs[st][:],
                        start=(j == 0), stop=(j == 15)),
                     reads=[("sx", st)], writes=[("ps", 7)])
            if tile == 0:
                dtap("d_ynfm2", xtm[:], [("p4done",)])
                dtap("d_x1", x1_fm[:], [("x1", j) for j in range(KD)])
            vmean = dtraw_t[0:1, :]; vmsq = tmpB_t[0:1, :]; vtmp = lndt_t[0:1, :]
            vrstd = totB_t[0:1, :]; vnmr = tmpA_t[0:1, :]
            lotmp = dtsoft_t[0:1, :]
            bf_hr = cum_t[0:1, :].bitcast(BF16)[:, 0:512]
            bf_lr = cum_t[0:1, :].bitcast(BF16)[:, 512:1024]
            bf_hn = outsc_t[0:1, :].bitcast(BF16)[:, 0:512]
            bf_ln = outsc_t[0:1, :].bitcast(BF16)[:, 512:1024]
            p.op("dve", lambda e: e.tensor_scalar(out=vmean, in0=pb(6, 0, 512)[0:1, :],
                                                  scalar1=1.0 / D, scalar2=None, op0=ALU.mult),
                 reads=[("ps", 6)], writes=["vmean", "dtraw"], tiny=True)
            p.op("dve", lambda e: e.tensor_scalar(out=vmsq, in0=pb(7, 0, 512)[0:1, :],
                                                  scalar1=1.0 / D, scalar2=None, op0=ALU.mult),
                 reads=[("ps", 7)], writes=["vmsq", "tmpB"], tiny=True)
            p.op("dve", lambda e: e.tensor_mul(vtmp, vmean, vmean),
                 reads=["vmean"], writes=["vtmp", "lndt"], tiny=True)
            p.op("dve", lambda e: e.tensor_sub(vmsq, vmsq, vtmp),
                 reads=["vmsq", "vtmp"], writes=["vmsq"], tiny=True)
            p.op("dve", lambda e: e.tensor_scalar(out=vtmp, in0=vmsq, scalar1=EPS,
                                                  scalar2=None, op0=ALU.add),
                 reads=["vmsq"], writes=["vtmp", "lndt"], tiny=True)
            p.op("dve", lambda e: e.tensor_scalar(
                    out=vrstd.bitcast(I32), in0=vtmp.bitcast(I32),
                    scalar1=1, scalar2=None, op0=ALU.arith_shift_right),
                 reads=["vtmp"], writes=["vrstd", "totB"], tiny=True)
            p.op("dve", lambda e: e.tensor_scalar(
                    out=vrstd.bitcast(I32), in0=vrstd.bitcast(I32),
                    scalar1=-1, scalar2=0x5F3759DF, op0=ALU.mult, op1=ALU.add),
                 reads=["vrstd"], writes=["vrstd"], tiny=True)
            for _nit in range(2):
                p.op("dve", lambda e: e.tensor_mul(vmsq, vrstd, vrstd),
                     reads=["vrstd"], writes=["vmsq"], tiny=True)
                p.op("dve", lambda e: e.tensor_mul(vmsq, vmsq, vtmp),
                     reads=["vmsq", "vtmp"], writes=["vmsq"], tiny=True)
                p.op("dve", lambda e: e.tensor_scalar(out=vmsq, in0=vmsq, scalar1=-0.5,
                                                      scalar2=1.5, op0=ALU.mult, op1=ALU.add),
                     reads=["vmsq"], writes=["vmsq"], tiny=True)
                p.op("dve", lambda e: e.tensor_mul(vrstd, vrstd, vmsq),
                     reads=["vmsq", "vrstd"], writes=["vrstd"], tiny=True)
            p.op("dve", lambda e: e.tensor_mul(vnmr, vmean, vrstd),
                 reads=["vmean", "vrstd"], writes=["vnmr", "tmpA"], tiny=True)
            p.op("dve", lambda e: e.tensor_scalar_mul(vnmr, vnmr, -1.0),
                 reads=["vnmr"], writes=["vnmr"], tiny=True)
            p.op("dve", lambda e: e.tensor_copy(out=bf_hr, in_=vrstd),
                 reads=["vrstd"], writes=["bfhr", "cum"], tiny=True)
            p.op("dve", lambda e: e.tensor_copy(out=bf_hr, in_=vrstd),
                 reads=[], writes=["bfhr"], tiny=True)
            p.op("dve", lambda e: e.tensor_sub(lotmp, vrstd, bf_hr),
                 reads=["vrstd", "bfhr"], writes=["lotmp", "dtsoft"], tiny=True)
            p.op("dve", lambda e: e.tensor_copy(out=bf_lr, in_=lotmp),
                 reads=["lotmp"], writes=["bflr"], tiny=True)
            p.op("dve", lambda e: e.tensor_copy(out=bf_lr, in_=lotmp),
                 reads=[], writes=["bflr"], tiny=True)
            p.op("dve", lambda e: e.tensor_copy(out=bf_hn, in_=vnmr),
                 reads=["vnmr"], writes=["bfhn", "outsc"], tiny=True)
            p.op("dve", lambda e: e.tensor_copy(out=bf_hn, in_=vnmr),
                 reads=[], writes=["bfhn"], tiny=True)
            p.op("dve", lambda e: e.tensor_sub(lotmp, vnmr, bf_hn),
                 reads=["vnmr", "bfhn"], writes=["lotmp"], tiny=True)
            p.op("dve", lambda e: e.tensor_copy(out=bf_ln, in_=lotmp),
                 reads=["lotmp"], writes=["bfln"], tiny=True)
            p.op("dve", lambda e: e.tensor_copy(out=bf_ln, in_=lotmp),
                 reads=[], writes=["bfln"], tiny=True)
            p.op("pe", lambda e: e.matmul(pb(2), c_ones1[:], bf_hr, start=True, stop=False),
                 reads=["bfhr"], writes=[("ps", 2)])
            p.op("pe", lambda e: e.matmul(pb(2), c_ones1[:], bf_lr, start=False, stop=True),
                 reads=["bflr"], writes=[("ps", 2)])
            p.op("pe", lambda e: e.matmul(pb(3), c_ones1[:], bf_hn, start=True, stop=False),
                 reads=["bfhn"], writes=[("ps", 3)])
            p.op("pe", lambda e: e.matmul(pb(3), c_ones1[:], bf_ln, start=False, stop=True),
                 reads=["bfln"], writes=[("ps", 3)])
            for j in range(KD):
                st = j % 2
                p.op("dve", lambda e, j=j, st=st: e.tensor_mul(
                        xas[st][:], x1_fm[:, j * T:(j + 1) * T], pb(2)),
                     reads=[("x1", j), ("ps", 2)], writes=[("cout", st)])
                p.op("dve", lambda e, st=st: e.tensor_add(xas[st][:], xas[st][:], pb(3)),
                     reads=[("cout", st), ("ps", 3)], writes=[("cout", st)])
                p.op("gq", lambda e, j=j, st=st: e.dma_start(out=scr_x1n[j], in_=xas[st][:]),
                     reads=[("cout", st)], writes=[("scrx1n", j)], dma=True)
                p.op("dve", lambda e, j=j, st=st: e.tensor_copy(
                        out=u_bf[:, j * T:(j + 1) * T], in_=xas[st][:]),
                     reads=[("cout", st)], writes=[("ubf", j)])

            if tile == 0:
                dtap("d_x1nbf", u_bf[:], [("ubf", j) for j in range(KD)])

            si_g = si_op + 64
            for jj in range(44):
                bg = 0 if jj % 2 == 0 else 4
                bu = 1 if jj % 2 == 0 else 5
                ensure_slices(tile, si_g + jj * 4, si_g + jj * 4 + 3)
                for k in range(KD):
                    si = si_g + jj * 4 + k // 8
                    p.op("pe", lambda e, bg=bg, tile=tile, si=si, k=k: e.matmul(
                            pb(bg), wap(tile, si, k % 8), u_bf[:, k * T:(k + 1) * T],
                            start=(k == 0), stop=(k == KD - 1)),
                         waits=[wres(tile, si), ("ubf", k)],
                         reads=([wres(tile, si)] if k % 8 == 7 else []),
                         writes=[("ps", bg)] if (k == KD - 1 or k == 0) else [])
                for k in range(KD):
                    si = si_g + jj * 4 + 2 + k // 8
                    p.op("pe", lambda e, bu=bu, tile=tile, si=si, k=k: e.matmul(
                            pb(bu), wap(tile, si, k % 8), u_bf[:, k * T:(k + 1) * T],
                            start=(k == 0), stop=(k == KD - 1)),
                         waits=[wres(tile, si), ("ubf", k)],
                         reads=([wres(tile, si)] if k % 8 == 7 else []) +
                               ([("ubf", k)] if jj == 43 else []),
                         writes=[("ps", bu)] if (k == KD - 1 or k == 0) else [])
                st = jj % 2
                p.op("act", lambda e, jj=jj, bg=bg, st=st: e.activation(
                        out=sxs[st][:], in_=pb(bg), func=AF.Silu, scale=c_sg[:, jj:jj + 1]),
                     reads=[("ps", bg)], writes=[("sx", st)])
                p.op("dve", lambda e, jj=jj, bu=bu, st=st: e.scalar_tensor_tensor(
                        out=h_bf[:, jj * 512:(jj + 1) * 512], in0=pb(bu),
                        scalar=c_su[:, jj:jj + 1], in1=sxs[st][:],
                        op0=ALU.mult, op1=ALU.mult),
                     reads=[("ps", bu), ("sx", st)], writes=[("h", jj)])
            if tile == 0:
                dtap("d_h", h_bf[:], [("h", jj) for jj in range(44)])
            if tile + 1 < NTILE:
                emit_uload(tile + 1)
            si_d = si_g + 176
            for j in range(16):
                bank = j % 2
                ensure_slices(tile, si_d + j * 6, si_d + j * 6 + 5)
                for k in range(KH):
                    si = si_d + j * 6 + k // 8
                    p.op("pe", lambda e, bank=bank, tile=tile, si=si, k=k: e.matmul(
                            pb(bank), wap(tile, si, k % 8),
                            h_bf[:, k * 512:(k + 1) * 512], start=(k == 0), stop=(k == KH - 1)),
                         waits=[wres(tile, si), ("h", k)],
                         reads=([wres(tile, si)] if (k % 8 == 7 or k == KH - 1) else []) +
                               ([("h", kk) for kk in range(KH)] if (j == 15 and k == KH - 1) else []),
                         writes=[("ps", bank)] if (k == KH - 1 or k == 0) else [])
                st = j % 2
                p.op("gq", lambda e, j=j, st=st: e.dma_start(out=x1ss[st], in_=scr_x1n[j]),
                     reads=[("scrx1n", j)], writes=[("cin", st)], dma=True)
                p.op("dve", lambda e, j=j, bank=bank, st=st: e.scalar_tensor_tensor(
                        out=xas[st][:], in0=pb(bank), scalar=c_sd[:, j:j + 1],
                        in1=x1ss[st], op0=ALU.mult, op1=ALU.add),
                     reads=[("ps", bank), ("cin", st)], writes=[("cout", st)])
                tb = 2 + j % 2
                for c in range(NCH):
                    p.op("pe", lambda e, c=c, st=st, tb=tb: e.matmul(
                            pb(tb, c * 128, (c + 1) * 128), xas[st][:, c * 128:(c + 1) * 128],
                            c_idf[:], is_transpose=True, start=(c == 0), stop=(c == 3)),
                         reads=[("cout", st)], writes=[("ps", tb)])
                for c in range(NCH):
                    p.op("dve", lambda e, c=c, j=j, tb=tb: e.tensor_copy(
                            out=x2tm[c][:, j * 128:(j + 1) * 128],
                            in_=pb(tb, c * 128, (c + 1) * 128)),
                         reads=[("ps", tb)], tiny=True,
                         writes=[("x2tm", c, j)] +
                                ([("zbf", zz) for zz in range(c * 8, c * 8 + 8)]
                                 if j == 0 else []))

            if tile == 0:
                dtap("d_x2tm", x2tm[0][:], [("x2tm", 0, jj) for jj in range(16)])
            for c in range(NCH):
                for ss in range(4):
                    p.op("act", lambda e, c=c, ss=ss: e.activation(
                            out=pb(4), in_=x2tm[c][:, ss * 512:(ss + 1) * 512], func=AF.Copy,
                            accum_out=stats[:, ss: ss + 1]),
                         reads=[("x2tm", c, jj) for jj in range(ss * 4, ss * 4 + 4)],
                         writes=[("bn", ss), ("ps", 4)], tiny=True)
                    p.op("act", lambda e, c=c, ss=ss: e.activation(
                            out=pb(5), in_=x2tm[c][:, ss * 512:(ss + 1) * 512], func=AF.Square,
                            accum_out=stats[:, 4 + ss: 5 + ss]),
                         reads=[("x2tm", c, jj) for jj in range(ss * 4, ss * 4 + 4)],
                         writes=[("bn", 4 + ss), ("ps", 5)], tiny=True)
                if TAP7 and tile == 7 and c == 3:
                    p.op("gq", lambda e: e.dma_start(out=d7["d7_stats"][:], in_=stats[:, 0:8]),
                         reads=[("bn", ss) for ss in range(8)], writes=[], dma=True)
                p.op("dve", lambda e: e.tensor_reduce(
                        out=lncol[:, 3:4], in_=stats[:, 0:4], axis=mybir.AxisListType.X, op=ALU.add),
                     reads=[("bn", 0), ("bn", 1), ("bn", 2), ("bn", 3)], writes=["l2sum"], tiny=True)
                p.op("dve", lambda e: e.tensor_reduce(
                        out=lncol[:, 4:5], in_=stats[:, 4:8], axis=mybir.AxisListType.X, op=ALU.add),
                     reads=[("bn", 4), ("bn", 5), ("bn", 6), ("bn", 7)], writes=["l2sq"], tiny=True)
                p.op("dve", lambda e: e.tensor_scalar(out=lncol[:, 5:6], in0=lncol[:, 3:4],
                                                      scalar1=1.0 / D, scalar2=None, op0=ALU.mult),
                     reads=["l2sum"], writes=["l2mean"], tiny=True)
                p.op("dve", lambda e: e.tensor_scalar(out=lncol[:, 6:7], in0=lncol[:, 4:5],
                                                      scalar1=1.0 / D, scalar2=None, op0=ALU.mult),
                     reads=["l2sq"], writes=["l2msq"], tiny=True)
                p.op("dve", lambda e: e.tensor_mul(lncol[:, 7:8], lncol[:, 5:6], lncol[:, 5:6]),
                     reads=["l2mean"], writes=["l2m2"], tiny=True)
                p.op("dve", lambda e: e.tensor_sub(lncol[:, 0:1], lncol[:, 6:7], lncol[:, 7:8]),
                     reads=["l2msq", "l2m2"], writes=["l2a"], tiny=True)
                p.op("dve", lambda e: e.tensor_scalar(out=lncol[:, 0:1], in0=lncol[:, 0:1],
                                                      scalar1=EPS, scalar2=None, op0=ALU.add),
                     reads=["l2a"], writes=["l2a"], tiny=True)
                p.op("dve", lambda e: e.tensor_scalar(
                        out=lncol[:, 1:2].bitcast(I32), in0=lncol[:, 0:1].bitcast(I32),
                        scalar1=1, scalar2=None, op0=ALU.arith_shift_right),
                     reads=["l2a"], writes=["l2r"], tiny=True)
                p.op("dve", lambda e: e.tensor_scalar(
                        out=lncol[:, 1:2].bitcast(I32), in0=lncol[:, 1:2].bitcast(I32),
                        scalar1=-1, scalar2=0x5F3759DF, op0=ALU.mult, op1=ALU.add),
                     reads=["l2r"], writes=["l2r"], tiny=True)
                for _nit in range(2):
                    p.op("dve", lambda e: e.tensor_mul(lncol[:, 7:8], lncol[:, 1:2], lncol[:, 1:2]),
                         reads=["l2r"], writes=["l2m2"], tiny=True)
                    p.op("dve", lambda e: e.tensor_mul(lncol[:, 7:8], lncol[:, 7:8], lncol[:, 0:1]),
                         reads=["l2m2", "l2a"], writes=["l2m2"], tiny=True)
                    p.op("dve", lambda e: e.tensor_scalar(out=lncol[:, 7:8], in0=lncol[:, 7:8],
                                                          scalar1=-0.5, scalar2=1.5,
                                                          op0=ALU.mult, op1=ALU.add),
                         reads=["l2m2"], writes=["l2m2"], tiny=True)
                    p.op("dve", lambda e: e.tensor_mul(lncol[:, 1:2], lncol[:, 1:2], lncol[:, 7:8]),
                         reads=["l2m2", "l2r"], writes=["l2r"], tiny=True)
                p.op("dve", lambda e: e.tensor_mul(lncol[:, 2:3], lncol[:, 5:6],
                                                   lncol[:, 1:2]),
                     reads=["l2mean", "l2r"], writes=["l2n"], tiny=True)
                p.op("dve", lambda e: e.tensor_scalar_mul(lncol[:, 2:3], lncol[:, 2:3], -1.0),
                     reads=["l2n"], writes=["l2n"], tiny=True)
                if TAP7 and tile == 7 and c == 3:
                    p.op("gq", lambda e: e.dma_start(out=d7["d7_lncol"][:], in_=lncol[:]),
                         reads=["l2sum", "l2sq", "l2mean", "l2msq", "l2m2", "l2a", "l2r", "l2n"],
                         writes=[], dma=True)
                    p.op("gq", lambda e: e.dma_start(out=d7["d7_x2"][:], in_=x2tm[3][:]),
                         reads=[("x2tm", 3, jj) for jj in range(16)], writes=[], dma=True)
                p.op("dve", lambda e, c=c: e.tensor_scalar(
                        out=x2tm[c][:], in0=x2tm[c][:], scalar1=lncol[:, 1:2],
                        scalar2=lncol[:, 2:3], op0=ALU.mult, op1=ALU.add),
                     reads=["l2r", "l2n"] + [("x2tm", c, jj) for jj in range(16)],
                     writes=[("x2f", c)] + [("x2tm", c, jj) for jj in range(16)])
                if tile == 0 and c == 0:
                    dtap("d_ln2", stats[:, 0:32], [("bn", ss) for ss in range(4)] + ["bnagg"])
                    dtap("d_x2post", x2tm[0][:], [("x2f", 0)])
                dst = o_d[s][b * T + c * 128: b * T + (c + 1) * 128, :]
                p.op("gq", lambda e, c=c, dst=dst: e.dma_start(out=dst, in_=x2tm[c][:]),
                     reads=[("x2f", c)] + [("x2tm", c, jj) for jj in range(16)],
                     writes=[("store", tile, c)], dma=True)

        p.finalize()
        totals = dict(p.totals)

        @block.tensor
        def _(eng):
            p.emit_stream(eng, sems, ("pe",))

        @block.scalar
        def _(eng):
            p.emit_stream(eng, sems, ("act",),
                          spacer=lambda e: e.activation(out=stats[:, 9:10], in_=tails[:, 0:1],
                                                        func=AF.Copy))

        @block.vector
        def _(eng):
            p.emit_stream(eng, sems, ("dve",),
                          spacer=lambda e: e.tensor_copy(out=stats[:, 8:9], in_=tails[:, 0:1]))

        @block.sync
        def _(eng):
            p.emit_stream(eng, sems, ("sync",))

        @block.gpsimd
        def _(eng):
            p.emit_stream(eng, sems, ("gq",))
            for qi in range(DMAPOOL):
                t = totals.get(("gq", qi), 0)
                if t:
                    eng.wait_ge(sems[("gq", qi)], t)

    import sys as _sys
    print(f"[kernel] ops={len(p.ops)} spacers={getattr(p, 'n_spacers', 0)}", file=_sys.stderr)
    return nc


_NC_CACHE = None


def kernel(**inputs):
    global _NC_CACHE
    shared, us = host_prep(inputs)
    if _NC_CACHE is None:
        _NC_CACHE = build_program()
    nc = _NC_CACHE
    in_maps = []
    for c in range(NCORES):
        m = dict(shared)
        m["u"] = us[c]
        in_maps.append(m)
    res = run_bass_kernel_spmd(nc, in_maps, core_ids=list(range(NCORES)))
    out = np.empty((16, 2048, D), np.float32)
    for c in range(NCORES):
        out[2 * c:2 * c + 2] = np.asarray(res.results[c]["o"])
    return out


# version 02
# speedup vs baseline: 1.0124x; 1.0124x over previous
import numpy as np
import ml_dtypes
import concourse.bass as bass
import concourse.mybir as mybir
from concourse.bass_utils import run_bass_kernel_spmd

F32 = mybir.dt.float32
BF16 = mybir.dt.bfloat16
I32 = mybir.dt.int32
AF = mybir.ActivationFunctionType
ALU = mybir.AluOpType
BF = ml_dtypes.bfloat16

NCORES = 8
D = 2048
DSSM = 4096
NH = 64
ECH_IN = 67
KD = 16
KF = 32
KH = 44
T = 512
NTILE = 8
NCH = 4
EPS = 1e-5

E_ORDER = [66] + list(range(32, 66)) + list(range(32))


def _wslices():
    sl = []
    for j in E_ORDER:
        sl.append(("w_in", j, 0, 8)); sl.append(("w_in", j, 8, 16))
    for j in range(16):
        for q in range(4):
            sl.append(("w_out", j, q * 8, q * 8 + 8))
    for j in range(44):
        sl.append(("wqg", j, 0, 8)); sl.append(("wqg", j, 8, 16))
        sl.append(("wqu", j, 0, 8)); sl.append(("wqu", j, 8, 16))
    for j in range(16):
        for q in range(6):
            sl.append(("wqd", j, q * 8, min(q * 8 + 8, 44)))
    return sl


WSLICES = _wslices()
DEBUG_TILE0 = False
NSL = len(WSLICES)
NSLOT = 8


DMAPOOL = 24


class Prog:

    def __init__(self):
        self.ops = []
        self.last_w = {}
        self.readers = {}
        self.dmacnt = {"sync": 0, "gq": 0}

    def op(self, sem_key, fn, reads=(), writes=(), dma=False, tiny=False, waits=()):
        idx = len(self.ops)
        raw = set()
        for r in reads:
            w = self.last_w.get(r)
            if w is not None:
                raw.add(w)
        for r in waits:
            w = self.last_w.get(r)
            if w is not None:
                raw.add(w)
        deps = set(raw)
        for r in writes:
            w = self.last_w.get(r)
            if w is not None:
                deps.add(w)
            for rd in self.readers.get(r, ()):
                deps.add(rd)
        if dma:
            queue = "sync" if sem_key.startswith("sync") else "gq"
            qi = self.dmacnt[queue]
            self.dmacnt[queue] = qi + 1
            stream = queue
            sem = (queue, qi % DMAPOOL)
        else:
            stream = sem_key
            sem = sem_key
        import os
        if os.environ.get("KDRAIN_ALL") and not dma:
            tiny = True
        self.ops.append({"sem": sem, "stream": stream, "fn": fn, "deps": deps,
                         "raw": raw, "dma": dma, "tiny": tiny})
        for r in reads:
            self.readers.setdefault(r, []).append(idx)
        for r in writes:
            self.last_w[r] = idx
            self.readers[r] = []
        return idx

    def finalize(self):
        n = len(self.ops)
        self.need_inc = [o["dma"] for o in self.ops]
        for o in self.ops:
            for d in o["deps"]:
                if self.ops[d]["stream"] != o["stream"] or self.ops[d]["dma"]:
                    self.need_inc[d] = True
        cnt = {}
        self.val = [0] * n
        for i, o in enumerate(self.ops):
            if self.need_inc[i]:
                cnt[o["sem"]] = cnt.get(o["sem"], 0) + (16 if o["dma"] else 1)
            self.val[i] = cnt.get(o["sem"], 0)
        self.totals = cnt

    def emit_stream(self, eng, sems, streams, spacer=None):
        watermark = {}
        drained_at = -1
        prev = None
        for i, o in enumerate(self.ops):
            if o["stream"] not in streams:
                continue
            want = {}
            need_drain = False
            for d in o["deps"]:
                dop = self.ops[d]
                if dop["stream"] == o["stream"] and not dop["dma"]:
                    if dop.get("tiny") and d > drained_at and not o["dma"]:
                        need_drain = True
                    continue
                if dop["stream"] == o["stream"] and dop["dma"] and not o["dma"]:
                    pass
                v = self.val[d]
                ds = dop["sem"]
                if v > want.get(ds, 0):
                    want[ds] = v
            if o["dma"]:
                pv = self.val[i] - 16
                if pv > 0 and pv > want.get(o["sem"], 0):
                    want[o["sem"]] = pv
            for ds, v in sorted(want.items(), key=lambda kv: str(kv[0])):
                if v > watermark.get(ds, 0):
                    eng.wait_ge(sems[ds], v)
                    watermark[ds] = v
            if need_drain:
                eng.drain()
                drained_at = i
            if (spacer is not None and prev is not None and prev in o["raw"]
                    and self.ops[prev]["tiny"] and not self.need_inc[prev]):
                spacer(eng)
                self.n_spacers = getattr(self, "n_spacers", 0) + 1
            ins = o["fn"](eng)
            if self.need_inc[i]:
                if o.get("tiny") and not o["dma"]:
                    eng.drain().then_inc(sems[o["sem"]], 1)
                    drained_at = i
                else:
                    ins.then_inc(sems[o["sem"]], 16 if o["dma"] else 1)
            prev = i


def host_prep(inputs):
    d = {}
    u = np.asarray(inputs["L0_attn_u"], np.float32)
    w_in = np.asarray(inputs["L0_attn_W_in"], np.float32)
    w_pad = np.zeros((ECH_IN * 128, D), np.float32)
    w_pad[:8512] = w_in
    d["w_in"] = np.ascontiguousarray(
        w_pad.reshape(ECH_IN, 128, KD, 128).transpose(0, 3, 2, 1)).reshape(ECH_IN, 128, KD * 128)
    w_out = np.asarray(inputs["L0_attn_W_out"], np.float32)
    d["w_out"] = np.ascontiguousarray(
        w_out.reshape(16, 128, KF, 128).transpose(0, 3, 2, 1)).reshape(16, 128, KF * 128)
    for nm, key, ne, kk in (("wqg", "L0_ffn_wq_gate", 44, KD),
                            ("wqu", "L0_ffn_wq_up", 44, KD),
                            ("wqd", "L0_ffn_wq_down", 16, KH)):
        wq = np.asarray(inputs[key], np.int32)
        d[nm] = np.ascontiguousarray(
            wq.reshape(ne, 128, kk, 128).transpose(0, 3, 2, 1)).reshape(ne, 128, kk * 128)
    cw = np.asarray(inputs["L0_attn_conv_w"], np.float32).reshape(4352, 4)
    d["cw"] = np.ascontiguousarray(cw.reshape(34, 128, 4).transpose(1, 0, 2).reshape(128, 136))
    d["cb"] = np.ascontiguousarray(np.asarray(inputs["L0_attn_conv_b"], np.float32).reshape(34, 128).T)
    d["dtb"] = np.ascontiguousarray(np.asarray(inputs["L0_attn_dt_bias"], np.float32).reshape(NH, 1))
    d["alog"] = np.ascontiguousarray(np.asarray(inputs["L0_attn_A_log"], np.float32).reshape(NH, 1))
    d["db"] = np.ascontiguousarray(
        np.broadcast_to(np.asarray(inputs["L0_attn_D"], np.float32)[None, :], (128, NH)))
    d["nw"] = np.ascontiguousarray(np.asarray(inputs["L0_attn_norm_w"], np.float32).reshape(32, 128).T)
    d["sg"] = np.ascontiguousarray(np.asarray(inputs["L0_ffn_s_gate"], np.float32).reshape(44, 128).T)
    d["su"] = np.ascontiguousarray(np.asarray(inputs["L0_ffn_s_up"], np.float32).reshape(44, 128).T)
    d["sd"] = np.ascontiguousarray(np.asarray(inputs["L0_ffn_s_down"], np.float32).reshape(16, 128).T)
    d["idf"] = np.eye(128, dtype=np.float32)
    d["idb"] = np.eye(128, dtype=np.float32).astype(BF)
    d["id64"] = np.eye(64, dtype=np.float32).astype(BF)
    d["ones128"] = np.ones((128, 1), np.float32).astype(BF)
    d["ones1"] = np.ones((1, 128), np.float32).astype(BF)
    mask = np.zeros((128, 128), np.float32)
    mv = np.float32(ml_dtypes.bfloat16(np.exp(5.0)))
    for s in range(128):
        mask[s, s:] = mv
    d["mask"] = mask.astype(BF)
    d["zeros"] = np.zeros((128, DSSM), np.float32)
    us = [np.ascontiguousarray(u[2 * c:2 * c + 2].transpose(0, 2, 1)) for c in range(NCORES)]
    return d, us


def build_program():
    nc = bass.Bass("TRN2", target_bir_lowering=False)
    io = {}

    def din(name, shape, dt):
        io[name] = nc.dram_tensor(name, shape, dt, kind="ExternalInput").ap()
        return io[name]

    u_d = din("u", [2, D, 2048], F32)
    wdram = {
        "w_in": din("w_in", [ECH_IN, 128, KD * 128], F32),
        "w_out": din("w_out", [16, 128, KF * 128], F32),
        "wqg": din("wqg", [44, 128, KD * 128], I32),
        "wqu": din("wqu", [44, 128, KD * 128], I32),
        "wqd": din("wqd", [16, 128, KH * 128], I32),
    }
    cw_d = din("cw", [128, 136], F32)
    cb_d = din("cb", [128, 34], F32)
    dtb_d = din("dtb", [NH, 1], F32)
    alog_d = din("alog", [NH, 1], F32)
    db_d = din("db", [128, NH], F32)
    nw_d = din("nw", [128, 32], F32)
    sg_d = din("sg", [128, 44], F32)
    su_d = din("su", [128, 44], F32)
    sd_d = din("sd", [128, 16], F32)
    idf_d = din("idf", [128, 128], F32)
    idb_d = din("idb", [128, 128], BF16)
    id64_d = din("id64", [64, 64], BF16)
    ones128_d = din("ones128", [128, 1], BF16)
    ones1_d = din("ones1", [1, 128], BF16)
    mask_d = din("mask", [128, 128], BF16)
    zeros_d = din("zeros", [128, DSSM], F32)

    o_d = nc.dram_tensor("o", [2, 2048, D], F32, kind="ExternalOutput").ap()
    scr = {
        "w_in": nc.dram_tensor("scr_in", [ECH_IN, 128, KD * 128], BF16, kind="Internal").ap(),
        "w_out": nc.dram_tensor("scr_out", [16, 128, KF * 128], BF16, kind="Internal").ap(),
        "wqg": nc.dram_tensor("scr_g", [44, 128, KD * 128], BF16, kind="Internal").ap(),
        "wqu": nc.dram_tensor("scr_u", [44, 128, KD * 128], BF16, kind="Internal").ap(),
        "wqd": nc.dram_tensor("scr_d", [16, 128, KH * 128], BF16, kind="Internal").ap(),
    }
    scr_x1n = nc.dram_tensor("scr_x1n", [KD, 128, T], F32, kind="Internal").ap()
    shield = nc.dram_tensor("shield", [NTILE, NCH, 128, 64], F32, kind="Internal").ap()
    import os as _os7
    TAP7 = bool(_os7.environ.get("K_TAP7"))
    d7 = {}
    if TAP7:
        d7["d7_stats"] = nc.dram_tensor("d7_stats", [128, 8], F32, kind="ExternalOutput").ap()
        d7["d7_lncol"] = nc.dram_tensor("d7_lncol", [128, 8], F32, kind="ExternalOutput").ap()
        d7["d7_x2"] = nc.dram_tensor("d7_x2", [128, 2048], F32, kind="ExternalOutput").ap()
    dbg = {}
    if DEBUG_TILE0:
        for nm, shape, dt in (
            ("d_ubf", [128, KD * T], BF16), ("d_zbf", [128, 32 * T], BF16),
            ("d_xtm", [128, 32 * T], BF16), ("d_bcbf", [128, 2 * T], BF16),
            ("d_dtsoft", [64, T], F32), ("d_cum", [64, T], F32),
            ("d_biasH", [64, T], BF16), ("d_biasL", [64, T], BF16),
            ("d_outscT", [128, 256], F32), ("d_expcum", [128, 256], F32),
            ("d_expTotB", [128, 256], F32), ("d_cumth", [128, 256], F32),
            ("d_Y0", [128, 4096], F32), ("d_ynfm", [128, 32 * T], BF16),
            ("d_x1", [128, KD * T], F32), ("d_x1nbf", [128, KD * T], BF16),
            ("d_h", [128, 44 * T], BF16), ("d_x2tm", [128, 2048], F32),
            ("d_S1", [128, 4096], F32),
            ("d_Ydsk", [128, 4096], F32), ("d_yg", [128, 4096], F32),
            ("d_Y1", [128, 4096], F32), ("d_M41", [128, 512], BF16), ("d_D41", [128, 512], BF16),
            ("d_ygc1", [128, 4096], F32), ("d_rstd1", [128, 53], F32),
            ("d_yn1", [128, 32 * T], BF16), ("d_ynfm2", [128, 32 * T], BF16),
            ("d_ln2", [128, 32], F32), ("d_x2post", [128, 2048], F32),
            ("d_rstd", [128, 53], F32), ("d_yn", [128, 32 * T], BF16),
        ):
            dbg[nm] = nc.dram_tensor(nm, shape, dt, kind="ExternalOutput").ap()

    p = Prog()

    import contextlib
    with contextlib.ExitStack() as _stk:
        c_idf = _stk.enter_context(nc.sbuf_tensor("c_idf", [128, 128], F32))
        c_idb = _stk.enter_context(nc.sbuf_tensor("c_idb", [128, 128], BF16))
        c_id64 = _stk.enter_context(nc.sbuf_tensor("c_id64", [64, 64], BF16))
        c_ones128 = _stk.enter_context(nc.sbuf_tensor("c_ones128", [128, 1], BF16))
        c_ones1 = _stk.enter_context(nc.sbuf_tensor("c_ones1", [1, 128], BF16))
        c_mask = _stk.enter_context(nc.sbuf_tensor("c_mask", [128, 128], BF16))
        c_cw = _stk.enter_context(nc.sbuf_tensor("c_cw", [128, 136], F32))
        c_cb = _stk.enter_context(nc.sbuf_tensor("c_cb", [128, 34], F32))
        c_dtb = _stk.enter_context(nc.sbuf_tensor("c_dtb", [NH, 1], F32))
        c_alog = _stk.enter_context(nc.sbuf_tensor("c_alog", [NH, 1], F32))
        c_negea = _stk.enter_context(nc.sbuf_tensor("c_negea", [NH, 2], F32))
        c_db = _stk.enter_context(nc.sbuf_tensor("c_db", [128, NH], F32))
        c_nw = _stk.enter_context(nc.sbuf_tensor("c_nw", [128, 32], F32))
        c_sg = _stk.enter_context(nc.sbuf_tensor("c_sg", [128, 44], F32))
        c_su = _stk.enter_context(nc.sbuf_tensor("c_su", [128, 44], F32))
        c_sd = _stk.enter_context(nc.sbuf_tensor("c_sd", [128, 16], F32))
        u_bf = _stk.enter_context(nc.sbuf_tensor("u_bf", [128, KD * T], BF16))
        zbf = _stk.enter_context(nc.sbuf_tensor("zbf", [128, 32 * T], BF16))
        xtm = _stk.enter_context(nc.sbuf_tensor("xtm", [128, 32 * T], BF16))
        bcbf = _stk.enter_context(nc.sbuf_tensor("bcbf", [128, 2 * T], BF16))
        S_sb = _stk.enter_context(nc.sbuf_tensor("S", [128, DSSM], F32))
        S_bf0 = _stk.enter_context(nc.sbuf_tensor("S_bf0", [128, 1024], BF16))
        S_bf1 = _stk.enter_context(nc.sbuf_tensor("S_bf1", [128, 1024], BF16))
        arena = _stk.enter_context(nc.sbuf_tensor("arena", [128, 22528], BF16))
        dtraw_t = _stk.enter_context(nc.sbuf_tensor("dtraw_t", [64, T], F32))
        tmpA_t = _stk.enter_context(nc.sbuf_tensor("tmpA_t", [64, T], F32))
        tmpB_t = _stk.enter_context(nc.sbuf_tensor("tmpB_t", [64, T], F32))
        dtsoft_t = _stk.enter_context(nc.sbuf_tensor("dtsoft_t", [64, T], F32))
        lndt_t = _stk.enter_context(nc.sbuf_tensor("lndt_t", [64, T], F32))
        cum_t = _stk.enter_context(nc.sbuf_tensor("cum_t", [64, T], F32))
        totB_t = _stk.enter_context(nc.sbuf_tensor("totB_t", [64, T], F32))
        outsc_t = _stk.enter_context(nc.sbuf_tensor("outsc_t", [64, T], F32))
        biasH = _stk.enter_context(nc.sbuf_tensor("biasH", [64, T], BF16))
        biasL = _stk.enter_context(nc.sbuf_tensor("biasL", [64, T], BF16))
        zt64 = _stk.enter_context(nc.sbuf_tensor("zt64", [64, 128], F32))
        cum_th = _stk.enter_context(nc.sbuf_tensor("cum_th", [128, 256], F32))
        expcum = _stk.enter_context(nc.sbuf_tensor("expcum", [128, 256], F32))
        outscT = _stk.enter_context(nc.sbuf_tensor("outscT", [128, 256], F32))
        expTotB = _stk.enter_context(nc.sbuf_tensor("expTotB", [128, 256], F32))
        Gtm0 = _stk.enter_context(nc.sbuf_tensor("Gtm0", [128, 128], BF16))
        Gtm1 = _stk.enter_context(nc.sbuf_tensor("Gtm1", [128, 128], BF16))
        BT0 = _stk.enter_context(nc.sbuf_tensor("BT0", [128, 128], BF16))
        BT1 = _stk.enter_context(nc.sbuf_tensor("BT1", [128, 128], BF16))
        M4_0 = _stk.enter_context(nc.sbuf_tensor("M4_0", [128, 512], BF16))
        M4_1 = _stk.enter_context(nc.sbuf_tensor("M4_1", [128, 512], BF16))
        D4_0 = _stk.enter_context(nc.sbuf_tensor("D4_0", [128, 512], BF16))
        D4_1 = _stk.enter_context(nc.sbuf_tensor("D4_1", [128, 512], BF16))
        xo0 = _stk.enter_context(nc.sbuf_tensor("xo0", [128, 256], BF16))
        xo1 = _stk.enter_context(nc.sbuf_tensor("xo1", [128, 256], BF16))
        t4_0 = _stk.enter_context(nc.sbuf_tensor("t4_0", [128, 256], BF16))
        t4_1 = _stk.enter_context(nc.sbuf_tensor("t4_1", [128, 256], BF16))
        stats = _stk.enter_context(nc.sbuf_tensor("stats", [128, 216], F32))
        cin0 = _stk.enter_context(nc.sbuf_tensor("cin0", [128, 515], F32))
        cin1 = _stk.enter_context(nc.sbuf_tensor("cin1", [128, 515], F32))
        cout0 = _stk.enter_context(nc.sbuf_tensor("cout0", [128, 512], F32))
        cout1 = _stk.enter_context(nc.sbuf_tensor("cout1", [128, 512], F32))
        sx0 = _stk.enter_context(nc.sbuf_tensor("sx0", [128, 512], BF16))
        sx1 = _stk.enter_context(nc.sbuf_tensor("sx1", [128, 512], BF16))
        tails = _stk.enter_context(nc.sbuf_tensor("tails", [128, 102], F32))
        wring = _stk.enter_context(nc.sbuf_tensor("wring", [128, NSLOT * 1024], BF16))
        ustg0 = _stk.enter_context(nc.sbuf_tensor("ustg0", [128, T], F32))
        lncol = _stk.enter_context(nc.sbuf_tensor("lncol", [128, 8], F32))
        psum = _stk.enter_context(nc.psum_tensor("psum", [128, 4096], F32))
        s_pe = _stk.enter_context(nc.semaphore("s_pe"))
        s_act = _stk.enter_context(nc.semaphore("s_act"))
        s_dve = _stk.enter_context(nc.semaphore("s_dve"))
        dma_sems = {}
        for qn in ("sync", "gq"):
            for qi in range(DMAPOOL):
                dma_sems[(qn, qi)] = _stk.enter_context(nc.semaphore(f"s_{qn}{qi}"))
        block = _stk.enter_context(nc.Block())
        sems = {"pe": s_pe, "act": s_act, "dve": s_dve}
        sems.update(dma_sems)
        Ys = [arena[:, 0:8192].bitcast(F32), arena[:, 8192:16384].bitcast(F32)]
        silu_sl = [arena[:, 16384:17408].bitcast(F32), arena[:, 17408:18432].bitcast(F32)]
        stage_f = [arena[:, 18432:20480].bitcast(F32), arena[:, 20480:22528].bitcast(F32)]
        stage_i = [arena[:, 18432:20480].bitcast(I32), arena[:, 20480:22528].bitcast(I32)]
        stage5_f = [xtm[:, 0:2048].bitcast(F32), xtm[:, 2048:4096].bitcast(F32)]
        stage5_i = [xtm[:, 0:2048].bitcast(I32), xtm[:, 2048:4096].bitcast(I32)]
        x1_fm = arena[:, 0:16384].bitcast(F32)
        h_bf = arena
        x2tm = [zbf[:, c * 4096:(c + 1) * 4096].bitcast(F32) for c in range(NCH)]
        Gtms = [Gtm0, Gtm1]; BTs = [BT0, BT1]; M4s = [M4_0, M4_1]; D4s = [D4_0, D4_1]
        xos = [xo0, xo1]; t4s = [t4_0, t4_1]; S_bfs = [S_bf0, S_bf1]
        cins = [cin0, cin1]; couts = [cout0, cout1]; sxs = [sx0, sx1]
        ustgs = [ustg0, ustg0]; xas = couts; xbs = sxs
        x1ss = [cin0[:, 0:512], cin1[:, 0:512]]

        def pb(b, lo=0, hi=512):
            return psum[:, 512 * b + lo: 512 * b + hi]

        def pb_bf(b, lo=0, hi=1024):
            return psum[:, 512 * b: 512 * (b + 1)].bitcast(BF16)[:, lo:hi]

        def xtm3(c):
            return xtm[:].rearrange("p (f r) -> p f r", r=512)[:, :, c * 128:(c + 1) * 128]

        def xtm_head(c, h):
            off = (h // 2) * 512 + c * 128 + (h % 2) * 64
            return xtm[:, off: off + 64]

        for dst, src, wr in (
            (c_idf[:], idf_d, []), (c_idb[:], idb_d, []), (c_id64[:], id64_d, []),
            (c_ones128[:], ones128_d, []), (c_ones1[:], ones1_d, []), (c_mask[:], mask_d, []),
            (c_cw[:], cw_d, []), (c_cb[:], cb_d, []), (c_dtb[:], dtb_d, ["dtb"]),
            (c_alog[:], alog_d, ["alog"]), (c_db[:], db_d, []), (c_nw[:], nw_d, []),
            (c_sg[:], sg_d, []), (c_su[:], su_d, []), (c_sd[:], sd_d, []),
            (zt64[:], zeros_d[0:64, 0:128], []),
        ):
            p.op("gq", lambda e, d=dst, s=src: e.dma_start(out=d, in_=s),
                 reads=[], writes=wr, dma=True)
        p.op("act", lambda e: e.activation(out=c_negea[:, 0:1], in_=c_alog[:], func=AF.Exp),
             reads=["alog"], writes=["negea"])
        for _tw in range(2):
            p.op("dve", lambda e: e.tensor_scalar_mul(c_negea[:, 1:2], c_negea[:, 0:1], -1.0),
                 reads=["negea"] if _tw == 0 else [], writes=["negea"], tiny=True)
        p.op("gq", lambda e: e.dma_start(out=tails[:], in_=zeros_d[:, 0:102]),
             reads=[], writes=["tails", "constbar"], dma=True)
        for ek in ("pe", "act", "dve"):
            p.op(ek, lambda e: e.nop(), reads=["constbar"], writes=[])

        wslot_of = {}
        wcursor = {"tile": -1, "i": 0}
        WLOOK = 10

        def emit_slice(tile, si):
            skey = "sync0" if tile < 4 else "sync1"
            mat, j, k0, k1 = WSLICES[si]
            gi = tile * NSL + si
            slot = gi % NSLOT
            wslot_of[gi] = slot
            ncol = (k1 - k0) * 128
            dstb = wring[:, slot * 1024: slot * 1024 + ncol]
            if tile == 0:
                st = si % 2
                is_int = mat in ("wqg", "wqu", "wqd")
                src = wdram[mat][j][:, k0 * 128: k1 * 128]
                if is_int:
                    extra = [("ynfm", cc, fc) for cc in range(NCH)
                             for fc in (range(0, 4) if st == 0 else range(4, 8))]
                    stg = stage5_i[st][:, 0:ncol]
                    p.op("sync0", lambda e, d=stg, s=src: e.dma_start(out=d, in_=s),
                         reads=[], writes=[("stg5", st)] + extra, dma=True)
                    p.op("dve", lambda e, d=dstb, s=stg: e.tensor_scalar(
                            out=d, in0=s, scalar1=128.0, scalar2=None, op0=ALU.subtract),
                         reads=[("stg5", st)], writes=[("wr", slot)])
                else:
                    stg = stage_f[st][:, 0:ncol]
                    p.op("sync0", lambda e, d=stg, s=src: e.dma_start(out=d, in_=s),
                         reads=[], writes=[("stg", st)], dma=True)
                    p.op("dve", lambda e, d=dstb, s=stg: e.tensor_copy(out=d, in_=s),
                         reads=[("stg", st)], writes=[("wr", slot)])
                sdst = scr[mat][j][:, k0 * 128: k1 * 128]
                p.op("gq", lambda e, d=sdst, s=dstb: e.dma_start(out=d, in_=s),
                     reads=[("wr", slot)], writes=[("scr", mat, j, k0)], dma=True)
            else:
                src = scr[mat][j][:, k0 * 128: k1 * 128]
                p.op(skey, lambda e, d=dstb, s=src: e.dma_start(out=d, in_=s),
                     reads=[("scr", mat, j, k0)], writes=[("wr", slot)], dma=True)

        def ensure_slices(tile, si_first, si_last, cap=None):
            if wcursor["tile"] != tile:
                wcursor["tile"] = tile
                wcursor["i"] = 0
            target = min(si_first + NSLOT, si_last + 1 + WLOOK, NSL)
            if cap is not None:
                target = min(target, cap)
            target = max(target, si_last + 1)
            while wcursor["i"] < target:
                emit_slice(tile, wcursor["i"])
                wcursor["i"] += 1

        def wres(tile, si):
            return ("wr", wslot_of[tile * NSL + si])

        def wap(tile, si, k):
            slot = wslot_of[tile * NSL + si]
            return wring[:, slot * 1024 + k * 128: slot * 1024 + (k + 1) * 128]

        import os as _os
        _tapsel = _os.environ.get("K_TAPS")
        _tapset = set(_tapsel.split(",")) if _tapsel else None

        def dtap(name, ap, reads):
            if _tapset is not None and name not in _tapset:
                return
            if DEBUG_TILE0 and name in dbg:
                p.op("gq", lambda e, d=dbg[name], s=ap: e.dma_start(out=d, in_=s),
                     reads=list(reads), writes=[], dma=True)

        def emit_uload(tile):
            s, b = divmod(tile, 4)
            for k in range(KD):
                st = k % 2
                src = u_d[s][k * 128:(k + 1) * 128, b * T:(b + 1) * T]
                p.op("gq", lambda e, st=st, src=src: e.dma_start(out=ustgs[st][:], in_=src),
                     reads=[], writes=[("ustg", 0)], dma=True)
                p.op("dve", lambda e, st=st, k=k: e.tensor_copy(
                        out=u_bf[:, k * T:(k + 1) * T], in_=ustgs[st][:]),
                     reads=[("ustg", 0)], writes=[("ubf", k)])

        for tile in range(NTILE):
            s, b = divmod(tile, 4)
            if tile == 0:
                emit_uload(0)
            if tile % 4 == 0:
                p.op("gq", lambda e: e.dma_start(out=S_sb[:], in_=zeros_d),
                     reads=[], writes=[("S", g) for g in range(16)], dma=True)
                p.op("gq", lambda e: e.dma_start(out=tails[:], in_=zeros_d[:, 0:102]),
                     reads=[], writes=["tails"], dma=True)

            for pos, j in enumerate(E_ORDER):
                bank = pos % 2
                si0 = 2 * pos
                ensure_slices(tile, si0, si0 + 1, cap=198)
                for k in range(KD):
                    si = si0 + (k // 8)
                    p.op("pe",
                         lambda e, bank=bank, tile=tile, si=si, k=k: e.matmul(
                             pb(bank), wap(tile, si, k % 8),
                             u_bf[:, k * T:(k + 1) * T],
                             start=(k == 0), stop=(k == KD - 1)),
                         waits=[wres(tile, si), ("ubf", k)],
                         reads=([wres(tile, si)] if k % 8 == 7 else []) +
                               ([("ubf", k)] if pos == len(E_ORDER) - 1 else []),
                         writes=[("ps", bank)] if (k == KD - 1 or k == 0) else [])
                if j == 66:
                    p.op("dve", lambda e, bank=bank: e.tensor_scalar(
                            out=dtraw_t[:], in0=pb(bank)[0:64, :],
                            scalar1=c_dtb[:], scalar2=None, op0=ALU.add),
                         reads=[("ps", bank), "dtb"], writes=["dtraw"], tiny=True)
                    dtraw = dtraw_t[:]; tmpA = tmpA_t[:]
                    tmpB = tmpB_t[:]; dtsoft = dtsoft_t[:]
                    lndt = lndt_t[:]; cum = cum_t[:]
                    totB = totB_t[:]; outsc = outsc_t[:]
                    p.op("act", lambda e: e.activation(out=tmpA, in_=dtraw, func=AF.Relu),
                         reads=["dtraw"], writes=["tmpA"], tiny=True)
                    p.op("act", lambda e: e.activation(out=tmpB, in_=dtraw, func=AF.Abs),
                         reads=["dtraw"], writes=["tmpB"], tiny=True)
                    p.op("act", lambda e: e.activation(out=tmpB, in_=tmpB, func=AF.Exp, scale=-1.0),
                         reads=["tmpB"], writes=["tmpB"], tiny=True)
                    p.op("act", lambda e: e.activation(out=tmpB, in_=tmpB, func=AF.Ln, bias=1.0),
                         reads=["tmpB"], writes=["tmpB"], tiny=True)
                    p.op("dve", lambda e: e.tensor_add(dtsoft, tmpA, tmpB),
                         reads=["tmpA", "tmpB"], writes=["dtsoft"], tiny=True)
                    p.op("act", lambda e: e.activation(out=lndt, in_=dtsoft, func=AF.Ln),
                         reads=["dtsoft"], writes=["lndt"], tiny=True)
                    p.op("dve", lambda e: e.tensor_scalar(out=tmpA, in0=dtsoft, scalar1=c_negea[:, 1:2],
                                                          scalar2=None, op0=ALU.mult),
                         reads=["dtsoft", "negea"], writes=["tmpA"], tiny=True)
                    for c in range(NCH):
                        p.op("dve", lambda e, c=c: e.tensor_tensor_scan(
                                out=cum[:, c * 128:(c + 1) * 128],
                                data0=tmpA[:, c * 128:(c + 1) * 128],
                                data1=zt64[:], initial=0.0, op0=ALU.add, op1=ALU.add),
                             reads=["tmpA"], writes=["cum"], tiny=True)
                    p.op("dve", lambda e: e.tensor_sub(dtraw, lndt, cum),
                         reads=["lndt", "cum"], writes=["dtraw"], tiny=True)
                    _C5 = float(np.log(np.float64(np.float32(ml_dtypes.bfloat16(np.exp(5.0))))))
                    p.op("dve", lambda e: e.tensor_scalar(out=dtraw, in0=dtraw, scalar1=-_C5,
                                                          scalar2=None, op0=ALU.add),
                         reads=["dtraw"], writes=["dtraw"], tiny=True)
                    p.op("dve", lambda e: e.tensor_copy(out=biasH[:], in_=dtraw),
                         reads=["dtraw"], writes=["biasHL"], tiny=True)
                    p.op("dve", lambda e: e.tensor_sub(tmpB, dtraw, biasH[:]),
                         reads=["dtraw", "biasHL"], writes=["tmpB"], tiny=True)
                    p.op("dve", lambda e: e.tensor_copy(out=biasL[:], in_=tmpB),
                         reads=["tmpB"], writes=["biasHL"], tiny=True)
                    for c in range(NCH):
                        p.op("dve", lambda e, c=c: e.tensor_copy(
                                out=totB[:, c * 128:(c + 1) * 128],
                                in_=cum[:, c * 128 + 127: c * 128 + 128].broadcast_to([64, 128])),
                             reads=["cum"], writes=["totB"], tiny=True)
                    p.op("dve", lambda e: e.tensor_sub(outsc, totB, cum),
                         reads=["totB", "cum"], writes=["outsc"], tiny=True)
                    p.op("dve", lambda e: e.tensor_add(outsc, outsc, lndt),
                         reads=["outsc", "lndt"], writes=["outsc"], tiny=True)
                    p.op("act", lambda e: e.activation(out=outsc, in_=outsc, func=AF.Exp),
                         reads=["outsc"], writes=["outsc"], tiny=True)
                    for c in range(NCH):
                        p.op("pe", lambda e, c=c: e.matmul(
                                pb(3, 0, 64), cum[:, c * 128:(c + 1) * 128], c_idf[0:64, 0:64],
                                is_transpose=True, start=True, stop=True),
                             reads=["cum"], writes=[("ps", 3)])
                        p.op("dve", lambda e, c=c: e.tensor_copy(
                                out=cum_th[:, c * 64:(c + 1) * 64], in_=pb(3, 0, 64)),
                             reads=[("ps", 3)], writes=["cum_th"], tiny=True)
                        p.op("pe", lambda e, c=c: e.matmul(
                                pb(3, 0, 64), outsc[:, c * 128:(c + 1) * 128], c_idf[0:64, 0:64],
                                is_transpose=True, start=True, stop=True),
                             reads=["outsc"], writes=[("ps", 3)])
                        p.op("dve", lambda e, c=c: e.tensor_copy(
                                out=outscT[:, c * 64:(c + 1) * 64], in_=pb(3, 0, 64)),
                             reads=[("ps", 3)], writes=["outscT"], tiny=True)
                        p.op("pe", lambda e, c=c: e.matmul(
                                pb(3, 0, 64), totB[:, c * 128:(c + 1) * 128], c_idf[0:64, 0:64],
                                is_transpose=True, start=True, stop=True),
                             reads=["totB"], writes=[("ps", 3)])
                        p.op("act", lambda e, c=c: e.activation(
                                out=expTotB[:, c * 64:(c + 1) * 64], in_=pb(3, 0, 64), func=AF.Exp),
                             reads=[("ps", 3)], writes=["expTotB"], tiny=True)
                        p.op("act", lambda e, c=c: e.activation(
                                out=expcum[:, c * 64:(c + 1) * 64],
                                in_=cum_th[:, c * 64:(c + 1) * 64], func=AF.Exp),
                             reads=["cum_th"], writes=["expcum"], tiny=True)
                elif 32 <= j < 66:
                    fc = j - 32
                    ci = fc % 2
                    p.op("dve", lambda e, fc=fc, ci=ci: e.tensor_copy(
                            out=cins[ci][:, 0:3], in_=tails[:, fc * 3:(fc + 1) * 3]),
                         reads=["tails"], writes=[("cin", ci)], tiny=True)
                    p.op("act", lambda e, bank=bank, ci=ci: e.activation(
                            out=cins[ci][:, 3:515], in_=pb(bank), func=AF.Copy),
                         reads=[("ps", bank)], writes=[("cin", ci)])
                    p.op("dve", lambda e, fc=fc, ci=ci: e.tensor_copy(
                            out=tails[:, fc * 3:(fc + 1) * 3], in_=cins[ci][:, 512:515]),
                         reads=[("cin", ci)], writes=["tails"], tiny=True)
                    p.op("dve", lambda e, ci=ci, fc=fc: e.tensor_scalar(
                            out=couts[ci][:], in0=cins[ci][:, 0:512],
                            scalar1=c_cw[:, fc * 4: fc * 4 + 1], scalar2=None, op0=ALU.mult),
                         reads=[("cin", ci)], writes=[("cout", ci)])
                    for tap in range(1, 4):
                        p.op("dve", lambda e, ci=ci, fc=fc, tap=tap: e.scalar_tensor_tensor(
                                out=couts[ci][:], in0=cins[ci][:, tap:512 + tap],
                                scalar=c_cw[:, fc * 4 + tap: fc * 4 + tap + 1],
                                in1=couts[ci][:], op0=ALU.mult, op1=ALU.add),
                             reads=[("cin", ci), ("cout", ci)], writes=[("cout", ci)])
                    if fc < 32:
                        p.op("act", lambda e, ci=ci, fc=fc: e.activation(
                                out=sxs[ci][:], in_=couts[ci][:], func=AF.Silu,
                                bias=c_cb[:, fc: fc + 1]),
                             reads=[("cout", ci)], writes=[("sx", ci)])
                        tb = 2 + fc % 2
                        extra = [("stg5", 0)] if fc == 0 else ([("stg5", 1)] if fc == 4 else [])
                        for c in range(NCH):
                            p.op("pe", lambda e, ci=ci, c=c, tb=tb: e.matmul(
                                    pb_bf(tb, c * 128, (c + 1) * 128),
                                    sxs[ci][:, c * 128:(c + 1) * 128],
                                    c_idb[:], is_transpose=True, start=(c == 0), stop=(c == 3)),
                                 reads=[("sx", ci)], writes=[("ps", tb)])
                        for c in range(NCH):
                            p.op("dve", lambda e, c=c, fc=fc, tb=tb: e.tensor_copy(
                                    out=xtm[:, fc * 512 + c * 128: fc * 512 + (c + 1) * 128],
                                    in_=pb_bf(tb, c * 128, (c + 1) * 128)),
                                 reads=[("ps", tb)],
                                 writes=[("xtm", c, fc), ("ynfm", c, fc)] + extra, tiny=True)
                    else:
                        bc = fc - 32
                        p.op("act", lambda e, ci=ci, fc=fc, bc=bc: e.activation(
                                out=bcbf[:, bc * 512:(bc + 1) * 512], in_=couts[ci][:],
                                func=AF.Silu, bias=c_cb[:, fc: fc + 1]),
                             reads=[("cout", ci)], writes=[("bcbf", bc)])
                else:
                    zc = j
                    p.op("act", lambda e, bank=bank, zc=zc: e.activation(
                            out=zbf[:, zc * 512:(zc + 1) * 512], in_=pb(bank), func=AF.Copy),
                         reads=[("ps", bank)],
                         writes=[("zbf", zc)] + ([("x2tm", zc // 8, jj) for jj in range(16)]
                                                 if zc % 8 == 0 else []))

            if tile == 0:
                dtap("d_ubf", u_bf[:], [("ubf", k) for k in range(KD)])
                dtap("d_zbf", zbf[:], [("zbf", zc) for zc in range(32)])
                dtap("d_xtm", xtm[:], [("xtm", c, fc) for c in range(NCH) for fc in range(32)])
                dtap("d_bcbf", bcbf[:], [("bcbf", 0), ("bcbf", 1)])
                dtap("d_dtsoft", dtsoft_t[:], ["dtsoft"])
                dtap("d_cum", cum_t[:], ["cum"])
                dtap("d_biasH", biasH[:], ["biasHL"])
                dtap("d_biasL", biasL[:], ["biasHL"])
                dtap("d_outscT", outscT[:], ["outscT"])
                dtap("d_expcum", expcum[:], ["expcum"])
                dtap("d_expTotB", expTotB[:], ["expTotB"])
                dtap("d_cumth", cum_th[:], ["cum_th"])

            for c in range(NCH):
                ci = c % 2
                B_c = bcbf[:, c * 128:(c + 1) * 128]
                C_c = bcbf[:, 512 + c * 128: 512 + (c + 1) * 128]
                p.op("pe", lambda e, B_c=B_c, C_c=C_c: e.matmul(
                        pb(6, 0, 128), B_c, C_c, start=True, stop=True),
                     reads=[("bcbf", 0), ("bcbf", 1)], writes=[("ps", 6)])
                p.op("dve", lambda e, ci=ci: e.tensor_mul(Gtms[ci][:], pb(6, 0, 128), c_mask[:]),
                     reads=[("ps", 6)], writes=[("Gtm", ci)], tiny=True)
                p.op("pe", lambda e, B_c=B_c: e.matmul(
                        pb_bf(7, 0, 128), B_c, c_idb[:], is_transpose=True, start=True, stop=True),
                     reads=[("bcbf", 0)], writes=[("ps", 7)])
                p.op("dve", lambda e, ci=ci: e.tensor_copy(out=BTs[ci][:], in_=pb_bf(7, 0, 128)),
                     reads=[("ps", 7)], writes=[("BT", ci)])

                for g in range(16):
                    gi = g % 2
                    q = g // 4
                    if g % 4 == 0:
                        p.op("act", lambda e, q=q: e.activation(
                                out=S_bfs[q % 2][:], in_=S_sb[:, q * 1024:(q + 1) * 1024],
                                func=AF.Copy),
                             reads=[("S", 4 * q), ("S", 4 * q + 1), ("S", 4 * q + 2),
                                    ("S", 4 * q + 3)],
                             writes=[("Sbf", q % 2)])
                    bR = 2 + gi
                    for i in range(4):
                        p.op("pe", lambda e, i=i, g=g, c=c, bR=bR: e.matmul(
                                pb(bR, i * 128, (i + 1) * 128),
                                cum_th[:, c * 64 + 4 * g + i: c * 64 + 4 * g + i + 1]
                                    .broadcast_to([128, 128]),
                                c_idf[:], is_transpose=True, start=(i == 0), stop=False),
                             reads=["cum_th"], writes=[("ps", bR)])
                    selb = c_id64[:, 4 * g: 4 * g + 4].unsqueeze(2).broadcast_to([64, 4, 128])
                    p.op("pe", lambda e, bR=bR, sb=selb, c=c: e.matmul(
                            pb(bR).rearrange("p (h t) -> p h t", h=4),
                            biasH[:, c * 128:(c + 1) * 128], sb, start=False, stop=False),
                         reads=["biasHL"], writes=[("ps", bR)])
                    p.op("pe", lambda e, bR=bR, sb=selb, c=c: e.matmul(
                            pb(bR).rearrange("p (h t) -> p h t", h=4),
                            biasL[:, c * 128:(c + 1) * 128], sb, start=False, stop=True),
                         reads=["biasHL"], writes=[("ps", bR)])
                    bAB = 4 + gi
                    p.op("act", lambda e, gi=gi, bR=bR, bAB=bAB: e.activation(
                            out=pb(bAB), in_=pb(bR), func=AF.Relu, scale=-1.0),
                         reads=[("ps", bR)], writes=[("ps", bAB)])
                    p.op("act", lambda e, gi=gi, bAB=bAB: e.activation(
                            out=D4s[gi][:], in_=pb(bAB), func=AF.Exp, scale=-1.0),
                         reads=[("ps", bAB)], writes=[("D4", gi)])
                    p.op("dve", lambda e, gi=gi, ci=ci: e.tensor_mul(
                            M4s[gi][:].rearrange("p (h t) -> p h t", h=4),
                            D4s[gi][:].rearrange("p (h t) -> p h t", h=4),
                            Gtms[ci][:].unsqueeze(1).broadcast_to([128, 4, 128])),
                         reads=[("D4", gi), ("Gtm", ci)], writes=[("M4", gi)])
                    xpair = xtm[:].rearrange("p (f r) -> p f r", r=512)[
                        :, 2 * g:2 * g + 2, c * 128:(c + 1) * 128].rearrange(
                        "p h (q w) -> p h q w", q=2)
                    oscb = outscT[:, c * 64 + 4 * g: c * 64 + 4 * g + 4].rearrange(
                        "p (h q) -> p h q", h=2).unsqueeze(3).broadcast_to([128, 2, 2, 64])
                    p.op("dve", lambda e, gi=gi, xp=xpair, ob=oscb: e.tensor_mul(
                            xos[gi][:].rearrange("p (h q w) -> p h q w", h=2, q=2),
                            xp, ob),
                         reads=[("xtm", c, 2 * g), ("xtm", c, 2 * g + 1), "outscT"],
                         writes=[("xo", gi)])
                    for i in range(4):
                        h = 4 * g + i
                        p.op("pe", lambda e, i=i, h=h, gi=gi, bAB=bAB, c=c: e.matmul(
                                pb(bAB, i * 64, (i + 1) * 64),
                                M4s[gi][:, i * 128:(i + 1) * 128],
                                xtm_head(c, h), start=True, stop=True),
                             reads=[("M4", gi), ("xtm", c, h // 2)], writes=[("ps", bAB)])
                    p.op("pe", lambda e, gi=gi, g=g, q=q, bAB=bAB, C_c=C_c: e.matmul(
                            pb(bAB, 256, 512), C_c,
                            S_bfs[q % 2][:, (g % 4) * 256:(g % 4 + 1) * 256],
                            start=True, stop=True),
                         reads=[("bcbf", 1), ("Sbf", q % 2)], writes=[("ps", bAB)])
                    bC = 6 + gi
                    p.op("pe", lambda e, gi=gi, bC=bC, ci=ci: e.matmul(
                            pb(bC, 0, 256), BTs[ci][:], xos[gi][:], start=True, stop=True),
                         reads=[("BT", ci), ("xo", gi)], writes=[("ps", bC)])
                    extra_y = []
                    if c == 0 and g == 0 and ci == 0:
                        extra_y = [("h", jj) for jj in range(44)] + \
                                  [("x1", j2) for j2 in range(KD)] + \
                                  [("stg", 0), ("stg", 1)]
                    p.op("dve", lambda e, gi=gi, g=g, c=c, bAB=bAB: e.tensor_mul(
                            t4s[gi][:].rearrange("p (h w) -> p h w", h=4),
                            pb(bAB, 256, 512).rearrange("p (h w) -> p h w", h=4),
                            expcum[:, c * 64 + 4 * g: c * 64 + 4 * g + 4].unsqueeze(2)
                                .broadcast_to([128, 4, 64])),
                         reads=[("ps", bAB), "expcum"], writes=[("t4", gi)], tiny=True)
                    p.op("dve", lambda e, gi=gi, g=g, ci=ci, bAB=bAB: e.tensor_add(
                            Ys[ci][:, g * 256:(g + 1) * 256], t4s[gi][:], pb(bAB, 0, 256)),
                         reads=[("t4", gi), ("ps", bAB)],
                         writes=[("Y", ci, g)] + extra_y, tiny=True)
                    p.op("dve", lambda e, g=g, c=c: e.tensor_mul(
                            S_sb[:, g * 256:(g + 1) * 256].rearrange("p (h w) -> p h w", h=4),
                            S_sb[:, g * 256:(g + 1) * 256].rearrange("p (h w) -> p h w", h=4),
                            expTotB[:, c * 64 + 4 * g: c * 64 + 4 * g + 4].unsqueeze(2)
                                .broadcast_to([128, 4, 64])),
                         reads=[("S", g), "expTotB"], writes=[("S", g)], tiny=True)
                    p.op("dve", lambda e, g=g, gi=gi, bC=bC: e.tensor_add(
                            S_sb[:, g * 256:(g + 1) * 256], S_sb[:, g * 256:(g + 1) * 256],
                            pb(bC, 0, 256)),
                         reads=[("S", g), ("ps", bC)], writes=[("S", g)], tiny=True)

                if tile == 0 and c == 0:
                    dtap("d_Y0", Ys[0][:], [("Y", 0, g) for g in range(16)])
                    dtap("d_S1", S_sb[:], [("S", g) for g in range(16)])
                if tile == 0 and c == 1:
                    dtap("d_Y1", Ys[1][:], [("Y", 1, g) for g in range(16)])
                    dtap("d_M41", M4s[1][:], [("M4", 1)])
                    dtap("d_D41", D4s[1][:], [("D4", 1)])

                dskb = c_db[:].rearrange("p (f q) -> p f q", q=2).unsqueeze(3) \
                    .broadcast_to([128, 32, 2, 64])
                xall4 = xtm[:].rearrange("p (f r) -> p f r", r=512)[
                    :, :, c * 128:(c + 1) * 128].rearrange("p f (q w) -> p f q w", q=2)
                p.op("dve", lambda e, xa4=xall4, db4=dskb: e.tensor_mul(xa4, xa4, db4),
                     reads=[("xtm", c, fc) for fc in range(32)],
                     writes=[("xtm", c, fc) for fc in range(32)], tiny=True)
                xall3 = xtm[:].rearrange("p (f r) -> p f r", r=512)[:, :, c * 128:(c + 1) * 128]
                p.op("dve", lambda e, ci=ci, xa3=xall3: e.tensor_add(
                        Ys[ci][:].rearrange("p (f r) -> p f r", r=128),
                        Ys[ci][:].rearrange("p (f r) -> p f r", r=128), xa3),
                     reads=[("Y", ci, g) for g in range(16)] +
                           [("xtm", c, fc) for fc in range(32)],
                     writes=[("Y", ci, g) for g in range(16)], tiny=True)
                if tile == 0 and c == 0:
                    dtap("d_Ydsk", Ys[0][:], [("Y", 0, g) for g in range(16)])
                for ss in range(8):
                    tb = 2 + ss % 2
                    for i in range(4):
                        fc = ss * 4 + i
                        p.op("pe", lambda e, fc=fc, i=i, tb=tb, c=c: e.matmul(
                                pb_bf(tb, i * 128, (i + 1) * 128),
                                zbf[:, fc * 512 + c * 128: fc * 512 + (c + 1) * 128],
                                c_idb[:], is_transpose=True, start=(i == 0), stop=(i == 3)),
                             reads=[("zbf", fc)], writes=[("ps", tb)])
                    extra_s = [("h", 32), ("h", 33), ("h", 34), ("h", 35)] \
                        if (ss < 2 and c == 0) else []
                    p.op("act", lambda e, ss=ss, tb=tb: e.activation(
                            out=silu_sl[ss % 2][:], in_=pb_bf(tb, 0, 512), func=AF.Silu),
                         reads=[("ps", tb)], writes=[("silu", ss % 2)] + extra_s)
                    p.op("dve", lambda e, ss=ss, ci=ci: e.tensor_mul(
                            Ys[ci][:, ss * 512:(ss + 1) * 512],
                            Ys[ci][:, ss * 512:(ss + 1) * 512], silu_sl[ss % 2][:]),
                         reads=[("silu", ss % 2), ("Y", ci, 2 * ss), ("Y", ci, 2 * ss + 1)],
                         writes=[("Y", ci, 2 * ss), ("Y", ci, 2 * ss + 1)], tiny=True)
                for ss in range(8):
                    p.op("act", lambda e, ss=ss, ci=ci, c=c: e.activation(
                            out=pb(4), in_=Ys[ci][:, ss * 512:(ss + 1) * 512], func=AF.Square,
                            accum_out=stats[:, c * 54 + ss: c * 54 + ss + 1]),
                         reads=[("Y", ci, 2 * ss), ("Y", ci, 2 * ss + 1)],
                         writes=[("bn", c, ss), ("ps", 4)], tiny=True)
                p.op("dve", lambda e, c=c: e.tensor_reduce(
                        out=stats[:, c * 54 + 50: c * 54 + 51], in_=stats[:, c * 54: c * 54 + 8],
                        axis=mybir.AxisListType.X, op=ALU.add),
                     reads=[("bn", c, ss) for ss in range(8)], writes=[("msq", c)], tiny=True)
                p.op("dve", lambda e, c=c: e.tensor_scalar(out=stats[:, c * 54 + 51: c * 54 + 52], in0=stats[:, c * 54 + 50: c * 54 + 51],
                                                      scalar1=1.0 / DSSM, scalar2=EPS, op0=ALU.mult, op1=ALU.add),
                     reads=[("msq", c)], writes=[("lnm", c)], tiny=True)
                p.op("dve", lambda e, c=c: e.tensor_scalar(
                        out=stats[:, c * 54 + 52: c * 54 + 53].bitcast(I32),
                        in0=stats[:, c * 54 + 51: c * 54 + 52].bitcast(I32),
                        scalar1=1, scalar2=None, op0=ALU.arith_shift_right),
                     reads=[("lnm", c)], writes=[("rstd_rms", c)], tiny=True)
                p.op("dve", lambda e, c=c: e.tensor_scalar(
                        out=stats[:, c * 54 + 52: c * 54 + 53].bitcast(I32),
                        in0=stats[:, c * 54 + 52: c * 54 + 53].bitcast(I32),
                        scalar1=-1, scalar2=0x5F3759DF, op0=ALU.mult, op1=ALU.add),
                     reads=[("rstd_rms", c)], writes=[("rstd_rms", c)], tiny=True)
                for _nit in range(2):
                    p.op("dve", lambda e, c=c: e.tensor_mul(
                            stats[:, c * 54 + 53: c * 54 + 54],
                            stats[:, c * 54 + 52: c * 54 + 53], stats[:, c * 54 + 52: c * 54 + 53]),
                         reads=[("rstd_rms", c)], writes=[("rsT", c)], tiny=True)
                    p.op("dve", lambda e, c=c: e.tensor_mul(
                            stats[:, c * 54 + 53: c * 54 + 54],
                            stats[:, c * 54 + 53: c * 54 + 54], stats[:, c * 54 + 51: c * 54 + 52]),
                         reads=[("rsT", c), ("lnm", c)], writes=[("rsT", c)], tiny=True)
                    p.op("dve", lambda e, c=c: e.tensor_scalar(
                            out=stats[:, c * 54 + 53: c * 54 + 54],
                            in0=stats[:, c * 54 + 53: c * 54 + 54], scalar1=-0.5, scalar2=1.5,
                            op0=ALU.mult, op1=ALU.add),
                         reads=[("rsT", c)], writes=[("rsT", c)], tiny=True)
                    p.op("dve", lambda e, c=c: e.tensor_mul(
                            stats[:, c * 54 + 52: c * 54 + 53],
                            stats[:, c * 54 + 52: c * 54 + 53], stats[:, c * 54 + 53: c * 54 + 54]),
                         reads=[("rsT", c)], writes=[("rstd_rms", c)], tiny=True)
                import os as _os2
                if _os2.environ.get("K_SHIELD"):
                    p.op("gq", lambda e, tile=tile, c=c: e.dma_start(
                            out=shield[tile][c][:, 0:53], in_=stats[:, c * 54: c * 54 + 53]),
                         reads=[("bn", c, ss) for ss in range(8)] +
                               [("bnagg", c), ("msq", c), ("lnm", c), ("rstd_rms", c)],
                         writes=[], dma=True)
                if tile == 0 and c == 0:
                    dtap("d_yg", Ys[0][:], [("Y", 0, g) for g in range(16)])
                    dtap("d_rstd", stats[:, 0:53],
                         [("bn", c, ss) for ss in range(8)] + [("bnagg", c), ("msq", c), ("lnm", c), ("rstd_rms", c)])
                if tile == 0 and c == 1:
                    dtap("d_ygc1", Ys[1][:], [("Y", 1, g) for g in range(16)])
                    dtap("d_rstd1", stats[:, 54:107],
                         [("bn", c, ss) for ss in range(8)] + [("bnagg", c), ("msq", c), ("lnm", c), ("rstd_rms", c)])
                p.op("dve", lambda e, ci=ci, xa3=xall3, c=c: e.tensor_scalar(
                        out=xa3, in0=Ys[ci][:].rearrange("p (f r) -> p f r", r=128),
                        scalar1=stats[:, c * 54 + 52: c * 54 + 53], scalar2=None, op0=ALU.mult),
                     reads=[("Y", ci, g) for g in range(16)] + [("rstd_rms", c)],
                     writes=[("xtm", c, fc) for fc in range(32)], tiny=True)
                if tile == 0 and c == 0:
                    dtap("d_yn", xtm[:], [("xtm", 0, fc) for fc in range(32)])
                if tile == 0 and c == 1:
                    dtap("d_yn1", xtm[:], [("xtm", 1, fc) for fc in range(32)])
                for ss in range(8):
                    tb = 2 + ss % 2
                    for i in range(4):
                        fc = ss * 4 + i
                        p.op("pe", lambda e, fc=fc, i=i, tb=tb, c=c: e.matmul(
                                pb_bf(tb, i * 128, (i + 1) * 128),
                                xtm[:, fc * 512 + c * 128: fc * 512 + (c + 1) * 128],
                                c_idb[:], is_transpose=True, start=(i == 0), stop=(i == 3)),
                             reads=[("xtm", c, fc)], writes=[("ps", tb)])
                    for i in range(4):
                        fc = ss * 4 + i
                        p.op("dve", lambda e, fc=fc, i=i, tb=tb, c=c: e.tensor_scalar(
                                out=xtm[:, fc * 512 + c * 128: fc * 512 + (c + 1) * 128],
                                in0=pb_bf(tb, i * 128, (i + 1) * 128),
                                scalar1=c_nw[:, fc: fc + 1], scalar2=None, op0=ALU.mult),
                             reads=[("ps", tb)], writes=[("ynfm", c, fc), ("xtm", c, fc)], tiny=True)

            if tile == 0:
                dtap("d_ynfm", xtm[:], [("ynfm", c, fc) for c in range(NCH) for fc in range(32)])

            si_op = 2 * len(E_ORDER)
            for j in range(16):
                bank = j % 2
                ensure_slices(tile, si_op + j * 4, si_op + j * 4 + 3, cap=198)
                for k in range(KF):
                    si = si_op + j * 4 + k // 8
                    p.op("pe", lambda e, bank=bank, tile=tile, si=si, k=k: e.matmul(
                            pb(bank), wap(tile, si, k % 8),
                            xtm[:, k * 512:(k + 1) * 512], start=(k == 0), stop=(k == KF - 1)),
                         waits=[wres(tile, si)] + [("ynfm", cc, k) for cc in range(NCH)],
                         reads=([wres(tile, si)] if k % 8 == 7 else []) +
                               ([("ynfm", cc, kk) for cc in range(NCH) for kk in range(KF)]
                                if (j == 15 and k == KF - 1) else []),
                         writes=([("ps", bank), ("p4done",)] if k == KF - 1 else
                                 ([("ps", bank)] if k == 0 else [])))
                st = j % 2
                src = u_d[s][j * 128:(j + 1) * 128, b * T:(b + 1) * T]
                p.op("gq", lambda e, st=st, src=src: e.dma_start(out=ustgs[st][:], in_=src),
                     reads=[], writes=[("ustg", 0)], dma=True)
                extra_x1 = [("Y", 0, 2 * j), ("Y", 0, 2 * j + 1)] if j < 8 else \
                           [("Y", 1, 2 * (j - 8)), ("Y", 1, 2 * (j - 8) + 1)]
                if j < 2:
                    extra_x1 += [("silu", j)]
                p.op("dve", lambda e, j=j, bank=bank, st=st: e.tensor_add(
                        x1_fm[:, j * T:(j + 1) * T], pb(bank), ustgs[st][:]),
                     reads=[("ps", bank), ("ustg", 0)], writes=[("x1", j)] + extra_x1)
                p.op("dve", lambda e, j=j, st=st: e.tensor_copy(
                        out=xbs[st][:], in_=x1_fm[:, j * T:(j + 1) * T]),
                     reads=[("x1", j)], writes=[("sx", st)])
                p.op("pe", lambda e, j=j, st=st: e.matmul(
                        pb(6, 0, 512)[0:1, :], c_ones128[:], xbs[st][:],
                        start=(j == 0), stop=(j == 15)),
                     reads=[("sx", st)], writes=[("ps", 6)])
                p.op("dve", lambda e, st=st: e.tensor_mul(xbs[st][:], xbs[st][:], xbs[st][:]),
                     reads=[("sx", st)], writes=[("sx", st)])
                p.op("pe", lambda e, j=j, st=st: e.matmul(
                        pb(7, 0, 512)[0:1, :], c_ones128[:], xbs[st][:],
                        start=(j == 0), stop=(j == 15)),
                     reads=[("sx", st)], writes=[("ps", 7)])
            if tile == 0:
                dtap("d_ynfm2", xtm[:], [("p4done",)])
                dtap("d_x1", x1_fm[:], [("x1", j) for j in range(KD)])
            vmean = dtraw_t[0:1, :]; vmsq = tmpB_t[0:1, :]; vtmp = lndt_t[0:1, :]
            vrstd = totB_t[0:1, :]; vnmr = tmpA_t[0:1, :]
            lotmp = dtsoft_t[0:1, :]
            bf_hr = cum_t[0:1, :].bitcast(BF16)[:, 0:512]
            bf_lr = cum_t[0:1, :].bitcast(BF16)[:, 512:1024]
            bf_hn = outsc_t[0:1, :].bitcast(BF16)[:, 0:512]
            bf_ln = outsc_t[0:1, :].bitcast(BF16)[:, 512:1024]
            p.op("dve", lambda e: e.tensor_scalar(out=vmean, in0=pb(6, 0, 512)[0:1, :],
                                                  scalar1=1.0 / D, scalar2=None, op0=ALU.mult),
                 reads=[("ps", 6)], writes=["vmean", "dtraw"], tiny=True)
            p.op("dve", lambda e: e.tensor_scalar(out=vmsq, in0=pb(7, 0, 512)[0:1, :],
                                                  scalar1=1.0 / D, scalar2=None, op0=ALU.mult),
                 reads=[("ps", 7)], writes=["vmsq", "tmpB"], tiny=True)
            p.op("dve", lambda e: e.tensor_mul(vtmp, vmean, vmean),
                 reads=["vmean"], writes=["vtmp", "lndt"], tiny=True)
            p.op("dve", lambda e: e.tensor_sub(vmsq, vmsq, vtmp),
                 reads=["vmsq", "vtmp"], writes=["vmsq"], tiny=True)
            p.op("dve", lambda e: e.tensor_scalar(out=vtmp, in0=vmsq, scalar1=EPS,
                                                  scalar2=None, op0=ALU.add),
                 reads=["vmsq"], writes=["vtmp", "lndt"], tiny=True)
            p.op("dve", lambda e: e.tensor_scalar(
                    out=vrstd.bitcast(I32), in0=vtmp.bitcast(I32),
                    scalar1=1, scalar2=None, op0=ALU.arith_shift_right),
                 reads=["vtmp"], writes=["vrstd", "totB"], tiny=True)
            p.op("dve", lambda e: e.tensor_scalar(
                    out=vrstd.bitcast(I32), in0=vrstd.bitcast(I32),
                    scalar1=-1, scalar2=0x5F3759DF, op0=ALU.mult, op1=ALU.add),
                 reads=["vrstd"], writes=["vrstd"], tiny=True)
            for _nit in range(2):
                p.op("dve", lambda e: e.tensor_mul(vmsq, vrstd, vrstd),
                     reads=["vrstd"], writes=["vmsq"], tiny=True)
                p.op("dve", lambda e: e.tensor_mul(vmsq, vmsq, vtmp),
                     reads=["vmsq", "vtmp"], writes=["vmsq"], tiny=True)
                p.op("dve", lambda e: e.tensor_scalar(out=vmsq, in0=vmsq, scalar1=-0.5,
                                                      scalar2=1.5, op0=ALU.mult, op1=ALU.add),
                     reads=["vmsq"], writes=["vmsq"], tiny=True)
                p.op("dve", lambda e: e.tensor_mul(vrstd, vrstd, vmsq),
                     reads=["vmsq", "vrstd"], writes=["vrstd"], tiny=True)
            p.op("dve", lambda e: e.tensor_mul(vnmr, vmean, vrstd),
                 reads=["vmean", "vrstd"], writes=["vnmr", "tmpA"], tiny=True)
            p.op("dve", lambda e: e.tensor_scalar_mul(vnmr, vnmr, -1.0),
                 reads=["vnmr"], writes=["vnmr"], tiny=True)
            p.op("dve", lambda e: e.tensor_copy(out=bf_hr, in_=vrstd),
                 reads=["vrstd"], writes=["bfhr", "cum"], tiny=True)
            p.op("dve", lambda e: e.tensor_copy(out=bf_hr, in_=vrstd),
                 reads=[], writes=["bfhr"], tiny=True)
            p.op("dve", lambda e: e.tensor_sub(lotmp, vrstd, bf_hr),
                 reads=["vrstd", "bfhr"], writes=["lotmp", "dtsoft"], tiny=True)
            p.op("dve", lambda e: e.tensor_copy(out=bf_lr, in_=lotmp),
                 reads=["lotmp"], writes=["bflr"], tiny=True)
            p.op("dve", lambda e: e.tensor_copy(out=bf_lr, in_=lotmp),
                 reads=[], writes=["bflr"], tiny=True)
            p.op("dve", lambda e: e.tensor_copy(out=bf_hn, in_=vnmr),
                 reads=["vnmr"], writes=["bfhn", "outsc"], tiny=True)
            p.op("dve", lambda e: e.tensor_copy(out=bf_hn, in_=vnmr),
                 reads=[], writes=["bfhn"], tiny=True)
            p.op("dve", lambda e: e.tensor_sub(lotmp, vnmr, bf_hn),
                 reads=["vnmr", "bfhn"], writes=["lotmp"], tiny=True)
            p.op("dve", lambda e: e.tensor_copy(out=bf_ln, in_=lotmp),
                 reads=["lotmp"], writes=["bfln"], tiny=True)
            p.op("dve", lambda e: e.tensor_copy(out=bf_ln, in_=lotmp),
                 reads=[], writes=["bfln"], tiny=True)
            p.op("pe", lambda e: e.matmul(pb(2), c_ones1[:], bf_hr, start=True, stop=False),
                 reads=["bfhr"], writes=[("ps", 2)])
            p.op("pe", lambda e: e.matmul(pb(2), c_ones1[:], bf_lr, start=False, stop=True),
                 reads=["bflr"], writes=[("ps", 2)])
            p.op("pe", lambda e: e.matmul(pb(3), c_ones1[:], bf_hn, start=True, stop=False),
                 reads=["bfhn"], writes=[("ps", 3)])
            p.op("pe", lambda e: e.matmul(pb(3), c_ones1[:], bf_ln, start=False, stop=True),
                 reads=["bfln"], writes=[("ps", 3)])
            for j in range(KD):
                st = j % 2
                p.op("dve", lambda e, j=j, st=st: e.tensor_mul(
                        xas[st][:], x1_fm[:, j * T:(j + 1) * T], pb(2)),
                     reads=[("x1", j), ("ps", 2)], writes=[("cout", st)])
                p.op("dve", lambda e, st=st: e.tensor_add(xas[st][:], xas[st][:], pb(3)),
                     reads=[("cout", st), ("ps", 3)], writes=[("cout", st)])
                p.op("gq", lambda e, j=j, st=st: e.dma_start(out=scr_x1n[j], in_=xas[st][:]),
                     reads=[("cout", st)], writes=[("scrx1n", j)], dma=True)
                p.op("dve", lambda e, j=j, st=st: e.tensor_copy(
                        out=u_bf[:, j * T:(j + 1) * T], in_=xas[st][:]),
                     reads=[("cout", st)], writes=[("ubf", j)])

            if tile == 0:
                dtap("d_x1nbf", u_bf[:], [("ubf", j) for j in range(KD)])

            si_g = si_op + 64
            for jj in range(44):
                bg = 0 if jj % 2 == 0 else 4
                bu = 1 if jj % 2 == 0 else 5
                ensure_slices(tile, si_g + jj * 4, si_g + jj * 4 + 3)
                for k in range(KD):
                    si = si_g + jj * 4 + k // 8
                    p.op("pe", lambda e, bg=bg, tile=tile, si=si, k=k: e.matmul(
                            pb(bg), wap(tile, si, k % 8), u_bf[:, k * T:(k + 1) * T],
                            start=(k == 0), stop=(k == KD - 1)),
                         waits=[wres(tile, si), ("ubf", k)],
                         reads=([wres(tile, si)] if k % 8 == 7 else []),
                         writes=[("ps", bg)] if (k == KD - 1 or k == 0) else [])
                for k in range(KD):
                    si = si_g + jj * 4 + 2 + k // 8
                    p.op("pe", lambda e, bu=bu, tile=tile, si=si, k=k: e.matmul(
                            pb(bu), wap(tile, si, k % 8), u_bf[:, k * T:(k + 1) * T],
                            start=(k == 0), stop=(k == KD - 1)),
                         waits=[wres(tile, si), ("ubf", k)],
                         reads=([wres(tile, si)] if k % 8 == 7 else []) +
                               ([("ubf", k)] if jj == 43 else []),
                         writes=[("ps", bu)] if (k == KD - 1 or k == 0) else [])
                st = jj % 2
                p.op("act", lambda e, jj=jj, bg=bg, st=st: e.activation(
                        out=sxs[st][:], in_=pb(bg), func=AF.Silu, scale=c_sg[:, jj:jj + 1]),
                     reads=[("ps", bg)], writes=[("sx", st)])
                p.op("dve", lambda e, jj=jj, bu=bu, st=st: e.scalar_tensor_tensor(
                        out=h_bf[:, jj * 512:(jj + 1) * 512], in0=pb(bu),
                        scalar=c_su[:, jj:jj + 1], in1=sxs[st][:],
                        op0=ALU.mult, op1=ALU.mult),
                     reads=[("ps", bu), ("sx", st)], writes=[("h", jj)])
            if tile == 0:
                dtap("d_h", h_bf[:], [("h", jj) for jj in range(44)])
            if tile + 1 < NTILE:
                emit_uload(tile + 1)
            si_d = si_g + 176
            for j in range(16):
                bank = j % 2
                ensure_slices(tile, si_d + j * 6, si_d + j * 6 + 5)
                for k in range(KH):
                    si = si_d + j * 6 + k // 8
                    p.op("pe", lambda e, bank=bank, tile=tile, si=si, k=k: e.matmul(
                            pb(bank), wap(tile, si, k % 8),
                            h_bf[:, k * 512:(k + 1) * 512], start=(k == 0), stop=(k == KH - 1)),
                         waits=[wres(tile, si), ("h", k)],
                         reads=([wres(tile, si)] if (k % 8 == 7 or k == KH - 1) else []) +
                               ([("h", kk) for kk in range(KH)] if (j == 15 and k == KH - 1) else []),
                         writes=[("ps", bank)] if (k == KH - 1 or k == 0) else [])
                st = j % 2
                p.op("gq", lambda e, j=j, st=st: e.dma_start(out=x1ss[st], in_=scr_x1n[j]),
                     reads=[("scrx1n", j)], writes=[("cin", st)], dma=True)
                p.op("dve", lambda e, j=j, bank=bank, st=st: e.scalar_tensor_tensor(
                        out=xas[st][:], in0=pb(bank), scalar=c_sd[:, j:j + 1],
                        in1=x1ss[st], op0=ALU.mult, op1=ALU.add),
                     reads=[("ps", bank), ("cin", st)], writes=[("cout", st)])
                tb = 2 + j % 2
                for c in range(NCH):
                    p.op("pe", lambda e, c=c, st=st, tb=tb: e.matmul(
                            pb(tb, c * 128, (c + 1) * 128), xas[st][:, c * 128:(c + 1) * 128],
                            c_idf[:], is_transpose=True, start=(c == 0), stop=(c == 3)),
                         reads=[("cout", st)], writes=[("ps", tb)])
                for c in range(NCH):
                    p.op("dve", lambda e, c=c, j=j, tb=tb: e.tensor_copy(
                            out=x2tm[c][:, j * 128:(j + 1) * 128],
                            in_=pb(tb, c * 128, (c + 1) * 128)),
                         reads=[("ps", tb)], tiny=True,
                         writes=[("x2tm", c, j)] +
                                ([("zbf", zz) for zz in range(c * 8, c * 8 + 8)]
                                 if j == 0 else []))

            if tile == 0:
                dtap("d_x2tm", x2tm[0][:], [("x2tm", 0, jj) for jj in range(16)])
            for c in range(NCH):
                for ss in range(4):
                    p.op("act", lambda e, c=c, ss=ss: e.activation(
                            out=pb(4), in_=x2tm[c][:, ss * 512:(ss + 1) * 512], func=AF.Copy,
                            accum_out=stats[:, ss: ss + 1]),
                         reads=[("x2tm", c, jj) for jj in range(ss * 4, ss * 4 + 4)],
                         writes=[("bn", ss), ("ps", 4)], tiny=True)
                    p.op("act", lambda e, c=c, ss=ss: e.activation(
                            out=pb(5), in_=x2tm[c][:, ss * 512:(ss + 1) * 512], func=AF.Square,
                            accum_out=stats[:, 4 + ss: 5 + ss]),
                         reads=[("x2tm", c, jj) for jj in range(ss * 4, ss * 4 + 4)],
                         writes=[("bn", 4 + ss), ("ps", 5)], tiny=True)
                if TAP7 and tile == 7 and c == 3:
                    p.op("gq", lambda e: e.dma_start(out=d7["d7_stats"][:], in_=stats[:, 0:8]),
                         reads=[("bn", ss) for ss in range(8)], writes=[], dma=True)
                p.op("dve", lambda e: e.tensor_reduce(
                        out=lncol[:, 3:4], in_=stats[:, 0:4], axis=mybir.AxisListType.X, op=ALU.add),
                     reads=[("bn", 0), ("bn", 1), ("bn", 2), ("bn", 3)], writes=["l2sum"], tiny=True)
                p.op("dve", lambda e: e.tensor_reduce(
                        out=lncol[:, 4:5], in_=stats[:, 4:8], axis=mybir.AxisListType.X, op=ALU.add),
                     reads=[("bn", 4), ("bn", 5), ("bn", 6), ("bn", 7)], writes=["l2sq"], tiny=True)
                p.op("dve", lambda e: e.tensor_scalar(out=lncol[:, 5:6], in0=lncol[:, 3:4],
                                                      scalar1=1.0 / D, scalar2=None, op0=ALU.mult),
                     reads=["l2sum"], writes=["l2mean"], tiny=True)
                p.op("dve", lambda e: e.tensor_scalar(out=lncol[:, 6:7], in0=lncol[:, 4:5],
                                                      scalar1=1.0 / D, scalar2=None, op0=ALU.mult),
                     reads=["l2sq"], writes=["l2msq"], tiny=True)
                p.op("dve", lambda e: e.tensor_mul(lncol[:, 7:8], lncol[:, 5:6], lncol[:, 5:6]),
                     reads=["l2mean"], writes=["l2m2"], tiny=True)
                p.op("dve", lambda e: e.tensor_sub(lncol[:, 0:1], lncol[:, 6:7], lncol[:, 7:8]),
                     reads=["l2msq", "l2m2"], writes=["l2a"], tiny=True)
                p.op("dve", lambda e: e.tensor_scalar(out=lncol[:, 0:1], in0=lncol[:, 0:1],
                                                      scalar1=EPS, scalar2=None, op0=ALU.add),
                     reads=["l2a"], writes=["l2a"], tiny=True)
                p.op("dve", lambda e: e.tensor_scalar(
                        out=lncol[:, 1:2].bitcast(I32), in0=lncol[:, 0:1].bitcast(I32),
                        scalar1=1, scalar2=None, op0=ALU.arith_shift_right),
                     reads=["l2a"], writes=["l2r"], tiny=True)
                p.op("dve", lambda e: e.tensor_scalar(
                        out=lncol[:, 1:2].bitcast(I32), in0=lncol[:, 1:2].bitcast(I32),
                        scalar1=-1, scalar2=0x5F3759DF, op0=ALU.mult, op1=ALU.add),
                     reads=["l2r"], writes=["l2r"], tiny=True)
                for _nit in range(2):
                    p.op("dve", lambda e: e.tensor_mul(lncol[:, 7:8], lncol[:, 1:2], lncol[:, 1:2]),
                         reads=["l2r"], writes=["l2m2"], tiny=True)
                    p.op("dve", lambda e: e.tensor_mul(lncol[:, 7:8], lncol[:, 7:8], lncol[:, 0:1]),
                         reads=["l2m2", "l2a"], writes=["l2m2"], tiny=True)
                    p.op("dve", lambda e: e.tensor_scalar(out=lncol[:, 7:8], in0=lncol[:, 7:8],
                                                          scalar1=-0.5, scalar2=1.5,
                                                          op0=ALU.mult, op1=ALU.add),
                         reads=["l2m2"], writes=["l2m2"], tiny=True)
                    p.op("dve", lambda e: e.tensor_mul(lncol[:, 1:2], lncol[:, 1:2], lncol[:, 7:8]),
                         reads=["l2m2", "l2r"], writes=["l2r"], tiny=True)
                p.op("dve", lambda e: e.tensor_mul(lncol[:, 2:3], lncol[:, 5:6],
                                                   lncol[:, 1:2]),
                     reads=["l2mean", "l2r"], writes=["l2n"], tiny=True)
                p.op("dve", lambda e: e.tensor_scalar_mul(lncol[:, 2:3], lncol[:, 2:3], -1.0),
                     reads=["l2n"], writes=["l2n"], tiny=True)
                if TAP7 and tile == 7 and c == 3:
                    p.op("gq", lambda e: e.dma_start(out=d7["d7_lncol"][:], in_=lncol[:]),
                         reads=["l2sum", "l2sq", "l2mean", "l2msq", "l2m2", "l2a", "l2r", "l2n"],
                         writes=[], dma=True)
                    p.op("gq", lambda e: e.dma_start(out=d7["d7_x2"][:], in_=x2tm[3][:]),
                         reads=[("x2tm", 3, jj) for jj in range(16)], writes=[], dma=True)
                p.op("dve", lambda e, c=c: e.tensor_scalar(
                        out=x2tm[c][:], in0=x2tm[c][:], scalar1=lncol[:, 1:2],
                        scalar2=lncol[:, 2:3], op0=ALU.mult, op1=ALU.add),
                     reads=["l2r", "l2n"] + [("x2tm", c, jj) for jj in range(16)],
                     writes=[("x2f", c)] + [("x2tm", c, jj) for jj in range(16)])
                if tile == 0 and c == 0:
                    dtap("d_ln2", stats[:, 0:32], [("bn", ss) for ss in range(4)] + ["bnagg"])
                    dtap("d_x2post", x2tm[0][:], [("x2f", 0)])
                dst = o_d[s][b * T + c * 128: b * T + (c + 1) * 128, :]
                p.op("gq", lambda e, c=c, dst=dst: e.dma_start(out=dst, in_=x2tm[c][:]),
                     reads=[("x2f", c)] + [("x2tm", c, jj) for jj in range(16)],
                     writes=[("store", tile, c)], dma=True)

        p.finalize()
        totals = dict(p.totals)

        @block.tensor
        def _(eng):
            p.emit_stream(eng, sems, ("pe",))

        @block.scalar
        def _(eng):
            p.emit_stream(eng, sems, ("act",),
                          spacer=lambda e: e.activation(out=stats[:, 9:10], in_=tails[:, 0:1],
                                                        func=AF.Copy))

        @block.vector
        def _(eng):
            p.emit_stream(eng, sems, ("dve",),
                          spacer=lambda e: e.tensor_copy(out=stats[:, 8:9], in_=tails[:, 0:1]))

        @block.sync
        def _(eng):
            p.emit_stream(eng, sems, ("sync",))

        @block.gpsimd
        def _(eng):
            p.emit_stream(eng, sems, ("gq",))
            for qi in range(DMAPOOL):
                t = totals.get(("gq", qi), 0)
                if t:
                    eng.wait_ge(sems[("gq", qi)], t)

    import sys as _sys
    print(f"[kernel] ops={len(p.ops)} spacers={getattr(p, 'n_spacers', 0)}", file=_sys.stderr)
    return nc


_NC_CACHE = None


def kernel(**inputs):
    global _NC_CACHE
    shared, us = host_prep(inputs)
    if _NC_CACHE is None:
        _NC_CACHE = build_program()
    nc = _NC_CACHE
    in_maps = []
    for c in range(NCORES):
        m = dict(shared)
        m["u"] = us[c]
        in_maps.append(m)
    res = run_bass_kernel_spmd(nc, in_maps, core_ids=list(range(NCORES)))
    out = np.empty((16, 2048, D), np.float32)
    for c in range(NCORES):
        out[2 * c:2 * c + 2] = np.asarray(res.results[c]["o"])
    return out


# version 03 (final)
# speedup vs baseline: 1.0188x; 1.0063x over previous
import numpy as np
import ml_dtypes
import concourse.bass as bass
import concourse.mybir as mybir
from concourse.bass_utils import run_bass_kernel_spmd

F32 = mybir.dt.float32
BF16 = mybir.dt.bfloat16
I32 = mybir.dt.int32
AF = mybir.ActivationFunctionType
ALU = mybir.AluOpType
BF = ml_dtypes.bfloat16

NCORES = 8
D = 2048
DSSM = 4096
NH = 64
ECH_IN = 67
KD = 16
KF = 32
KH = 44
T = 512
NTILE = 8
NCH = 4
EPS = 1e-5

E_ORDER = [66] + list(range(32, 66)) + list(range(32))


def _wslices():
    sl = []
    for j in E_ORDER:
        sl.append(("w_in", j, 0, 8)); sl.append(("w_in", j, 8, 16))
    for j in range(16):
        for q in range(4):
            sl.append(("w_out", j, q * 8, q * 8 + 8))
    for j in range(44):
        sl.append(("wqg", j, 0, 8)); sl.append(("wqg", j, 8, 16))
        sl.append(("wqu", j, 0, 8)); sl.append(("wqu", j, 8, 16))
    for j in range(16):
        for q in range(6):
            sl.append(("wqd", j, q * 8, min(q * 8 + 8, 44)))
    return sl


WSLICES = _wslices()
DEBUG_TILE0 = False
NSL = len(WSLICES)
NSLOT = 8


DMAPOOL = 24


class Prog:

    def __init__(self):
        self.ops = []
        self.last_w = {}
        self.readers = {}
        self.dmacnt = {"sync": 0, "gq": 0}

    def op(self, sem_key, fn, reads=(), writes=(), dma=False, tiny=False, waits=()):
        idx = len(self.ops)
        raw = set()
        for r in reads:
            w = self.last_w.get(r)
            if w is not None:
                raw.add(w)
        for r in waits:
            w = self.last_w.get(r)
            if w is not None:
                raw.add(w)
        deps = set(raw)
        for r in writes:
            w = self.last_w.get(r)
            if w is not None:
                deps.add(w)
            for rd in self.readers.get(r, ()):
                deps.add(rd)
        if dma:
            queue = "sync" if sem_key.startswith("sync") else "gq"
            qi = self.dmacnt[queue]
            self.dmacnt[queue] = qi + 1
            stream = queue
            sem = (queue, qi % DMAPOOL)
        else:
            stream = sem_key
            sem = sem_key
        import os
        if os.environ.get("KDRAIN_ALL") and not dma:
            tiny = True
        self.ops.append({"sem": sem, "stream": stream, "fn": fn, "deps": deps,
                         "raw": raw, "dma": dma, "tiny": tiny})
        for r in reads:
            self.readers.setdefault(r, []).append(idx)
        for r in writes:
            self.last_w[r] = idx
            self.readers[r] = []
        return idx

    def finalize(self):
        n = len(self.ops)
        self.need_inc = [o["dma"] for o in self.ops]
        for o in self.ops:
            for d in o["deps"]:
                if self.ops[d]["stream"] != o["stream"] or self.ops[d]["dma"]:
                    self.need_inc[d] = True
        cnt = {}
        self.val = [0] * n
        for i, o in enumerate(self.ops):
            if self.need_inc[i]:
                cnt[o["sem"]] = cnt.get(o["sem"], 0) + (16 if o["dma"] else 1)
            self.val[i] = cnt.get(o["sem"], 0)
        self.totals = cnt

    def emit_stream(self, eng, sems, streams, spacer=None):
        watermark = {}
        drained_at = -1
        prev = None
        for i, o in enumerate(self.ops):
            if o["stream"] not in streams:
                continue
            want = {}
            need_drain = False
            for d in o["deps"]:
                dop = self.ops[d]
                if dop["stream"] == o["stream"] and not dop["dma"]:
                    if dop.get("tiny") and d > drained_at and not o["dma"]:
                        need_drain = True
                    continue
                if dop["stream"] == o["stream"] and dop["dma"] and not o["dma"]:
                    pass
                v = self.val[d]
                ds = dop["sem"]
                if v > want.get(ds, 0):
                    want[ds] = v
            if o["dma"]:
                pv = self.val[i] - 16
                if pv > 0 and pv > want.get(o["sem"], 0):
                    want[o["sem"]] = pv
            for ds, v in sorted(want.items(), key=lambda kv: str(kv[0])):
                if v > watermark.get(ds, 0):
                    eng.wait_ge(sems[ds], v)
                    watermark[ds] = v
            if need_drain:
                eng.drain()
                drained_at = i
            if (spacer is not None and prev is not None and prev in o["raw"]
                    and self.ops[prev]["tiny"] and not self.need_inc[prev]):
                spacer(eng)
                self.n_spacers = getattr(self, "n_spacers", 0) + 1
            ins = o["fn"](eng)
            if self.need_inc[i]:
                if o.get("tiny") and not o["dma"]:
                    eng.drain().then_inc(sems[o["sem"]], 1)
                    drained_at = i
                else:
                    ins.then_inc(sems[o["sem"]], 16 if o["dma"] else 1)
            prev = i


def host_prep(inputs):
    d = {}
    u = np.asarray(inputs["L0_attn_u"], np.float32)
    w_in = np.asarray(inputs["L0_attn_W_in"], np.float32)
    w_pad = np.zeros((ECH_IN * 128, D), np.float32)
    w_pad[:8512] = w_in
    d["w_in"] = np.ascontiguousarray(
        w_pad.reshape(ECH_IN, 128, KD, 128).transpose(0, 3, 2, 1)).reshape(ECH_IN, 128, KD * 128)
    w_out = np.asarray(inputs["L0_attn_W_out"], np.float32)
    d["w_out"] = np.ascontiguousarray(
        w_out.reshape(16, 128, KF, 128).transpose(0, 3, 2, 1)).reshape(16, 128, KF * 128)
    for nm, key, ne, kk in (("wqg", "L0_ffn_wq_gate", 44, KD),
                            ("wqu", "L0_ffn_wq_up", 44, KD),
                            ("wqd", "L0_ffn_wq_down", 16, KH)):
        wq = np.asarray(inputs[key], np.int32)
        d[nm] = np.ascontiguousarray(
            wq.reshape(ne, 128, kk, 128).transpose(0, 3, 2, 1)).reshape(ne, 128, kk * 128)
    cw = np.asarray(inputs["L0_attn_conv_w"], np.float32).reshape(4352, 4)
    d["cw"] = np.ascontiguousarray(cw.reshape(34, 128, 4).transpose(1, 0, 2).reshape(128, 136))
    d["cb"] = np.ascontiguousarray(np.asarray(inputs["L0_attn_conv_b"], np.float32).reshape(34, 128).T)
    d["dtb"] = np.ascontiguousarray(np.asarray(inputs["L0_attn_dt_bias"], np.float32).reshape(NH, 1))
    d["alog"] = np.ascontiguousarray(np.asarray(inputs["L0_attn_A_log"], np.float32).reshape(NH, 1))
    d["db"] = np.ascontiguousarray(
        np.broadcast_to(np.asarray(inputs["L0_attn_D"], np.float32)[None, :], (128, NH)))
    d["nw"] = np.ascontiguousarray(np.asarray(inputs["L0_attn_norm_w"], np.float32).reshape(32, 128).T)
    d["sg"] = np.ascontiguousarray(np.asarray(inputs["L0_ffn_s_gate"], np.float32).reshape(44, 128).T)
    d["su"] = np.ascontiguousarray(np.asarray(inputs["L0_ffn_s_up"], np.float32).reshape(44, 128).T)
    d["sd"] = np.ascontiguousarray(np.asarray(inputs["L0_ffn_s_down"], np.float32).reshape(16, 128).T)
    d["idf"] = np.eye(128, dtype=np.float32)
    d["idb"] = np.eye(128, dtype=np.float32).astype(BF)
    d["id64"] = np.eye(64, dtype=np.float32).astype(BF)
    d["ones128"] = np.ones((128, 1), np.float32).astype(BF)
    d["ones1"] = np.ones((1, 128), np.float32).astype(BF)
    mask = np.zeros((128, 128), np.float32)
    mv = np.float32(ml_dtypes.bfloat16(np.exp(5.0)))
    for s in range(128):
        mask[s, s:] = mv
    d["mask"] = mask.astype(BF)
    d["zeros"] = np.zeros((128, DSSM), np.float32)
    us = [np.ascontiguousarray(u[2 * c:2 * c + 2].transpose(0, 2, 1)) for c in range(NCORES)]
    return d, us


def build_program():
    nc = bass.Bass("TRN2", target_bir_lowering=False)
    io = {}

    def din(name, shape, dt):
        io[name] = nc.dram_tensor(name, shape, dt, kind="ExternalInput").ap()
        return io[name]

    u_d = din("u", [2, D, 2048], F32)
    wdram = {
        "w_in": din("w_in", [ECH_IN, 128, KD * 128], F32),
        "w_out": din("w_out", [16, 128, KF * 128], F32),
        "wqg": din("wqg", [44, 128, KD * 128], I32),
        "wqu": din("wqu", [44, 128, KD * 128], I32),
        "wqd": din("wqd", [16, 128, KH * 128], I32),
    }
    cw_d = din("cw", [128, 136], F32)
    cb_d = din("cb", [128, 34], F32)
    dtb_d = din("dtb", [NH, 1], F32)
    alog_d = din("alog", [NH, 1], F32)
    db_d = din("db", [128, NH], F32)
    nw_d = din("nw", [128, 32], F32)
    sg_d = din("sg", [128, 44], F32)
    su_d = din("su", [128, 44], F32)
    sd_d = din("sd", [128, 16], F32)
    idf_d = din("idf", [128, 128], F32)
    idb_d = din("idb", [128, 128], BF16)
    id64_d = din("id64", [64, 64], BF16)
    ones128_d = din("ones128", [128, 1], BF16)
    ones1_d = din("ones1", [1, 128], BF16)
    mask_d = din("mask", [128, 128], BF16)
    zeros_d = din("zeros", [128, DSSM], F32)

    o_d = nc.dram_tensor("o", [2, 2048, D], F32, kind="ExternalOutput").ap()
    scr = {
        "w_in": nc.dram_tensor("scr_in", [ECH_IN, 128, KD * 128], BF16, kind="Internal").ap(),
        "w_out": nc.dram_tensor("scr_out", [16, 128, KF * 128], BF16, kind="Internal").ap(),
        "wqg": nc.dram_tensor("scr_g", [44, 128, KD * 128], BF16, kind="Internal").ap(),
        "wqu": nc.dram_tensor("scr_u", [44, 128, KD * 128], BF16, kind="Internal").ap(),
        "wqd": nc.dram_tensor("scr_d", [16, 128, KH * 128], BF16, kind="Internal").ap(),
    }
    scr_x1n = nc.dram_tensor("scr_x1n", [KD, 128, T], F32, kind="Internal").ap()
    shield = nc.dram_tensor("shield", [NTILE, NCH, 128, 64], F32, kind="Internal").ap()
    import os as _os7
    TAP7 = bool(_os7.environ.get("K_TAP7"))
    d7 = {}
    if TAP7:
        d7["d7_stats"] = nc.dram_tensor("d7_stats", [128, 8], F32, kind="ExternalOutput").ap()
        d7["d7_lncol"] = nc.dram_tensor("d7_lncol", [128, 8], F32, kind="ExternalOutput").ap()
        d7["d7_x2"] = nc.dram_tensor("d7_x2", [128, 2048], F32, kind="ExternalOutput").ap()
    dbg = {}
    if DEBUG_TILE0:
        for nm, shape, dt in (
            ("d_ubf", [128, KD * T], BF16), ("d_zbf", [128, 32 * T], BF16),
            ("d_xtm", [128, 32 * T], BF16), ("d_bcbf", [128, 2 * T], BF16),
            ("d_dtsoft", [64, T], F32), ("d_cum", [64, T], F32),
            ("d_biasH", [64, T], BF16), ("d_biasL", [64, T], BF16),
            ("d_outscT", [128, 256], F32), ("d_expcum", [128, 256], F32),
            ("d_expTotB", [128, 256], F32), ("d_cumth", [128, 256], F32),
            ("d_Y0", [128, 4096], F32), ("d_ynfm", [128, 32 * T], BF16),
            ("d_x1", [128, KD * T], F32), ("d_x1nbf", [128, KD * T], BF16),
            ("d_h", [128, 44 * T], BF16), ("d_x2tm", [128, 2048], F32),
            ("d_S1", [128, 4096], F32),
            ("d_Ydsk", [128, 4096], F32), ("d_yg", [128, 4096], F32),
            ("d_Y1", [128, 4096], F32), ("d_M41", [128, 512], BF16), ("d_D41", [128, 512], BF16),
            ("d_ygc1", [128, 4096], F32), ("d_rstd1", [128, 53], F32),
            ("d_yn1", [128, 32 * T], BF16), ("d_ynfm2", [128, 32 * T], BF16),
            ("d_ln2", [128, 32], F32), ("d_x2post", [128, 2048], F32),
            ("d_rstd", [128, 53], F32), ("d_yn", [128, 32 * T], BF16),
        ):
            dbg[nm] = nc.dram_tensor(nm, shape, dt, kind="ExternalOutput").ap()

    p = Prog()

    import contextlib
    with contextlib.ExitStack() as _stk:
        c_idf = _stk.enter_context(nc.sbuf_tensor("c_idf", [128, 128], F32))
        c_idb = _stk.enter_context(nc.sbuf_tensor("c_idb", [128, 128], BF16))
        c_id64 = _stk.enter_context(nc.sbuf_tensor("c_id64", [64, 64], BF16))
        c_ones128 = _stk.enter_context(nc.sbuf_tensor("c_ones128", [128, 1], BF16))
        c_ones1 = _stk.enter_context(nc.sbuf_tensor("c_ones1", [1, 128], BF16))
        c_mask = _stk.enter_context(nc.sbuf_tensor("c_mask", [128, 128], BF16))
        c_cw = _stk.enter_context(nc.sbuf_tensor("c_cw", [128, 136], F32))
        c_cb = _stk.enter_context(nc.sbuf_tensor("c_cb", [128, 34], F32))
        c_dtb = _stk.enter_context(nc.sbuf_tensor("c_dtb", [NH, 1], F32))
        c_alog = _stk.enter_context(nc.sbuf_tensor("c_alog", [NH, 1], F32))
        c_negea = _stk.enter_context(nc.sbuf_tensor("c_negea", [NH, 2], F32))
        c_db = _stk.enter_context(nc.sbuf_tensor("c_db", [128, NH], F32))
        c_nw = _stk.enter_context(nc.sbuf_tensor("c_nw", [128, 32], F32))
        c_sg = _stk.enter_context(nc.sbuf_tensor("c_sg", [128, 44], F32))
        c_su = _stk.enter_context(nc.sbuf_tensor("c_su", [128, 44], F32))
        c_sd = _stk.enter_context(nc.sbuf_tensor("c_sd", [128, 16], F32))
        u_bf = _stk.enter_context(nc.sbuf_tensor("u_bf", [128, KD * T], BF16))
        zbf = _stk.enter_context(nc.sbuf_tensor("zbf", [128, 32 * T], BF16))
        xtm = _stk.enter_context(nc.sbuf_tensor("xtm", [128, 32 * T], BF16))
        bcbf = _stk.enter_context(nc.sbuf_tensor("bcbf", [128, 2 * T], BF16))
        S_sb = _stk.enter_context(nc.sbuf_tensor("S", [128, DSSM], F32))
        S_bf0 = _stk.enter_context(nc.sbuf_tensor("S_bf0", [128, 1024], BF16))
        S_bf1 = _stk.enter_context(nc.sbuf_tensor("S_bf1", [128, 1024], BF16))
        arena = _stk.enter_context(nc.sbuf_tensor("arena", [128, 22528], BF16))
        dtraw_t = _stk.enter_context(nc.sbuf_tensor("dtraw_t", [64, T], F32))
        tmpA_t = _stk.enter_context(nc.sbuf_tensor("tmpA_t", [64, T], F32))
        tmpB_t = _stk.enter_context(nc.sbuf_tensor("tmpB_t", [64, T], F32))
        dtsoft_t = _stk.enter_context(nc.sbuf_tensor("dtsoft_t", [64, T], F32))
        lndt_t = _stk.enter_context(nc.sbuf_tensor("lndt_t", [64, T], F32))
        cum_t = _stk.enter_context(nc.sbuf_tensor("cum_t", [64, T], F32))
        totB_t = _stk.enter_context(nc.sbuf_tensor("totB_t", [64, T], F32))
        outsc_t = _stk.enter_context(nc.sbuf_tensor("outsc_t", [64, T], F32))
        biasH = _stk.enter_context(nc.sbuf_tensor("biasH", [64, T], BF16))
        biasL = _stk.enter_context(nc.sbuf_tensor("biasL", [64, T], BF16))
        zt64 = _stk.enter_context(nc.sbuf_tensor("zt64", [64, 128], F32))
        cum_th = _stk.enter_context(nc.sbuf_tensor("cum_th", [128, 256], F32))
        expcum = _stk.enter_context(nc.sbuf_tensor("expcum", [128, 256], F32))
        outscT = _stk.enter_context(nc.sbuf_tensor("outscT", [128, 256], F32))
        expTotB = _stk.enter_context(nc.sbuf_tensor("expTotB", [128, 256], F32))
        Gtm0 = _stk.enter_context(nc.sbuf_tensor("Gtm0", [128, 128], BF16))
        Gtm1 = _stk.enter_context(nc.sbuf_tensor("Gtm1", [128, 128], BF16))
        BT0 = _stk.enter_context(nc.sbuf_tensor("BT0", [128, 128], BF16))
        BT1 = _stk.enter_context(nc.sbuf_tensor("BT1", [128, 128], BF16))
        M4_0 = _stk.enter_context(nc.sbuf_tensor("M4_0", [128, 512], BF16))
        M4_1 = _stk.enter_context(nc.sbuf_tensor("M4_1", [128, 512], BF16))
        D4_0 = _stk.enter_context(nc.sbuf_tensor("D4_0", [128, 512], BF16))
        D4_1 = _stk.enter_context(nc.sbuf_tensor("D4_1", [128, 512], BF16))
        xo0 = _stk.enter_context(nc.sbuf_tensor("xo0", [128, 256], BF16))
        xo1 = _stk.enter_context(nc.sbuf_tensor("xo1", [128, 256], BF16))
        t4_0 = _stk.enter_context(nc.sbuf_tensor("t4_0", [128, 256], BF16))
        t4_1 = _stk.enter_context(nc.sbuf_tensor("t4_1", [128, 256], BF16))
        stats = _stk.enter_context(nc.sbuf_tensor("stats", [128, 216], F32))
        cin0 = _stk.enter_context(nc.sbuf_tensor("cin0", [128, 515], F32))
        cin1 = _stk.enter_context(nc.sbuf_tensor("cin1", [128, 515], F32))
        cout0 = _stk.enter_context(nc.sbuf_tensor("cout0", [128, 512], F32))
        cout1 = _stk.enter_context(nc.sbuf_tensor("cout1", [128, 512], F32))
        sx0 = _stk.enter_context(nc.sbuf_tensor("sx0", [128, 512], BF16))
        sx1 = _stk.enter_context(nc.sbuf_tensor("sx1", [128, 512], BF16))
        tails = _stk.enter_context(nc.sbuf_tensor("tails", [128, 102], F32))
        wring = _stk.enter_context(nc.sbuf_tensor("wring", [128, NSLOT * 1024], BF16))
        ustg0 = _stk.enter_context(nc.sbuf_tensor("ustg0", [128, T], F32))
        lncol = _stk.enter_context(nc.sbuf_tensor("lncol", [128, 8], F32))
        psum = _stk.enter_context(nc.psum_tensor("psum", [128, 4096], F32))
        s_pe = _stk.enter_context(nc.semaphore("s_pe"))
        s_act = _stk.enter_context(nc.semaphore("s_act"))
        s_dve = _stk.enter_context(nc.semaphore("s_dve"))
        dma_sems = {}
        for qn in ("sync", "gq"):
            for qi in range(DMAPOOL):
                dma_sems[(qn, qi)] = _stk.enter_context(nc.semaphore(f"s_{qn}{qi}"))
        block = _stk.enter_context(nc.Block())
        sems = {"pe": s_pe, "act": s_act, "dve": s_dve}
        sems.update(dma_sems)
        Ys = [arena[:, 0:8192].bitcast(F32), arena[:, 8192:16384].bitcast(F32)]
        silu_sl = [arena[:, 16384:17408].bitcast(F32), arena[:, 17408:18432].bitcast(F32)]
        stage_f = [arena[:, 18432:20480].bitcast(F32), arena[:, 20480:22528].bitcast(F32)]
        stage_i = [arena[:, 18432:20480].bitcast(I32), arena[:, 20480:22528].bitcast(I32)]
        stage5_f = [xtm[:, 0:2048].bitcast(F32), xtm[:, 2048:4096].bitcast(F32)]
        stage5_i = [xtm[:, 0:2048].bitcast(I32), xtm[:, 2048:4096].bitcast(I32)]
        x1_fm = arena[:, 0:16384].bitcast(F32)
        h_bf = arena
        x2tm = [zbf[:, c * 4096:(c + 1) * 4096].bitcast(F32) for c in range(NCH)]
        Gtms = [Gtm0, Gtm1]; BTs = [BT0, BT1]; M4s = [M4_0, M4_1]; D4s = [D4_0, D4_1]
        xos = [xo0, xo1]; t4s = [t4_0, t4_1]; S_bfs = [S_bf0, S_bf1]
        cins = [cin0, cin1]; couts = [cout0, cout1]; sxs = [sx0, sx1]
        ustgs = [ustg0, ustg0]; xas = couts; xbs = sxs
        x1ss = [cin0[:, 0:512], cin1[:, 0:512]]

        def pb(b, lo=0, hi=512):
            return psum[:, 512 * b + lo: 512 * b + hi]

        def pb_bf(b, lo=0, hi=1024):
            return psum[:, 512 * b: 512 * (b + 1)].bitcast(BF16)[:, lo:hi]

        def xtm3(c):
            return xtm[:].rearrange("p (f r) -> p f r", r=512)[:, :, c * 128:(c + 1) * 128]

        def xtm_head(c, h):
            off = (h // 2) * 512 + c * 128 + (h % 2) * 64
            return xtm[:, off: off + 64]

        for dst, src, wr in (
            (c_idf[:], idf_d, []), (c_idb[:], idb_d, []), (c_id64[:], id64_d, []),
            (c_ones128[:], ones128_d, []), (c_ones1[:], ones1_d, []), (c_mask[:], mask_d, []),
            (c_cw[:], cw_d, []), (c_cb[:], cb_d, []), (c_dtb[:], dtb_d, ["dtb"]),
            (c_alog[:], alog_d, ["alog"]), (c_db[:], db_d, []), (c_nw[:], nw_d, []),
            (c_sg[:], sg_d, []), (c_su[:], su_d, []), (c_sd[:], sd_d, []),
            (zt64[:], zeros_d[0:64, 0:128], []),
        ):
            p.op("gq", lambda e, d=dst, s=src: e.dma_start(out=d, in_=s),
                 reads=[], writes=wr, dma=True)
        p.op("act", lambda e: e.activation(out=c_negea[:, 0:1], in_=c_alog[:], func=AF.Exp),
             reads=["alog"], writes=["negea"])
        for _tw in range(2):
            p.op("dve", lambda e: e.tensor_scalar_mul(c_negea[:, 1:2], c_negea[:, 0:1], -1.0),
                 reads=["negea"] if _tw == 0 else [], writes=["negea"], tiny=True)
        p.op("gq", lambda e: e.dma_start(out=tails[:], in_=zeros_d[:, 0:102]),
             reads=[], writes=["tails", "constbar"], dma=True)
        for ek in ("pe", "act", "dve"):
            p.op(ek, lambda e: e.nop(), reads=["constbar"], writes=[])

        wslot_of = {}
        wcursor = {"tile": -1, "i": 0}
        WLOOK = 6

        def emit_slice(tile, si):
            skey = "sync0" if si % 2 == 0 else "gq"
            mat, j, k0, k1 = WSLICES[si]
            gi = tile * NSL + si
            slot = gi % NSLOT
            wslot_of[gi] = slot
            ncol = (k1 - k0) * 128
            dstb = wring[:, slot * 1024: slot * 1024 + ncol]
            if tile == 0:
                st = si % 2
                is_int = mat in ("wqg", "wqu", "wqd")
                src = wdram[mat][j][:, k0 * 128: k1 * 128]
                if is_int:
                    extra = [("ynfm", cc, fc) for cc in range(NCH)
                             for fc in (range(0, 4) if st == 0 else range(4, 8))]
                    stg = stage5_i[st][:, 0:ncol]
                    p.op("sync0", lambda e, d=stg, s=src: e.dma_start(out=d, in_=s),
                         reads=[], writes=[("stg5", st)] + extra, dma=True)
                    p.op("dve", lambda e, d=dstb, s=stg: e.tensor_scalar(
                            out=d, in0=s, scalar1=128.0, scalar2=None, op0=ALU.subtract),
                         reads=[("stg5", st)], writes=[("wr", slot)])
                else:
                    stg = stage_f[st][:, 0:ncol]
                    p.op("sync0", lambda e, d=stg, s=src: e.dma_start(out=d, in_=s),
                         reads=[], writes=[("stg", st)], dma=True)
                    p.op("dve", lambda e, d=dstb, s=stg: e.tensor_copy(out=d, in_=s),
                         reads=[("stg", st)], writes=[("wr", slot)])
                sdst = scr[mat][j][:, k0 * 128: k1 * 128]
                p.op("gq", lambda e, d=sdst, s=dstb: e.dma_start(out=d, in_=s),
                     reads=[("wr", slot)], writes=[("scr", mat, j, k0)], dma=True)
            else:
                src = scr[mat][j][:, k0 * 128: k1 * 128]
                p.op(skey, lambda e, d=dstb, s=src: e.dma_start(out=d, in_=s),
                     reads=[("scr", mat, j, k0)], writes=[("wr", slot)], dma=True)

        def ensure_slices(tile, si_first, si_last, cap=None):
            if wcursor["tile"] != tile:
                wcursor["tile"] = tile
                wcursor["i"] = 0
            target = min(si_first + NSLOT, si_last + 1 + WLOOK, NSL)
            if cap is not None:
                target = min(target, cap)
            target = max(target, si_last + 1)
            while wcursor["i"] < target:
                emit_slice(tile, wcursor["i"])
                wcursor["i"] += 1

        def wres(tile, si):
            return ("wr", wslot_of[tile * NSL + si])

        def wap(tile, si, k):
            slot = wslot_of[tile * NSL + si]
            return wring[:, slot * 1024 + k * 128: slot * 1024 + (k + 1) * 128]

        import os as _os
        _tapsel = _os.environ.get("K_TAPS")
        _tapset = set(_tapsel.split(",")) if _tapsel else None

        def dtap(name, ap, reads):
            if _tapset is not None and name not in _tapset:
                return
            if DEBUG_TILE0 and name in dbg:
                p.op("gq", lambda e, d=dbg[name], s=ap: e.dma_start(out=d, in_=s),
                     reads=list(reads), writes=[], dma=True)

        def emit_uload(tile):
            s, b = divmod(tile, 4)
            for k in range(KD):
                st = k % 2
                src = u_d[s][k * 128:(k + 1) * 128, b * T:(b + 1) * T]
                p.op("gq", lambda e, st=st, src=src: e.dma_start(out=ustgs[st][:], in_=src),
                     reads=[], writes=[("ustg", 0)], dma=True)
                p.op("dve", lambda e, st=st, k=k: e.tensor_copy(
                        out=u_bf[:, k * T:(k + 1) * T], in_=ustgs[st][:]),
                     reads=[("ustg", 0)], writes=[("ubf", k)])

        for tile in range(NTILE):
            s, b = divmod(tile, 4)
            if tile == 0:
                emit_uload(0)
            if tile % 4 == 0:
                p.op("gq", lambda e: e.dma_start(out=S_sb[:], in_=zeros_d),
                     reads=[], writes=[("S", g) for g in range(16)], dma=True)
                p.op("gq", lambda e: e.dma_start(out=tails[:], in_=zeros_d[:, 0:102]),
                     reads=[], writes=["tails"], dma=True)

            for pos, j in enumerate(E_ORDER):
                bank = pos % 2
                si0 = 2 * pos
                ensure_slices(tile, si0, si0 + 1, cap=198)
                for k in range(KD):
                    si = si0 + (k // 8)
                    p.op("pe",
                         lambda e, bank=bank, tile=tile, si=si, k=k: e.matmul(
                             pb(bank), wap(tile, si, k % 8),
                             u_bf[:, k * T:(k + 1) * T],
                             start=(k == 0), stop=(k == KD - 1)),
                         waits=[wres(tile, si), ("ubf", k)],
                         reads=([wres(tile, si)] if k % 8 == 7 else []) +
                               ([("ubf", k)] if pos == len(E_ORDER) - 1 else []),
                         writes=[("ps", bank)] if (k == KD - 1 or k == 0) else [])
                if j == 66:
                    p.op("dve", lambda e, bank=bank: e.tensor_scalar(
                            out=dtraw_t[:], in0=pb(bank)[0:64, :],
                            scalar1=c_dtb[:], scalar2=None, op0=ALU.add),
                         reads=[("ps", bank), "dtb"], writes=["dtraw"], tiny=True)
                    dtraw = dtraw_t[:]; tmpA = tmpA_t[:]
                    tmpB = tmpB_t[:]; dtsoft = dtsoft_t[:]
                    lndt = lndt_t[:]; cum = cum_t[:]
                    totB = totB_t[:]; outsc = outsc_t[:]
                    p.op("act", lambda e: e.activation(out=tmpA, in_=dtraw, func=AF.Relu),
                         reads=["dtraw"], writes=["tmpA"], tiny=True)
                    p.op("act", lambda e: e.activation(out=tmpB, in_=dtraw, func=AF.Abs),
                         reads=["dtraw"], writes=["tmpB"], tiny=True)
                    p.op("act", lambda e: e.activation(out=tmpB, in_=tmpB, func=AF.Exp, scale=-1.0),
                         reads=["tmpB"], writes=["tmpB"], tiny=True)
                    p.op("act", lambda e: e.activation(out=tmpB, in_=tmpB, func=AF.Ln, bias=1.0),
                         reads=["tmpB"], writes=["tmpB"], tiny=True)
                    p.op("dve", lambda e: e.tensor_add(dtsoft, tmpA, tmpB),
                         reads=["tmpA", "tmpB"], writes=["dtsoft"], tiny=True)
                    p.op("act", lambda e: e.activation(out=lndt, in_=dtsoft, func=AF.Ln),
                         reads=["dtsoft"], writes=["lndt"], tiny=True)
                    p.op("dve", lambda e: e.tensor_scalar(out=tmpA, in0=dtsoft, scalar1=c_negea[:, 1:2],
                                                          scalar2=None, op0=ALU.mult),
                         reads=["dtsoft", "negea"], writes=["tmpA"], tiny=True)
                    for c in range(NCH):
                        p.op("dve", lambda e, c=c: e.tensor_tensor_scan(
                                out=cum[:, c * 128:(c + 1) * 128],
                                data0=tmpA[:, c * 128:(c + 1) * 128],
                                data1=zt64[:], initial=0.0, op0=ALU.add, op1=ALU.add),
                             reads=["tmpA"], writes=["cum"], tiny=True)
                    p.op("dve", lambda e: e.tensor_sub(dtraw, lndt, cum),
                         reads=["lndt", "cum"], writes=["dtraw"], tiny=True)
                    _C5 = float(np.log(np.float64(np.float32(ml_dtypes.bfloat16(np.exp(5.0))))))
                    p.op("dve", lambda e: e.tensor_scalar(out=dtraw, in0=dtraw, scalar1=-_C5,
                                                          scalar2=None, op0=ALU.add),
                         reads=["dtraw"], writes=["dtraw"], tiny=True)
                    p.op("dve", lambda e: e.tensor_copy(out=biasH[:], in_=dtraw),
                         reads=["dtraw"], writes=["biasHL"], tiny=True)
                    p.op("dve", lambda e: e.tensor_sub(tmpB, dtraw, biasH[:]),
                         reads=["dtraw", "biasHL"], writes=["tmpB"], tiny=True)
                    p.op("dve", lambda e: e.tensor_copy(out=biasL[:], in_=tmpB),
                         reads=["tmpB"], writes=["biasHL"], tiny=True)
                    for c in range(NCH):
                        p.op("dve", lambda e, c=c: e.tensor_copy(
                                out=totB[:, c * 128:(c + 1) * 128],
                                in_=cum[:, c * 128 + 127: c * 128 + 128].broadcast_to([64, 128])),
                             reads=["cum"], writes=["totB"], tiny=True)
                    p.op("dve", lambda e: e.tensor_sub(outsc, totB, cum),
                         reads=["totB", "cum"], writes=["outsc"], tiny=True)
                    p.op("dve", lambda e: e.tensor_add(outsc, outsc, lndt),
                         reads=["outsc", "lndt"], writes=["outsc"], tiny=True)
                    p.op("act", lambda e: e.activation(out=outsc, in_=outsc, func=AF.Exp),
                         reads=["outsc"], writes=["outsc"], tiny=True)
                    for c in range(NCH):
                        p.op("pe", lambda e, c=c: e.matmul(
                                pb(3, 0, 64), cum[:, c * 128:(c + 1) * 128], c_idf[0:64, 0:64],
                                is_transpose=True, start=True, stop=True),
                             reads=["cum"], writes=[("ps", 3)])
                        p.op("dve", lambda e, c=c: e.tensor_copy(
                                out=cum_th[:, c * 64:(c + 1) * 64], in_=pb(3, 0, 64)),
                             reads=[("ps", 3)], writes=["cum_th"], tiny=True)
                        p.op("pe", lambda e, c=c: e.matmul(
                                pb(3, 0, 64), outsc[:, c * 128:(c + 1) * 128], c_idf[0:64, 0:64],
                                is_transpose=True, start=True, stop=True),
                             reads=["outsc"], writes=[("ps", 3)])
                        p.op("dve", lambda e, c=c: e.tensor_copy(
                                out=outscT[:, c * 64:(c + 1) * 64], in_=pb(3, 0, 64)),
                             reads=[("ps", 3)], writes=["outscT"], tiny=True)
                        p.op("pe", lambda e, c=c: e.matmul(
                                pb(3, 0, 64), totB[:, c * 128:(c + 1) * 128], c_idf[0:64, 0:64],
                                is_transpose=True, start=True, stop=True),
                             reads=["totB"], writes=[("ps", 3)])
                        p.op("act", lambda e, c=c: e.activation(
                                out=expTotB[:, c * 64:(c + 1) * 64], in_=pb(3, 0, 64), func=AF.Exp),
                             reads=[("ps", 3)], writes=["expTotB"], tiny=True)
                        p.op("act", lambda e, c=c: e.activation(
                                out=expcum[:, c * 64:(c + 1) * 64],
                                in_=cum_th[:, c * 64:(c + 1) * 64], func=AF.Exp),
                             reads=["cum_th"], writes=["expcum"], tiny=True)
                elif 32 <= j < 66:
                    fc = j - 32
                    ci = fc % 2
                    p.op("dve", lambda e, fc=fc, ci=ci: e.tensor_copy(
                            out=cins[ci][:, 0:3], in_=tails[:, fc * 3:(fc + 1) * 3]),
                         reads=["tails"], writes=[("cin", ci)], tiny=True)
                    p.op("act", lambda e, bank=bank, ci=ci: e.activation(
                            out=cins[ci][:, 3:515], in_=pb(bank), func=AF.Copy),
                         reads=[("ps", bank)], writes=[("cin", ci)])
                    p.op("dve", lambda e, fc=fc, ci=ci: e.tensor_copy(
                            out=tails[:, fc * 3:(fc + 1) * 3], in_=cins[ci][:, 512:515]),
                         reads=[("cin", ci)], writes=["tails"], tiny=True)
                    p.op("dve", lambda e, ci=ci, fc=fc: e.tensor_scalar(
                            out=couts[ci][:], in0=cins[ci][:, 0:512],
                            scalar1=c_cw[:, fc * 4: fc * 4 + 1], scalar2=None, op0=ALU.mult),
                         reads=[("cin", ci)], writes=[("cout", ci)])
                    for tap in range(1, 4):
                        p.op("dve", lambda e, ci=ci, fc=fc, tap=tap: e.scalar_tensor_tensor(
                                out=couts[ci][:], in0=cins[ci][:, tap:512 + tap],
                                scalar=c_cw[:, fc * 4 + tap: fc * 4 + tap + 1],
                                in1=couts[ci][:], op0=ALU.mult, op1=ALU.add),
                             reads=[("cin", ci), ("cout", ci)], writes=[("cout", ci)])
                    if fc < 32:
                        p.op("act", lambda e, ci=ci, fc=fc: e.activation(
                                out=sxs[ci][:], in_=couts[ci][:], func=AF.Silu,
                                bias=c_cb[:, fc: fc + 1]),
                             reads=[("cout", ci)], writes=[("sx", ci)])
                        tb = 2 + fc % 2
                        extra = [("stg5", 0)] if fc == 0 else ([("stg5", 1)] if fc == 4 else [])
                        for c in range(NCH):
                            p.op("pe", lambda e, ci=ci, c=c, tb=tb: e.matmul(
                                    pb_bf(tb, c * 128, (c + 1) * 128),
                                    sxs[ci][:, c * 128:(c + 1) * 128],
                                    c_idb[:], is_transpose=True, start=(c == 0), stop=(c == 3)),
                                 reads=[("sx", ci)], writes=[("ps", tb)])
                        for c in range(NCH):
                            p.op("dve", lambda e, c=c, fc=fc, tb=tb: e.tensor_copy(
                                    out=xtm[:, fc * 512 + c * 128: fc * 512 + (c + 1) * 128],
                                    in_=pb_bf(tb, c * 128, (c + 1) * 128)),
                                 reads=[("ps", tb)],
                                 writes=[("xtm", c, fc), ("ynfm", c, fc)] + extra, tiny=True)
                    else:
                        bc = fc - 32
                        p.op("act", lambda e, ci=ci, fc=fc, bc=bc: e.activation(
                                out=bcbf[:, bc * 512:(bc + 1) * 512], in_=couts[ci][:],
                                func=AF.Silu, bias=c_cb[:, fc: fc + 1]),
                             reads=[("cout", ci)], writes=[("bcbf", bc)])
                else:
                    zc = j
                    p.op("act", lambda e, bank=bank, zc=zc: e.activation(
                            out=zbf[:, zc * 512:(zc + 1) * 512], in_=pb(bank), func=AF.Copy),
                         reads=[("ps", bank)],
                         writes=[("zbf", zc)] + ([("x2tm", zc // 8, jj) for jj in range(16)]
                                                 if zc % 8 == 0 else []))

            if tile == 0:
                dtap("d_ubf", u_bf[:], [("ubf", k) for k in range(KD)])
                dtap("d_zbf", zbf[:], [("zbf", zc) for zc in range(32)])
                dtap("d_xtm", xtm[:], [("xtm", c, fc) for c in range(NCH) for fc in range(32)])
                dtap("d_bcbf", bcbf[:], [("bcbf", 0), ("bcbf", 1)])
                dtap("d_dtsoft", dtsoft_t[:], ["dtsoft"])
                dtap("d_cum", cum_t[:], ["cum"])
                dtap("d_biasH", biasH[:], ["biasHL"])
                dtap("d_biasL", biasL[:], ["biasHL"])
                dtap("d_outscT", outscT[:], ["outscT"])
                dtap("d_expcum", expcum[:], ["expcum"])
                dtap("d_expTotB", expTotB[:], ["expTotB"])
                dtap("d_cumth", cum_th[:], ["cum_th"])

            for c in range(NCH):
                ci = c % 2
                B_c = bcbf[:, c * 128:(c + 1) * 128]
                C_c = bcbf[:, 512 + c * 128: 512 + (c + 1) * 128]
                p.op("pe", lambda e, B_c=B_c, C_c=C_c: e.matmul(
                        pb(6, 0, 128), B_c, C_c, start=True, stop=True),
                     reads=[("bcbf", 0), ("bcbf", 1)], writes=[("ps", 6)])
                p.op("dve", lambda e, ci=ci: e.tensor_mul(Gtms[ci][:], pb(6, 0, 128), c_mask[:]),
                     reads=[("ps", 6)], writes=[("Gtm", ci)], tiny=True)
                p.op("pe", lambda e, B_c=B_c: e.matmul(
                        pb_bf(7, 0, 128), B_c, c_idb[:], is_transpose=True, start=True, stop=True),
                     reads=[("bcbf", 0)], writes=[("ps", 7)])
                p.op("dve", lambda e, ci=ci: e.tensor_copy(out=BTs[ci][:], in_=pb_bf(7, 0, 128)),
                     reads=[("ps", 7)], writes=[("BT", ci)])

                for g in range(16):
                    gi = g % 2
                    q = g // 4
                    if g % 4 == 0:
                        p.op("act", lambda e, q=q: e.activation(
                                out=S_bfs[q % 2][:], in_=S_sb[:, q * 1024:(q + 1) * 1024],
                                func=AF.Copy),
                             reads=[("S", 4 * q), ("S", 4 * q + 1), ("S", 4 * q + 2),
                                    ("S", 4 * q + 3)],
                             writes=[("Sbf", q % 2)])
                    bR = 2 + gi
                    for i in range(4):
                        p.op("pe", lambda e, i=i, g=g, c=c, bR=bR: e.matmul(
                                pb(bR, i * 128, (i + 1) * 128),
                                cum_th[:, c * 64 + 4 * g + i: c * 64 + 4 * g + i + 1]
                                    .broadcast_to([128, 128]),
                                c_idf[:], is_transpose=True, start=(i == 0), stop=False),
                             reads=["cum_th"], writes=[("ps", bR)])
                    selb = c_id64[:, 4 * g: 4 * g + 4].unsqueeze(2).broadcast_to([64, 4, 128])
                    p.op("pe", lambda e, bR=bR, sb=selb, c=c: e.matmul(
                            pb(bR).rearrange("p (h t) -> p h t", h=4),
                            biasH[:, c * 128:(c + 1) * 128], sb, start=False, stop=False),
                         reads=["biasHL"], writes=[("ps", bR)])
                    p.op("pe", lambda e, bR=bR, sb=selb, c=c: e.matmul(
                            pb(bR).rearrange("p (h t) -> p h t", h=4),
                            biasL[:, c * 128:(c + 1) * 128], sb, start=False, stop=True),
                         reads=["biasHL"], writes=[("ps", bR)])
                    bAB = 4 + gi
                    p.op("act", lambda e, gi=gi, bR=bR, bAB=bAB: e.activation(
                            out=pb(bAB), in_=pb(bR), func=AF.Relu, scale=-1.0),
                         reads=[("ps", bR)], writes=[("ps", bAB)])
                    p.op("act", lambda e, gi=gi, bAB=bAB: e.activation(
                            out=D4s[gi][:], in_=pb(bAB), func=AF.Exp, scale=-1.0),
                         reads=[("ps", bAB)], writes=[("D4", gi)])
                    p.op("dve", lambda e, gi=gi, ci=ci: e.tensor_mul(
                            M4s[gi][:].rearrange("p (h t) -> p h t", h=4),
                            D4s[gi][:].rearrange("p (h t) -> p h t", h=4),
                            Gtms[ci][:].unsqueeze(1).broadcast_to([128, 4, 128])),
                         reads=[("D4", gi), ("Gtm", ci)], writes=[("M4", gi)])
                    xpair = xtm[:].rearrange("p (f r) -> p f r", r=512)[
                        :, 2 * g:2 * g + 2, c * 128:(c + 1) * 128].rearrange(
                        "p h (q w) -> p h q w", q=2)
                    oscb = outscT[:, c * 64 + 4 * g: c * 64 + 4 * g + 4].rearrange(
                        "p (h q) -> p h q", h=2).unsqueeze(3).broadcast_to([128, 2, 2, 64])
                    p.op("dve", lambda e, gi=gi, xp=xpair, ob=oscb: e.tensor_mul(
                            xos[gi][:].rearrange("p (h q w) -> p h q w", h=2, q=2),
                            xp, ob),
                         reads=[("xtm", c, 2 * g), ("xtm", c, 2 * g + 1), "outscT"],
                         writes=[("xo", gi)])
                    for i in range(4):
                        h = 4 * g + i
                        p.op("pe", lambda e, i=i, h=h, gi=gi, bAB=bAB, c=c: e.matmul(
                                pb(bAB, i * 64, (i + 1) * 64),
                                M4s[gi][:, i * 128:(i + 1) * 128],
                                xtm_head(c, h), start=True, stop=True),
                             reads=[("M4", gi), ("xtm", c, h // 2)], writes=[("ps", bAB)])
                    p.op("pe", lambda e, gi=gi, g=g, q=q, bAB=bAB, C_c=C_c: e.matmul(
                            pb(bAB, 256, 512), C_c,
                            S_bfs[q % 2][:, (g % 4) * 256:(g % 4 + 1) * 256],
                            start=True, stop=True),
                         reads=[("bcbf", 1), ("Sbf", q % 2)], writes=[("ps", bAB)])
                    bC = 6 + gi
                    p.op("pe", lambda e, gi=gi, bC=bC, ci=ci: e.matmul(
                            pb(bC, 0, 256), BTs[ci][:], xos[gi][:], start=True, stop=True),
                         reads=[("BT", ci), ("xo", gi)], writes=[("ps", bC)])
                    extra_y = []
                    if c == 0 and g == 0 and ci == 0:
                        extra_y = [("h", jj) for jj in range(44)] + \
                                  [("x1", j2) for j2 in range(KD)] + \
                                  [("stg", 0), ("stg", 1)]
                    p.op("dve", lambda e, gi=gi, g=g, c=c, bAB=bAB: e.tensor_mul(
                            t4s[gi][:].rearrange("p (h w) -> p h w", h=4),
                            pb(bAB, 256, 512).rearrange("p (h w) -> p h w", h=4),
                            expcum[:, c * 64 + 4 * g: c * 64 + 4 * g + 4].unsqueeze(2)
                                .broadcast_to([128, 4, 64])),
                         reads=[("ps", bAB), "expcum"], writes=[("t4", gi)], tiny=True)
                    p.op("dve", lambda e, gi=gi, g=g, ci=ci, bAB=bAB: e.tensor_add(
                            Ys[ci][:, g * 256:(g + 1) * 256], t4s[gi][:], pb(bAB, 0, 256)),
                         reads=[("t4", gi), ("ps", bAB)],
                         writes=[("Y", ci, g)] + extra_y, tiny=True)
                    p.op("dve", lambda e, g=g, c=c: e.tensor_mul(
                            S_sb[:, g * 256:(g + 1) * 256].rearrange("p (h w) -> p h w", h=4),
                            S_sb[:, g * 256:(g + 1) * 256].rearrange("p (h w) -> p h w", h=4),
                            expTotB[:, c * 64 + 4 * g: c * 64 + 4 * g + 4].unsqueeze(2)
                                .broadcast_to([128, 4, 64])),
                         reads=[("S", g), "expTotB"], writes=[("S", g)], tiny=True)
                    p.op("dve", lambda e, g=g, gi=gi, bC=bC: e.tensor_add(
                            S_sb[:, g * 256:(g + 1) * 256], S_sb[:, g * 256:(g + 1) * 256],
                            pb(bC, 0, 256)),
                         reads=[("S", g), ("ps", bC)], writes=[("S", g)], tiny=True)

                if tile == 0 and c == 0:
                    dtap("d_Y0", Ys[0][:], [("Y", 0, g) for g in range(16)])
                    dtap("d_S1", S_sb[:], [("S", g) for g in range(16)])
                if tile == 0 and c == 1:
                    dtap("d_Y1", Ys[1][:], [("Y", 1, g) for g in range(16)])
                    dtap("d_M41", M4s[1][:], [("M4", 1)])
                    dtap("d_D41", D4s[1][:], [("D4", 1)])

                dskb = c_db[:].rearrange("p (f q) -> p f q", q=2).unsqueeze(3) \
                    .broadcast_to([128, 32, 2, 64])
                xall4 = xtm[:].rearrange("p (f r) -> p f r", r=512)[
                    :, :, c * 128:(c + 1) * 128].rearrange("p f (q w) -> p f q w", q=2)
                p.op("dve", lambda e, xa4=xall4, db4=dskb: e.tensor_mul(xa4, xa4, db4),
                     reads=[("xtm", c, fc) for fc in range(32)],
                     writes=[("xtm", c, fc) for fc in range(32)], tiny=True)
                xall3 = xtm[:].rearrange("p (f r) -> p f r", r=512)[:, :, c * 128:(c + 1) * 128]
                p.op("dve", lambda e, ci=ci, xa3=xall3: e.tensor_add(
                        Ys[ci][:].rearrange("p (f r) -> p f r", r=128),
                        Ys[ci][:].rearrange("p (f r) -> p f r", r=128), xa3),
                     reads=[("Y", ci, g) for g in range(16)] +
                           [("xtm", c, fc) for fc in range(32)],
                     writes=[("Y", ci, g) for g in range(16)], tiny=True)
                if tile == 0 and c == 0:
                    dtap("d_Ydsk", Ys[0][:], [("Y", 0, g) for g in range(16)])
                for ss in range(8):
                    tb = 2 + ss % 2
                    for i in range(4):
                        fc = ss * 4 + i
                        p.op("pe", lambda e, fc=fc, i=i, tb=tb, c=c: e.matmul(
                                pb_bf(tb, i * 128, (i + 1) * 128),
                                zbf[:, fc * 512 + c * 128: fc * 512 + (c + 1) * 128],
                                c_idb[:], is_transpose=True, start=(i == 0), stop=(i == 3)),
                             reads=[("zbf", fc)], writes=[("ps", tb)])
                    extra_s = [("h", 32), ("h", 33), ("h", 34), ("h", 35)] \
                        if (ss < 2 and c == 0) else []
                    p.op("act", lambda e, ss=ss, tb=tb: e.activation(
                            out=silu_sl[ss % 2][:], in_=pb_bf(tb, 0, 512), func=AF.Silu),
                         reads=[("ps", tb)], writes=[("silu", ss % 2)] + extra_s)
                    p.op("dve", lambda e, ss=ss, ci=ci: e.tensor_mul(
                            Ys[ci][:, ss * 512:(ss + 1) * 512],
                            Ys[ci][:, ss * 512:(ss + 1) * 512], silu_sl[ss % 2][:]),
                         reads=[("silu", ss % 2), ("Y", ci, 2 * ss), ("Y", ci, 2 * ss + 1)],
                         writes=[("Y", ci, 2 * ss), ("Y", ci, 2 * ss + 1)], tiny=True)
                for ss in range(8):
                    p.op("act", lambda e, ss=ss, ci=ci, c=c: e.activation(
                            out=pb(4), in_=Ys[ci][:, ss * 512:(ss + 1) * 512], func=AF.Square,
                            accum_out=stats[:, c * 54 + ss: c * 54 + ss + 1]),
                         reads=[("Y", ci, 2 * ss), ("Y", ci, 2 * ss + 1)],
                         writes=[("bn", c, ss), ("ps", 4)], tiny=True)
                p.op("dve", lambda e, c=c: e.tensor_reduce(
                        out=stats[:, c * 54 + 50: c * 54 + 51], in_=stats[:, c * 54: c * 54 + 8],
                        axis=mybir.AxisListType.X, op=ALU.add),
                     reads=[("bn", c, ss) for ss in range(8)], writes=[("msq", c)], tiny=True)
                p.op("dve", lambda e, c=c: e.tensor_scalar(out=stats[:, c * 54 + 51: c * 54 + 52], in0=stats[:, c * 54 + 50: c * 54 + 51],
                                                      scalar1=1.0 / DSSM, scalar2=EPS, op0=ALU.mult, op1=ALU.add),
                     reads=[("msq", c)], writes=[("lnm", c)], tiny=True)
                p.op("dve", lambda e, c=c: e.tensor_scalar(
                        out=stats[:, c * 54 + 52: c * 54 + 53].bitcast(I32),
                        in0=stats[:, c * 54 + 51: c * 54 + 52].bitcast(I32),
                        scalar1=1, scalar2=None, op0=ALU.arith_shift_right),
                     reads=[("lnm", c)], writes=[("rstd_rms", c)], tiny=True)
                p.op("dve", lambda e, c=c: e.tensor_scalar(
                        out=stats[:, c * 54 + 52: c * 54 + 53].bitcast(I32),
                        in0=stats[:, c * 54 + 52: c * 54 + 53].bitcast(I32),
                        scalar1=-1, scalar2=0x5F3759DF, op0=ALU.mult, op1=ALU.add),
                     reads=[("rstd_rms", c)], writes=[("rstd_rms", c)], tiny=True)
                for _nit in range(2):
                    p.op("dve", lambda e, c=c: e.tensor_mul(
                            stats[:, c * 54 + 53: c * 54 + 54],
                            stats[:, c * 54 + 52: c * 54 + 53], stats[:, c * 54 + 52: c * 54 + 53]),
                         reads=[("rstd_rms", c)], writes=[("rsT", c)], tiny=True)
                    p.op("dve", lambda e, c=c: e.tensor_mul(
                            stats[:, c * 54 + 53: c * 54 + 54],
                            stats[:, c * 54 + 53: c * 54 + 54], stats[:, c * 54 + 51: c * 54 + 52]),
                         reads=[("rsT", c), ("lnm", c)], writes=[("rsT", c)], tiny=True)
                    p.op("dve", lambda e, c=c: e.tensor_scalar(
                            out=stats[:, c * 54 + 53: c * 54 + 54],
                            in0=stats[:, c * 54 + 53: c * 54 + 54], scalar1=-0.5, scalar2=1.5,
                            op0=ALU.mult, op1=ALU.add),
                         reads=[("rsT", c)], writes=[("rsT", c)], tiny=True)
                    p.op("dve", lambda e, c=c: e.tensor_mul(
                            stats[:, c * 54 + 52: c * 54 + 53],
                            stats[:, c * 54 + 52: c * 54 + 53], stats[:, c * 54 + 53: c * 54 + 54]),
                         reads=[("rsT", c)], writes=[("rstd_rms", c)], tiny=True)
                import os as _os2
                if _os2.environ.get("K_SHIELD"):
                    p.op("gq", lambda e, tile=tile, c=c: e.dma_start(
                            out=shield[tile][c][:, 0:53], in_=stats[:, c * 54: c * 54 + 53]),
                         reads=[("bn", c, ss) for ss in range(8)] +
                               [("bnagg", c), ("msq", c), ("lnm", c), ("rstd_rms", c)],
                         writes=[], dma=True)
                if tile == 0 and c == 0:
                    dtap("d_yg", Ys[0][:], [("Y", 0, g) for g in range(16)])
                    dtap("d_rstd", stats[:, 0:53],
                         [("bn", c, ss) for ss in range(8)] + [("bnagg", c), ("msq", c), ("lnm", c), ("rstd_rms", c)])
                if tile == 0 and c == 1:
                    dtap("d_ygc1", Ys[1][:], [("Y", 1, g) for g in range(16)])
                    dtap("d_rstd1", stats[:, 54:107],
                         [("bn", c, ss) for ss in range(8)] + [("bnagg", c), ("msq", c), ("lnm", c), ("rstd_rms", c)])
                p.op("dve", lambda e, ci=ci, xa3=xall3, c=c: e.tensor_scalar(
                        out=xa3, in0=Ys[ci][:].rearrange("p (f r) -> p f r", r=128),
                        scalar1=stats[:, c * 54 + 52: c * 54 + 53], scalar2=None, op0=ALU.mult),
                     reads=[("Y", ci, g) for g in range(16)] + [("rstd_rms", c)],
                     writes=[("xtm", c, fc) for fc in range(32)], tiny=True)
                if tile == 0 and c == 0:
                    dtap("d_yn", xtm[:], [("xtm", 0, fc) for fc in range(32)])
                if tile == 0 and c == 1:
                    dtap("d_yn1", xtm[:], [("xtm", 1, fc) for fc in range(32)])
                for ss in range(8):
                    tb = 2 + ss % 2
                    for i in range(4):
                        fc = ss * 4 + i
                        p.op("pe", lambda e, fc=fc, i=i, tb=tb, c=c: e.matmul(
                                pb_bf(tb, i * 128, (i + 1) * 128),
                                xtm[:, fc * 512 + c * 128: fc * 512 + (c + 1) * 128],
                                c_idb[:], is_transpose=True, start=(i == 0), stop=(i == 3)),
                             reads=[("xtm", c, fc)], writes=[("ps", tb)])
                    for i in range(4):
                        fc = ss * 4 + i
                        p.op("dve", lambda e, fc=fc, i=i, tb=tb, c=c: e.tensor_scalar(
                                out=xtm[:, fc * 512 + c * 128: fc * 512 + (c + 1) * 128],
                                in0=pb_bf(tb, i * 128, (i + 1) * 128),
                                scalar1=c_nw[:, fc: fc + 1], scalar2=None, op0=ALU.mult),
                             reads=[("ps", tb)], writes=[("ynfm", c, fc), ("xtm", c, fc)], tiny=True)

            if tile == 0:
                dtap("d_ynfm", xtm[:], [("ynfm", c, fc) for c in range(NCH) for fc in range(32)])

            si_op = 2 * len(E_ORDER)
            for j in range(16):
                bank = j % 2
                ensure_slices(tile, si_op + j * 4, si_op + j * 4 + 3, cap=198)
                for k in range(KF):
                    si = si_op + j * 4 + k // 8
                    p.op("pe", lambda e, bank=bank, tile=tile, si=si, k=k: e.matmul(
                            pb(bank), wap(tile, si, k % 8),
                            xtm[:, k * 512:(k + 1) * 512], start=(k == 0), stop=(k == KF - 1)),
                         waits=[wres(tile, si)] + [("ynfm", cc, k) for cc in range(NCH)],
                         reads=([wres(tile, si)] if k % 8 == 7 else []) +
                               ([("ynfm", cc, kk) for cc in range(NCH) for kk in range(KF)]
                                if (j == 15 and k == KF - 1) else []),
                         writes=([("ps", bank), ("p4done",)] if k == KF - 1 else
                                 ([("ps", bank)] if k == 0 else [])))
                st = j % 2
                src = u_d[s][j * 128:(j + 1) * 128, b * T:(b + 1) * T]
                p.op("gq", lambda e, st=st, src=src: e.dma_start(out=ustgs[st][:], in_=src),
                     reads=[], writes=[("ustg", 0)], dma=True)
                extra_x1 = [("Y", 0, 2 * j), ("Y", 0, 2 * j + 1)] if j < 8 else \
                           [("Y", 1, 2 * (j - 8)), ("Y", 1, 2 * (j - 8) + 1)]
                if j < 2:
                    extra_x1 += [("silu", j)]
                p.op("dve", lambda e, j=j, bank=bank, st=st: e.tensor_add(
                        x1_fm[:, j * T:(j + 1) * T], pb(bank), ustgs[st][:]),
                     reads=[("ps", bank), ("ustg", 0)], writes=[("x1", j)] + extra_x1)
                p.op("dve", lambda e, j=j, st=st: e.tensor_copy(
                        out=xbs[st][:], in_=x1_fm[:, j * T:(j + 1) * T]),
                     reads=[("x1", j)], writes=[("sx", st)])
                p.op("pe", lambda e, j=j, st=st: e.matmul(
                        pb(6, 0, 512)[0:1, :], c_ones128[:], xbs[st][:],
                        start=(j == 0), stop=(j == 15)),
                     reads=[("sx", st)], writes=[("ps", 6)])
                p.op("dve", lambda e, st=st: e.tensor_mul(xbs[st][:], xbs[st][:], xbs[st][:]),
                     reads=[("sx", st)], writes=[("sx", st)])
                p.op("pe", lambda e, j=j, st=st: e.matmul(
                        pb(7, 0, 512)[0:1, :], c_ones128[:], xbs[st][:],
                        start=(j == 0), stop=(j == 15)),
                     reads=[("sx", st)], writes=[("ps", 7)])
            if tile == 0:
                dtap("d_ynfm2", xtm[:], [("p4done",)])
                dtap("d_x1", x1_fm[:], [("x1", j) for j in range(KD)])
            vmean = dtraw_t[0:1, :]; vmsq = tmpB_t[0:1, :]; vtmp = lndt_t[0:1, :]
            vrstd = totB_t[0:1, :]; vnmr = tmpA_t[0:1, :]
            lotmp = dtsoft_t[0:1, :]
            bf_hr = cum_t[0:1, :].bitcast(BF16)[:, 0:512]
            bf_lr = cum_t[0:1, :].bitcast(BF16)[:, 512:1024]
            bf_hn = outsc_t[0:1, :].bitcast(BF16)[:, 0:512]
            bf_ln = outsc_t[0:1, :].bitcast(BF16)[:, 512:1024]
            p.op("dve", lambda e: e.tensor_scalar(out=vmean, in0=pb(6, 0, 512)[0:1, :],
                                                  scalar1=1.0 / D, scalar2=None, op0=ALU.mult),
                 reads=[("ps", 6)], writes=["vmean", "dtraw"], tiny=True)
            p.op("dve", lambda e: e.tensor_scalar(out=vmsq, in0=pb(7, 0, 512)[0:1, :],
                                                  scalar1=1.0 / D, scalar2=None, op0=ALU.mult),
                 reads=[("ps", 7)], writes=["vmsq", "tmpB"], tiny=True)
            p.op("dve", lambda e: e.tensor_mul(vtmp, vmean, vmean),
                 reads=["vmean"], writes=["vtmp", "lndt"], tiny=True)
            p.op("dve", lambda e: e.tensor_sub(vmsq, vmsq, vtmp),
                 reads=["vmsq", "vtmp"], writes=["vmsq"], tiny=True)
            p.op("dve", lambda e: e.tensor_scalar(out=vtmp, in0=vmsq, scalar1=EPS,
                                                  scalar2=None, op0=ALU.add),
                 reads=["vmsq"], writes=["vtmp", "lndt"], tiny=True)
            p.op("dve", lambda e: e.tensor_scalar(
                    out=vrstd.bitcast(I32), in0=vtmp.bitcast(I32),
                    scalar1=1, scalar2=None, op0=ALU.arith_shift_right),
                 reads=["vtmp"], writes=["vrstd", "totB"], tiny=True)
            p.op("dve", lambda e: e.tensor_scalar(
                    out=vrstd.bitcast(I32), in0=vrstd.bitcast(I32),
                    scalar1=-1, scalar2=0x5F3759DF, op0=ALU.mult, op1=ALU.add),
                 reads=["vrstd"], writes=["vrstd"], tiny=True)
            for _nit in range(2):
                p.op("dve", lambda e: e.tensor_mul(vmsq, vrstd, vrstd),
                     reads=["vrstd"], writes=["vmsq"], tiny=True)
                p.op("dve", lambda e: e.tensor_mul(vmsq, vmsq, vtmp),
                     reads=["vmsq", "vtmp"], writes=["vmsq"], tiny=True)
                p.op("dve", lambda e: e.tensor_scalar(out=vmsq, in0=vmsq, scalar1=-0.5,
                                                      scalar2=1.5, op0=ALU.mult, op1=ALU.add),
                     reads=["vmsq"], writes=["vmsq"], tiny=True)
                p.op("dve", lambda e: e.tensor_mul(vrstd, vrstd, vmsq),
                     reads=["vmsq", "vrstd"], writes=["vrstd"], tiny=True)
            p.op("dve", lambda e: e.tensor_mul(vnmr, vmean, vrstd),
                 reads=["vmean", "vrstd"], writes=["vnmr", "tmpA"], tiny=True)
            p.op("dve", lambda e: e.tensor_scalar_mul(vnmr, vnmr, -1.0),
                 reads=["vnmr"], writes=["vnmr"], tiny=True)
            p.op("dve", lambda e: e.tensor_copy(out=bf_hr, in_=vrstd),
                 reads=["vrstd"], writes=["bfhr", "cum"], tiny=True)
            p.op("dve", lambda e: e.tensor_copy(out=bf_hr, in_=vrstd),
                 reads=[], writes=["bfhr"], tiny=True)
            p.op("dve", lambda e: e.tensor_sub(lotmp, vrstd, bf_hr),
                 reads=["vrstd", "bfhr"], writes=["lotmp", "dtsoft"], tiny=True)
            p.op("dve", lambda e: e.tensor_copy(out=bf_lr, in_=lotmp),
                 reads=["lotmp"], writes=["bflr"], tiny=True)
            p.op("dve", lambda e: e.tensor_copy(out=bf_lr, in_=lotmp),
                 reads=[], writes=["bflr"], tiny=True)
            p.op("dve", lambda e: e.tensor_copy(out=bf_hn, in_=vnmr),
                 reads=["vnmr"], writes=["bfhn", "outsc"], tiny=True)
            p.op("dve", lambda e: e.tensor_copy(out=bf_hn, in_=vnmr),
                 reads=[], writes=["bfhn"], tiny=True)
            p.op("dve", lambda e: e.tensor_sub(lotmp, vnmr, bf_hn),
                 reads=["vnmr", "bfhn"], writes=["lotmp"], tiny=True)
            p.op("dve", lambda e: e.tensor_copy(out=bf_ln, in_=lotmp),
                 reads=["lotmp"], writes=["bfln"], tiny=True)
            p.op("dve", lambda e: e.tensor_copy(out=bf_ln, in_=lotmp),
                 reads=[], writes=["bfln"], tiny=True)
            p.op("pe", lambda e: e.matmul(pb(2), c_ones1[:], bf_hr, start=True, stop=False),
                 reads=["bfhr"], writes=[("ps", 2)])
            p.op("pe", lambda e: e.matmul(pb(2), c_ones1[:], bf_lr, start=False, stop=True),
                 reads=["bflr"], writes=[("ps", 2)])
            p.op("pe", lambda e: e.matmul(pb(3), c_ones1[:], bf_hn, start=True, stop=False),
                 reads=["bfhn"], writes=[("ps", 3)])
            p.op("pe", lambda e: e.matmul(pb(3), c_ones1[:], bf_ln, start=False, stop=True),
                 reads=["bfln"], writes=[("ps", 3)])
            for j in range(KD):
                st = j % 2
                p.op("dve", lambda e, j=j, st=st: e.tensor_mul(
                        xas[st][:], x1_fm[:, j * T:(j + 1) * T], pb(2)),
                     reads=[("x1", j), ("ps", 2)], writes=[("cout", st)])
                p.op("dve", lambda e, st=st: e.tensor_add(xas[st][:], xas[st][:], pb(3)),
                     reads=[("cout", st), ("ps", 3)], writes=[("cout", st)])
                p.op("gq", lambda e, j=j, st=st: e.dma_start(out=scr_x1n[j], in_=xas[st][:]),
                     reads=[("cout", st)], writes=[("scrx1n", j)], dma=True)
                p.op("dve", lambda e, j=j, st=st: e.tensor_copy(
                        out=u_bf[:, j * T:(j + 1) * T], in_=xas[st][:]),
                     reads=[("cout", st)], writes=[("ubf", j)])

            if tile == 0:
                dtap("d_x1nbf", u_bf[:], [("ubf", j) for j in range(KD)])

            si_g = si_op + 64
            for jj in range(44):
                bg = 0 if jj % 2 == 0 else 4
                bu = 1 if jj % 2 == 0 else 5
                ensure_slices(tile, si_g + jj * 4, si_g + jj * 4 + 3)
                for k in range(KD):
                    si = si_g + jj * 4 + k // 8
                    p.op("pe", lambda e, bg=bg, tile=tile, si=si, k=k: e.matmul(
                            pb(bg), wap(tile, si, k % 8), u_bf[:, k * T:(k + 1) * T],
                            start=(k == 0), stop=(k == KD - 1)),
                         waits=[wres(tile, si), ("ubf", k)],
                         reads=([wres(tile, si)] if k % 8 == 7 else []),
                         writes=[("ps", bg)] if (k == KD - 1 or k == 0) else [])
                for k in range(KD):
                    si = si_g + jj * 4 + 2 + k // 8
                    p.op("pe", lambda e, bu=bu, tile=tile, si=si, k=k: e.matmul(
                            pb(bu), wap(tile, si, k % 8), u_bf[:, k * T:(k + 1) * T],
                            start=(k == 0), stop=(k == KD - 1)),
                         waits=[wres(tile, si), ("ubf", k)],
                         reads=([wres(tile, si)] if k % 8 == 7 else []) +
                               ([("ubf", k)] if jj == 43 else []),
                         writes=[("ps", bu)] if (k == KD - 1 or k == 0) else [])
                st = jj % 2
                p.op("act", lambda e, jj=jj, bg=bg, st=st: e.activation(
                        out=sxs[st][:], in_=pb(bg), func=AF.Silu, scale=c_sg[:, jj:jj + 1]),
                     reads=[("ps", bg)], writes=[("sx", st)])
                p.op("dve", lambda e, jj=jj, bu=bu, st=st: e.scalar_tensor_tensor(
                        out=h_bf[:, jj * 512:(jj + 1) * 512], in0=pb(bu),
                        scalar=c_su[:, jj:jj + 1], in1=sxs[st][:],
                        op0=ALU.mult, op1=ALU.mult),
                     reads=[("ps", bu), ("sx", st)], writes=[("h", jj)])
            if tile == 0:
                dtap("d_h", h_bf[:], [("h", jj) for jj in range(44)])
            if tile + 1 < NTILE:
                emit_uload(tile + 1)
            si_d = si_g + 176
            for j in range(16):
                bank = j % 2
                ensure_slices(tile, si_d + j * 6, si_d + j * 6 + 5)
                for k in range(KH):
                    si = si_d + j * 6 + k // 8
                    p.op("pe", lambda e, bank=bank, tile=tile, si=si, k=k: e.matmul(
                            pb(bank), wap(tile, si, k % 8),
                            h_bf[:, k * 512:(k + 1) * 512], start=(k == 0), stop=(k == KH - 1)),
                         waits=[wres(tile, si), ("h", k)],
                         reads=([wres(tile, si)] if (k % 8 == 7 or k == KH - 1) else []) +
                               ([("h", kk) for kk in range(KH)] if (j == 15 and k == KH - 1) else []),
                         writes=[("ps", bank)] if (k == KH - 1 or k == 0) else [])
                st = j % 2
                p.op("gq", lambda e, j=j, st=st: e.dma_start(out=x1ss[st], in_=scr_x1n[j]),
                     reads=[("scrx1n", j)], writes=[("cin", st)], dma=True)
                p.op("dve", lambda e, j=j, bank=bank, st=st: e.scalar_tensor_tensor(
                        out=xas[st][:], in0=pb(bank), scalar=c_sd[:, j:j + 1],
                        in1=x1ss[st], op0=ALU.mult, op1=ALU.add),
                     reads=[("ps", bank), ("cin", st)], writes=[("cout", st)])
                tb = 2 + j % 2
                for c in range(NCH):
                    p.op("pe", lambda e, c=c, st=st, tb=tb: e.matmul(
                            pb(tb, c * 128, (c + 1) * 128), xas[st][:, c * 128:(c + 1) * 128],
                            c_idf[:], is_transpose=True, start=(c == 0), stop=(c == 3)),
                         reads=[("cout", st)], writes=[("ps", tb)])
                for c in range(NCH):
                    p.op("dve", lambda e, c=c, j=j, tb=tb: e.tensor_copy(
                            out=x2tm[c][:, j * 128:(j + 1) * 128],
                            in_=pb(tb, c * 128, (c + 1) * 128)),
                         reads=[("ps", tb)], tiny=True,
                         writes=[("x2tm", c, j)] +
                                ([("zbf", zz) for zz in range(c * 8, c * 8 + 8)]
                                 if j == 0 else []))

            if tile == 0:
                dtap("d_x2tm", x2tm[0][:], [("x2tm", 0, jj) for jj in range(16)])
            for c in range(NCH):
                for ss in range(4):
                    p.op("act", lambda e, c=c, ss=ss: e.activation(
                            out=pb(4), in_=x2tm[c][:, ss * 512:(ss + 1) * 512], func=AF.Copy,
                            accum_out=stats[:, ss: ss + 1]),
                         reads=[("x2tm", c, jj) for jj in range(ss * 4, ss * 4 + 4)],
                         writes=[("bn", ss), ("ps", 4)], tiny=True)
                    p.op("act", lambda e, c=c, ss=ss: e.activation(
                            out=pb(5), in_=x2tm[c][:, ss * 512:(ss + 1) * 512], func=AF.Square,
                            accum_out=stats[:, 4 + ss: 5 + ss]),
                         reads=[("x2tm", c, jj) for jj in range(ss * 4, ss * 4 + 4)],
                         writes=[("bn", 4 + ss), ("ps", 5)], tiny=True)
                if TAP7 and tile == 7 and c == 3:
                    p.op("gq", lambda e: e.dma_start(out=d7["d7_stats"][:], in_=stats[:, 0:8]),
                         reads=[("bn", ss) for ss in range(8)], writes=[], dma=True)
                p.op("dve", lambda e: e.tensor_reduce(
                        out=lncol[:, 3:4], in_=stats[:, 0:4], axis=mybir.AxisListType.X, op=ALU.add),
                     reads=[("bn", 0), ("bn", 1), ("bn", 2), ("bn", 3)], writes=["l2sum"], tiny=True)
                p.op("dve", lambda e: e.tensor_reduce(
                        out=lncol[:, 4:5], in_=stats[:, 4:8], axis=mybir.AxisListType.X, op=ALU.add),
                     reads=[("bn", 4), ("bn", 5), ("bn", 6), ("bn", 7)], writes=["l2sq"], tiny=True)
                p.op("dve", lambda e: e.tensor_scalar(out=lncol[:, 5:6], in0=lncol[:, 3:4],
                                                      scalar1=1.0 / D, scalar2=None, op0=ALU.mult),
                     reads=["l2sum"], writes=["l2mean"], tiny=True)
                p.op("dve", lambda e: e.tensor_scalar(out=lncol[:, 6:7], in0=lncol[:, 4:5],
                                                      scalar1=1.0 / D, scalar2=None, op0=ALU.mult),
                     reads=["l2sq"], writes=["l2msq"], tiny=True)
                p.op("dve", lambda e: e.tensor_mul(lncol[:, 7:8], lncol[:, 5:6], lncol[:, 5:6]),
                     reads=["l2mean"], writes=["l2m2"], tiny=True)
                p.op("dve", lambda e: e.tensor_sub(lncol[:, 0:1], lncol[:, 6:7], lncol[:, 7:8]),
                     reads=["l2msq", "l2m2"], writes=["l2a"], tiny=True)
                p.op("dve", lambda e: e.tensor_scalar(out=lncol[:, 0:1], in0=lncol[:, 0:1],
                                                      scalar1=EPS, scalar2=None, op0=ALU.add),
                     reads=["l2a"], writes=["l2a"], tiny=True)
                p.op("dve", lambda e: e.tensor_scalar(
                        out=lncol[:, 1:2].bitcast(I32), in0=lncol[:, 0:1].bitcast(I32),
                        scalar1=1, scalar2=None, op0=ALU.arith_shift_right),
                     reads=["l2a"], writes=["l2r"], tiny=True)
                p.op("dve", lambda e: e.tensor_scalar(
                        out=lncol[:, 1:2].bitcast(I32), in0=lncol[:, 1:2].bitcast(I32),
                        scalar1=-1, scalar2=0x5F3759DF, op0=ALU.mult, op1=ALU.add),
                     reads=["l2r"], writes=["l2r"], tiny=True)
                for _nit in range(2):
                    p.op("dve", lambda e: e.tensor_mul(lncol[:, 7:8], lncol[:, 1:2], lncol[:, 1:2]),
                         reads=["l2r"], writes=["l2m2"], tiny=True)
                    p.op("dve", lambda e: e.tensor_mul(lncol[:, 7:8], lncol[:, 7:8], lncol[:, 0:1]),
                         reads=["l2m2", "l2a"], writes=["l2m2"], tiny=True)
                    p.op("dve", lambda e: e.tensor_scalar(out=lncol[:, 7:8], in0=lncol[:, 7:8],
                                                          scalar1=-0.5, scalar2=1.5,
                                                          op0=ALU.mult, op1=ALU.add),
                         reads=["l2m2"], writes=["l2m2"], tiny=True)
                    p.op("dve", lambda e: e.tensor_mul(lncol[:, 1:2], lncol[:, 1:2], lncol[:, 7:8]),
                         reads=["l2m2", "l2r"], writes=["l2r"], tiny=True)
                p.op("dve", lambda e: e.tensor_mul(lncol[:, 2:3], lncol[:, 5:6],
                                                   lncol[:, 1:2]),
                     reads=["l2mean", "l2r"], writes=["l2n"], tiny=True)
                p.op("dve", lambda e: e.tensor_scalar_mul(lncol[:, 2:3], lncol[:, 2:3], -1.0),
                     reads=["l2n"], writes=["l2n"], tiny=True)
                if TAP7 and tile == 7 and c == 3:
                    p.op("gq", lambda e: e.dma_start(out=d7["d7_lncol"][:], in_=lncol[:]),
                         reads=["l2sum", "l2sq", "l2mean", "l2msq", "l2m2", "l2a", "l2r", "l2n"],
                         writes=[], dma=True)
                    p.op("gq", lambda e: e.dma_start(out=d7["d7_x2"][:], in_=x2tm[3][:]),
                         reads=[("x2tm", 3, jj) for jj in range(16)], writes=[], dma=True)
                p.op("dve", lambda e, c=c: e.tensor_scalar(
                        out=x2tm[c][:], in0=x2tm[c][:], scalar1=lncol[:, 1:2],
                        scalar2=lncol[:, 2:3], op0=ALU.mult, op1=ALU.add),
                     reads=["l2r", "l2n"] + [("x2tm", c, jj) for jj in range(16)],
                     writes=[("x2f", c)] + [("x2tm", c, jj) for jj in range(16)])
                if tile == 0 and c == 0:
                    dtap("d_ln2", stats[:, 0:32], [("bn", ss) for ss in range(4)] + ["bnagg"])
                    dtap("d_x2post", x2tm[0][:], [("x2f", 0)])
                dst = o_d[s][b * T + c * 128: b * T + (c + 1) * 128, :]
                p.op("gq", lambda e, c=c, dst=dst: e.dma_start(out=dst, in_=x2tm[c][:]),
                     reads=[("x2f", c)] + [("x2tm", c, jj) for jj in range(16)],
                     writes=[("store", tile, c)], dma=True)

        p.finalize()
        totals = dict(p.totals)

        @block.tensor
        def _(eng):
            p.emit_stream(eng, sems, ("pe",))

        @block.scalar
        def _(eng):
            p.emit_stream(eng, sems, ("act",),
                          spacer=lambda e: e.activation(out=stats[:, 9:10], in_=tails[:, 0:1],
                                                        func=AF.Copy))

        @block.vector
        def _(eng):
            p.emit_stream(eng, sems, ("dve",),
                          spacer=lambda e: e.tensor_copy(out=stats[:, 8:9], in_=tails[:, 0:1]))

        @block.sync
        def _(eng):
            p.emit_stream(eng, sems, ("sync",))

        @block.gpsimd
        def _(eng):
            p.emit_stream(eng, sems, ("gq",))
            for qi in range(DMAPOOL):
                t = totals.get(("gq", qi), 0)
                if t:
                    eng.wait_ge(sems[("gq", qi)], t)

    import sys as _sys
    print(f"[kernel] ops={len(p.ops)} spacers={getattr(p, 'n_spacers', 0)}", file=_sys.stderr)
    return nc


_NC_CACHE = None


def kernel(**inputs):
    global _NC_CACHE
    shared, us = host_prep(inputs)
    if _NC_CACHE is None:
        _NC_CACHE = build_program()
    nc = _NC_CACHE
    in_maps = []
    for c in range(NCORES):
        m = dict(shared)
        m["u"] = us[c]
        in_maps.append(m)
    res = run_bass_kernel_spmd(nc, in_maps, core_ids=list(range(NCORES)))
    out = np.empty((16, 2048, D), np.float32)
    for c in range(NCORES):
        out[2 * c:2 * c + 2] = np.asarray(res.results[c]["o"])
    return out
